# version 03
# baseline (speedup 1.0000x reference)
import numpy as np
import ml_dtypes
from contextlib import ExitStack

import concourse.bass as bass
import concourse.mybir as mybir
from concourse.bass import IndirectOffsetOnAxis
from concourse.bass_utils import run_bass_kernel_spmd

F32 = mybir.dt.float32
BF16 = mybir.dt.bfloat16
I32 = mybir.dt.int32
AF = mybir.ActivationFunctionType
ALU = mybir.AluOpType
BF = ml_dtypes.bfloat16

NCORES = 8
D = 1024
SEQ = 2048
RPC = 4096
NCH = 8
TT = 512
NTT = RPC // TT
TPS = SEQ // TT
BIG = 1.0e9
OOBIDX = float(8 * 2048)
DBG = False

ECAP = 2048
GCAP = 10240

ENG_SEM = {"tensor": "pe", "vector": "vv", "scalar": "aa"}
DMA_Q = {"s": ("sync", "dms"), "g": ("gpsimd", "dmg"), "c": ("scalar", "dmc")}


def _merge(d, sem, cnt):
    if d.get(sem, 0) < cnt:
        d[sem] = cnt


class Buf:
    __slots__ = ("w", "r", "name", "wbig", "rsmall")

    def __init__(self, name=""):
        self.name = name
        self.w = {}
        self.r = {}
        self.wbig = True
        self.rsmall = False


def gw(buf, i, n):
    return dict(outs_war=(buf,), finals=(buf,) if i == n - 1 else ())


class Kb:
    def __init__(self, nc, es):
        self.nc = nc
        self.es = es
        self.streams = {e: [] for e in ("tensor", "vector", "scalar", "sync", "gpsimd")}
        self.sems = {}
        self.cnt = {}
        self.wm = {e: {} for e in self.streams}

    def sem(self, name):
        if name not in self.sems:
            self.sems[name] = self.es.enter_context(self.nc.semaphore(name))
            self.cnt[name] = 0
        return self.sems[name]

    def op(self, eng, meth, *args, ins=(), outs=(), outs_war=(), finals=(),
           sem=None, inc=1, **kw):
        if sem is None:
            sem = ENG_SEM[eng]
        try:
            big = args and hasattr(args[0], "free_size") and args[0].free_size() >= 384
        except Exception:
            big = False
        deps = {}
        dsmall = {}
        for b in ins:
            for s, c in b.w.items():
                _merge(deps, s, c)
                if not b.wbig:
                    dsmall[s] = True
        for b in tuple(outs) + tuple(outs_war):
            for s, c in b.w.items():
                _merge(deps, s, c)
                if not b.wbig:
                    dsmall[s] = True
            for s, c in b.r.items():
                _merge(deps, s, c)
                if b.rsmall:
                    dsmall[s] = True
        self.sem(sem)
        self.cnt[sem] += inc
        mark = (sem, self.cnt[sem])
        own = ENG_SEM.get(eng)
        waits = []
        for s, c in deps.items():
            if c <= 0:
                continue
            if s == own:
                if eng == "tensor":
                    continue
                if eng == "vector" and big and not dsmall.get(s):
                    continue
            if self.wm[eng].get(s, 0) >= c:
                continue
            waits.append((self.sems[s], c))
            self.wm[eng][s] = c
        semh = self.sems[sem]

        def fn(e, waits=waits, meth=meth, args=args, kw=kw, semh=semh, inc=inc):
            for h, c in waits:
                e.wait_ge(h, c)
            getattr(e, meth)(*args, **kw).then_inc(semh, inc)

        self.streams[eng].append(fn)
        for b in ins:
            _merge(b.r, *mark)
            if sem == ENG_SEM.get(eng) and not big:
                b.rsmall = True
        for b in tuple(outs) + tuple(finals):
            b.w = {mark[0]: mark[1]}
            b.r = {}
            b.wbig = big or eng not in ("vector", "scalar")
            b.rsmall = False
        return mark

    def dma(self, q, out, in_, **kw):
        eng, sem = DMA_Q[q]
        return self.op(eng, "dma_start", sem=sem, inc=16, out=out, in_=in_, **kw)

    def idma(self, out, out_offset, in_, in_offset, nmax=None, compute=ALU.bypass, **kw):
        return self.op("gpsimd", "indirect_dma_start", out, out_offset, in_,
                       in_offset, sem="dmg", inc=16, compute_op=compute, **kw)

    def mm(self, bank, out_ap, parts):
        n = len(parts)
        for j, (lh, rh, ib) in enumerate(parts):
            self.op("tensor", "matmul", out_ap, lh, rh,
                    start=(j == 0), stop=(j == n - 1), ins=ib,
                    outs_war=(bank,) if j == 0 else (),
                    finals=(bank,) if j == n - 1 else ())

    def sb(self, name, shape, dtype):
        t = self.es.enter_context(self.nc.sbuf_tensor(name, list(shape), dtype))
        return t, Buf(name)

    def ps(self, name, shape, dtype=F32):
        t = self.es.enter_context(self.nc.psum_tensor(name, list(shape), dtype))
        return t, Buf(name)



def _vec128(v):
    v = np.asarray(v, np.float32)
    m = v.shape[0]
    if m < 128:
        v = np.concatenate([v, np.zeros(128 - m, np.float32)])
        return np.ascontiguousarray(v.reshape(1, 128).transpose())
    return np.ascontiguousarray(v.reshape(m // 128, 128).transpose())


def stage_inputs(inp):
    g = {}
    g["w_l0a_in"] = np.ascontiguousarray(np.asarray(inp["L0_attn_in_w"], np.float32).T)
    g["w_l0a_pw"] = np.ascontiguousarray(np.asarray(inp["L0_attn_pw_w"], np.float32).T)
    g["w_l0a_win"] = np.ascontiguousarray(np.concatenate(
        [np.asarray(inp["L0_attn_f_win"], np.float32).T,
         np.asarray(inp["L0_attn_b_win"], np.float32).T], axis=1))
    g["w_l0a_wout"] = np.ascontiguousarray(np.concatenate(
        [np.asarray(inp["L0_attn_f_wout"], np.float32).T,
         np.asarray(inp["L0_attn_b_wout"], np.float32).T], axis=0))
    g["w_l0a_out"] = np.ascontiguousarray(np.asarray(inp["L0_attn_out_w"], np.float32).T)
    g["v_l0a_inb"] = _vec128(inp["L0_attn_in_b"])
    g["v_l0a_dwb"] = _vec128(inp["L0_attn_dw_b"])
    g["v_l0a_pwb"] = _vec128(inp["L0_attn_pw_b"])
    g["v_l0a_outb"] = _vec128(inp["L0_attn_out_b"])
    g["v_l0a_ns"] = _vec128(inp["L0_attn_norm_scale"])
    dw = np.asarray(inp["L0_attn_dw_w"], np.float32)[:, 0, :]
    g["v_l0a_dww"] = np.ascontiguousarray(
        dw.reshape(8, 128, 3).transpose(1, 0, 2).reshape(128, 24))
    ssm = np.zeros((128, 4), np.float32)
    ssm[:64, 0] = inp["L0_attn_f_logA"]; ssm[64:, 0] = inp["L0_attn_b_logA"]
    ssm[:64, 1] = inp["L0_attn_f_B"];    ssm[64:, 1] = inp["L0_attn_b_B"]
    ssm[:64, 2] = inp["L0_attn_f_C"];    ssm[64:, 2] = inp["L0_attn_b_C"]
    ssm[:64, 3] = float(inp["L0_attn_f_logdt"]); ssm[64:, 3] = float(inp["L0_attn_b_logdt"])
    g["v_l0a_ssm"] = ssm

    qw_ = np.asarray(inp["L0_ffn_qw"], np.float32)
    qb_ = np.asarray(inp["L0_ffn_qb"], np.float32)
    pairs = [(0, 4), (1, 5), (2, 6), (3, 7), (8, 12), (9, 13), (10, 14), (11, 15)]
    qw_cols = []
    qb_cols = np.zeros((128, 8), np.float32)
    for j, (ha, hb) in enumerate(pairs):
        qw_cols.append(qw_[:, ha * 64:(ha + 1) * 64])
        qw_cols.append(qw_[:, hb * 64:(hb + 1) * 64])
        qb_cols[:64, j] = qb_[ha * 64:(ha + 1) * 64]
        qb_cols[64:, j] = qb_[hb * 64:(hb + 1) * 64]
    g["w_l0f_qw"] = np.ascontiguousarray(np.concatenate(qw_cols, axis=1))
    g["w_l0f_kw"] = np.ascontiguousarray(np.asarray(inp["L0_ffn_kw"], np.float32))
    g["w_l0f_vw"] = np.ascontiguousarray(np.asarray(inp["L0_ffn_vw"], np.float32))
    g["w_l0f_ow"] = np.ascontiguousarray(np.asarray(inp["L0_ffn_ow"], np.float32))
    g["w_l0f_gate"] = np.ascontiguousarray(np.asarray(inp["L0_ffn_gate_w"], np.float32))
    g["w_l0f_w1"] = np.ascontiguousarray(np.asarray(inp["L0_ffn_w1"], np.float32))
    g["w_l0f_w2"] = np.ascontiguousarray(np.asarray(inp["L0_ffn_w2"], np.float32))
    g["v_l0f_qb"] = qb_cols
    g["v_l0f_kb"] = _vec128(inp["L0_ffn_kb"])
    g["v_l0f_vb"] = _vec128(inp["L0_ffn_vb"])
    g["v_l0f_ob"] = _vec128(inp["L0_ffn_ob"])
    g["v_l0f_n1"] = _vec128(inp["L0_ffn_norm1_w"])
    g["v_l0f_n2"] = _vec128(inp["L0_ffn_norm2_w"])

    Wqkv = np.asarray(inp["L1_attn_W_qkv"], np.float32)
    bqkv = np.asarray(inp["L1_attn_b_qkv"], np.float32)
    qk_cols = []
    qkb_cols = np.zeros((128, 16), np.float32)
    vb_cols = np.zeros((64, 16), np.float32)
    for h in range(16):
        qk_cols.append(Wqkv[:, h * 64:(h + 1) * 64])
        qk_cols.append(Wqkv[:, 1024 + h * 64:1024 + (h + 1) * 64])
        qkb_cols[:64, h] = bqkv[h * 64:(h + 1) * 64]
        qkb_cols[64:, h] = bqkv[1024 + h * 64:1024 + (h + 1) * 64]
        vb_cols[:, h] = bqkv[2048 + h * 64:2048 + (h + 1) * 64]
    g["w_l1a_qk"] = np.ascontiguousarray(np.concatenate(qk_cols, axis=1))
    g["w_l1a_v"] = np.ascontiguousarray(Wqkv[:, 2048:])
    g["v_l1a_qkb"] = qkb_cols
    g["v_l1a_vb2"] = np.ascontiguousarray(np.vstack([vb_cols, vb_cols]))
    g["w_l1a_out"] = np.ascontiguousarray(np.asarray(inp["L1_attn_W_out"], np.float32))
    g["v_l1a_outb"] = _vec128(inp["L1_attn_b_out"])
    gb = np.zeros((128, 2), np.float32)
    gb[:64, 0] = inp["L1_attn_q_gamma"]; gb[64:, 0] = inp["L1_attn_k_gamma"]
    gb[:64, 1] = inp["L1_attn_q_beta"];  gb[64:, 1] = inp["L1_attn_k_beta"]
    g["v_l1a_gb"] = gb

    g["w_l1f_router"] = np.ascontiguousarray(np.asarray(inp["L1_ffn_router_w"], np.float32).T)
    g["v_l1f_corr"] = np.ascontiguousarray(
        np.asarray(inp["L1_ffn_correction_bias"], np.float32).reshape(1, 16))
    g["w_l1f_g"] = np.ascontiguousarray(np.asarray(inp["L1_ffn_gate_w"], np.float32))
    g["w_l1f_u"] = np.ascontiguousarray(np.asarray(inp["L1_ffn_up_w"], np.float32))
    g["w_l1f_d"] = np.ascontiguousarray(np.asarray(inp["L1_ffn_down_w"], np.float32))
    g["w_l1f_sg"] = np.ascontiguousarray(np.asarray(inp["L1_ffn_shared_gate_w"], np.float32).T)
    g["w_l1f_su"] = np.ascontiguousarray(np.asarray(inp["L1_ffn_shared_up_w"], np.float32).T)
    g["w_l1f_sd"] = np.ascontiguousarray(np.asarray(inp["L1_ffn_shared_down_w"], np.float32).T)

    x = np.asarray(inp["L0_attn_T"], np.float32)
    xs = [np.ascontiguousarray(x[2 * c:2 * c + 2].reshape(RPC, D).T) for c in range(NCORES)]
    cms = [np.asarray([[1.0 if r < c else 0.0] for r in range(8)], np.float32)
           for c in range(NCORES)]
    return g, xs, cms



class Prog:
    def __init__(self, blocks, g):
        self.blocks = blocks
        self.gshapes = {n: v.shape for n, v in g.items()}
        nc = bass.Bass("TRN2", target_bir_lowering=False,
                       dynamic_dma_scratch_size=49152)
        self.nc = nc
        self.es = ExitStack()
        self.k = Kb(nc, self.es)
        self.dram = {}
        self.dbuf = {}
        self.used_inputs = set()
        self._internal("xdram", [D, RPC], F32)
        self._internal("gate_sp", [D, RPC], BF16)
        self._internal("sdram", [D, RPC], F32)
        self._internal("fout", [D, RPC], F32)
        self._internal("hg", [2048, RPC], BF16)
        self._internal("ebuf", [8 * ECAP + 1, D], BF16)
        self._internal("ybuf", [8 * ECAP + 1, D], F32)
        self._internal("cc_in", [1, 8], F32)
        self._internal("cc_out", [8, 8], F32)
        self._internal("rowsd", [16, RPC], F32)
        self._internal("cmbd", [16, RPC], F32)
        self.dram["x_out"] = nc.dram_tensor("x_out", [D, RPC], F32, kind="ExternalOutput")
        self.dbuf["x_out"] = Buf("x_out")
        self.const = {}
        self.cadd("ident", np.eye(128, dtype=BF))
        self.cadd("identf", np.eye(128, dtype=np.float32))
        self.cadd("iota8B", (np.broadcast_to(np.arange(8, dtype=np.float32), (128, 8)) + 65536.0).copy())
        self.cadd("ustrict", np.ascontiguousarray(np.triu(np.ones((128, 128), BF), 1)))
        self.cadd("ones1", np.ones((1, 128), BF))
        self.cadd("ones1h", np.ones((1, 64), BF))
        self.cadd("onesc1", np.ones((128, 1), BF))
        oh0 = np.zeros((128, 1), BF); oh0[:64] = 1
        oh1 = np.zeros((128, 1), BF); oh1[64:] = 1
        self.cadd("onesc_h0", oh0)
        self.cadd("onesc_h1", oh1)
        self.cadd("iota8", np.broadcast_to(np.arange(8, dtype=np.float32), (128, 8)).copy())
        self.cadd("iota16", np.broadcast_to(np.arange(16, dtype=np.float32), (128, 16)).copy())
        self.cadd("iota16B", (np.broadcast_to(np.arange(16, dtype=np.float32), (128, 16)) + 65536.0).copy())
        self.cadd("grp16", np.broadcast_to((np.arange(16) // 4).astype(np.float32), (128, 16)).copy())
        kk = np.arange(128)[:, None]; qq = np.arange(128)[None, :]
        self.cadd("m_diag", np.where(qq >= kk, 0.0, -BIG).astype(np.float32))
        self.cadd("m_win", np.where(qq < kk, 0.0, -BIG).astype(np.float32))
        self.cadd("m_full", np.full((128, 128), -BIG, np.float32))

    def cadd(self, name, arr):
        self.const[name] = (self.nc.inline_tensor(np.ascontiguousarray(arr), name="c_" + name),
                            arr.dtype, arr.shape)

    def _internal(self, name, shape, dtype):
        self.dram[name] = self.nc.dram_tensor(name, list(shape), dtype, kind="Internal")
        self.dbuf[name] = Buf(name)

    def d(self, name):
        if name not in self.dram:
            if name.endswith("_bf"):
                base = name[:-3]
                self._internal(name, list(self.gshapes[base]), BF16)
            else:
                self.dram[name] = self.nc.dram_tensor(
                    name, list(self.gshapes[name]) if name in self.gshapes else
                    ([D, RPC] if name == "x_in" else [8, 1]), F32, kind="ExternalInput")
                self.dbuf[name] = Buf(name)
                self.used_inputs.add(name)
        return self.dram[name].ap(), self.dbuf[name]

    def build(self):
        k = self.k
        with self.es:
            self.pool = []
            for i in range(5):
                t, b = k.ps(f"p{i}", [128, 512], F32)
                self.pool.append((t, b))
            self.pt, self.ptb = k.ps("pt", [128, 1024], BF16)
            self.pb1, self.pb1b = k.ps("pb1", [128, 512], F32)
            self.pb2, self.pb2b = k.ps("pb2", [128, 512], F32)
            self._pool_i = 0
            self.csb = {}
            for n, (t, dt, shp) in self.const.items():
                sdt = BF16 if dt == BF else F32
                st, sbf = k.sb("cs_" + n, list(shp), sdt)
                k.dma("s", st[:], t.ap(), outs=(sbf,))
                self.csb[n] = (st, sbf)
            self.cflt, self.cfltb = k.sb("cflt", [128, 8], F32)
            self._fconsts = {}
            self.wslab, self.wslabb = k.sb("wslab", [128, 16384], BF16)
            self.abig, self.abigb = k.sb("abig", [128, 16384], BF16)
            self.bbig, self.bbigb = k.sb("bbig", [128, 16384], BF16)
            self.f1, self.f1b = k.sb("f1", [128, 4096], F32)
            self.s1b, self.s1bb = k.sb("s1b", [128, 4096], BF16)
            self.s2b, self.s2bb = k.sb("s2b", [128, 4096], BF16)
            self.sm, self.smb = k.sb("sm", [128, 2048], F32)
            self.smb_t, self.smb_b = k.sb("smb", [128, 2048], BF16)
            self.rsc, self.rscb = k.sb("rsc", [1, 2048], F32)
            self.vt, self.vtb = k.sb("vt", [128, 4 * 16 * 65], BF16)
            self.rbf, self.rbfb = k.sb("rbf", [1, 2048], BF16)
            for blk in self.blocks:
                BLOCK_BUILDERS[blk](self)
            with self.nc.Block() as block:
                for eng in ("sync", "gpsimd", "tensor", "vector", "scalar"):
                    fns = self.k.streams[eng]
                    getattr(block, eng)(lambda e, fns=fns: [f(e) for f in fns])
        return self.nc

    def fconst(self, val, rows=1):
        if val not in self._fconsts:
            i = len(self._fconsts)
            self.k.op("vector", "memset", self.cflt[:, i:i + 1], float(val),
                      outs=(self.cfltb,))
            self._fconsts[val] = i
        i = self._fconsts[val]
        return self.cflt[0:rows, i:i + 1]

    def bank(self):
        t, b = self.pool[self._pool_i % len(self.pool)]
        self._pool_i += 1
        return t, b

    def precast(self, name, nsplit=1):
        k = self.k
        src, sbuf_ = self.d(name)
        dst, dbuf_ = self.d(name + "_bf")
        n0 = src.shape[0]
        step = n0 // nsplit
        for i in range(nsplit):
            k.dma("g", dst[i * step:(i + 1) * step], src[i * step:(i + 1) * step],
                  ins=(sbuf_,), **gw(dbuf_, i, nsplit))

    def load_w(self, name, cols, koff=0, kdim=None, moff=0, mdim=None, expert=None,
               q="s"):
        k = self.k
        ap, dbuf_ = self.d(name + "_bf")
        if expert is not None:
            ap = ap[expert]
        Kfull, Mfull = ap.shape
        kdim = kdim if kdim is not None else Kfull
        mdim = mdim if mdim is not None else Mfull
        nkc = kdim // 128
        view = ap[koff:koff + kdim, moff:moff + mdim].rearrange(
            "(kc p) m -> p kc m", p=128)
        dst = self.wslab[:, cols:cols + nkc * mdim].rearrange(
            "p (kc m) -> p kc m", m=mdim)
        k.dma(q, dst, view, ins=(dbuf_,), outs=(self.wslabb,))

        def acc(kc, mc=0, width=128, mo=None):
            c0 = cols + kc * mdim + (mo if mo is not None else mc * 128)
            return self.wslab[:, c0:c0 + width]
        return acc

    def dbg_dump(self, tag, src_name):
        if not DBG:
            return
        ap, b = self.d(src_name)
        name = "dbg_" + tag
        self.dram[name] = self.nc.dram_tensor(name, list(ap.shape), ap.dtype,
                                              kind="ExternalOutput")
        self.dbuf[name] = Buf(name)
        self.k.dma("c", self.dram[name].ap(), ap[:], ins=(b,),
                   outs=(self.dbuf[name],))

    def load_vec(self, name):
        if not hasattr(self, "_vecs"):
            self._vecs = {}
        if name in self._vecs:
            return self._vecs[name]
        k = self.k
        ap, dbuf_ = self.d(name)
        t, b = k.sb("sv_" + name, list(ap.shape), F32)
        k.dma("s", t[:], ap[:], ins=(dbuf_,), outs=(b,))
        self._vecs[name] = (t, b)
        return t, b

    def bcast(self, hi, lo, psum_t, psum_b, n, pair2=None):
        k = self.k
        if pair2 is None:
            cs, csb_ = self.csb["ones1"]
            k.mm(psum_b, psum_t[:, 0:n],
                 [(cs[0:1, :], hi, [csb_, self.rbfb]),
                  (cs[0:1, :], lo, [csb_, self.rbfb])])
        else:
            ch, chb = self.csb["ones1h"]
            k.mm(psum_b, psum_t[0:64, 0:n],
                 [(ch[0:1, :], hi, [chb, self.rbfb]),
                  (ch[0:1, :], lo, [chb, self.rbfb])])
            k.mm(psum_b, psum_t[64:128, 0:n],
                 [(ch[0:1, :], pair2[0], [chb, self.rbfb]),
                  (ch[0:1, :], pair2[1], [chb, self.rbfb])])

    def use_rows(self, slots, tt, width=TT):
        k = self.k
        rd, rdb = self.d("rowsd")
        out = []
        for i, sl in enumerate(slots):
            fsl = self.rsc[0:1, i * TT:i * TT + width]
            k.dma("s", fsl, rd[sl:sl + 1, tt * TT:tt * TT + width],
                  ins=(rdb,), outs=(self.rscb,))
            hi = self.rbf[0:1, (2 * i) * TT:(2 * i) * TT + width]
            lo = self.rbf[0:1, (2 * i + 1) * TT:(2 * i + 1) * TT + width]
            k.op("vector", "tensor_copy", hi, fsl, ins=(self.rscb,), outs=(self.rbfb,))
            k.op("vector", "tensor_tensor", lo, fsl, hi, ALU.subtract,
                 ins=(self.rscb, self.rbfb), outs=(self.rbfb,))
            out.append((hi, lo))
        return out

    def stats_pass(self, src_name, kind, out_slots):
        k = self.k
        src, srcb = self.d(src_name)
        rd, rdb = self.d("rowsd")
        cs1, cs1b = self.csb["onesc1"]
        xbt, xbtb = self.s1b, self.s1bb
        xsq, xsqb = self.s2b, self.s2bb
        R, Rb = self.rsc, self.rscb

        def rs(i):
            return R[0:1, i * TT:(i + 1) * TT]

        for tt in range(NTT):
            sl = slice(tt * TT, (tt + 1) * TT)
            for c in range(NCH):
                k.dma("g", xbt[:, c * TT:(c + 1) * TT],
                      src[c * 128:(c + 1) * 128, sl], ins=(srcb,), **gw(xbtb, c, NCH))
            for c in range(NCH):
                k.op("scalar", "activation", xsq[:, c * TT:(c + 1) * TT],
                     xbt[:, c * TT:(c + 1) * TT], AF.Square, ins=(xbtb,),
                     **gw(xsqb, c, NCH))
            p2, p2b = self.bank()
            k.mm(p2b, p2[0:1, 0:TT],
                 [(cs1[:, 0:1], xsq[:, c * TT:(c + 1) * TT], [cs1b, xsqb])
                  for c in range(NCH)])
            if kind in ("ln", "lnrms"):
                p1, p1b = self.bank()
                k.mm(p1b, p1[0:1, 0:TT],
                     [(cs1[:, 0:1], xbt[:, c * TT:(c + 1) * TT], [cs1b, xbtb])
                      for c in range(NCH)])
                k.op("vector", "tensor_scalar", rs(0), p1[0:1, 0:TT], 1.0 / D, None,
                     ALU.mult, ins=(p1b,), outs=(Rb,))
                k.op("vector", "tensor_tensor", rs(1), rs(0), rs(0), ALU.mult,
                     ins=(Rb,), outs=(Rb,))
                k.op("vector", "scalar_tensor_tensor", rs(1), p2[0:1, 0:TT], 1.0 / D,
                     rs(1), ALU.mult, ALU.subtract, ins=(p2b, Rb), outs=(Rb,))
                k.op("scalar", "activation", rs(2), rs(1), AF.Ln, bias=np.float32(1e-5),
                     ins=(Rb,), outs=(Rb,))
                k.op("scalar", "activation", rs(2), rs(2), AF.Exp, scale=-0.5,
                     ins=(Rb,), outs=(Rb,))
                k.op("vector", "tensor_tensor", rs(3), rs(0), rs(2), ALU.mult,
                     ins=(Rb,), outs=(Rb,))
                k.op("vector", "tensor_scalar", rs(3), rs(3), -1.0, None, ALU.mult,
                     ins=(Rb,), outs=(Rb,))
                k.dma("s", rd[out_slots[0]:out_slots[0] + 1, sl], rs(2), ins=(Rb,), outs=(rdb,))
                k.dma("s", rd[out_slots[1]:out_slots[1] + 1, sl], rs(3), ins=(Rb,), outs=(rdb,))
                if kind == "lnrms":
                    k.op("vector", "tensor_tensor", rs(0), rs(2), rs(2), ALU.mult,
                         ins=(Rb,), outs=(Rb,))
                    k.op("vector", "tensor_tensor", rs(0), rs(0), rs(1), ALU.mult,
                         ins=(Rb,), outs=(Rb,))
                    k.op("scalar", "activation", rs(0), rs(0), AF.Ln, bias=np.float32(1e-6),
                         ins=(Rb,), outs=(Rb,))
                    k.op("scalar", "activation", rs(0), rs(0), AF.Exp, scale=-0.5,
                         ins=(Rb,), outs=(Rb,))
                    k.op("vector", "tensor_tensor", rs(1), rs(2), rs(0), ALU.mult,
                         ins=(Rb,), outs=(Rb,))
                    k.dma("s", rd[out_slots[2]:out_slots[2] + 1, sl], rs(1), ins=(Rb,), outs=(rdb,))
                    k.op("vector", "tensor_tensor", rs(1), rs(3), rs(0), ALU.mult,
                         ins=(Rb,), outs=(Rb,))
                    k.dma("s", rd[out_slots[3]:out_slots[3] + 1, sl], rs(1), ins=(Rb,), outs=(rdb,))
                    k.op("vector", "tensor_scalar", rs(1), rs(2), 1.0, None, ALU.add,
                         ins=(Rb,), outs=(Rb,))
                    k.dma("s", rd[out_slots[4]:out_slots[4] + 1, sl], rs(1), ins=(Rb,), outs=(rdb,))
            else:
                k.op("scalar", "activation", rs(2), p2[0:1, 0:TT], AF.Ln, bias=np.float32(1e-6),
                     scale=1.0 / D, ins=(p2b,), outs=(Rb,))
                k.op("scalar", "activation", rs(2), rs(2), AF.Exp, scale=-0.5,
                     ins=(Rb,), outs=(Rb,))
                k.dma("s", rd[out_slots[0]:out_slots[0] + 1, sl], rs(2), ins=(Rb,), outs=(rdb,))



def build_l0a(P):
    k = P.k
    src_name = "x_in" if P.blocks[0] == "l0a" else "xdram"
    dst_name = "x_out" if P.blocks[-1] == "l0a" else "xdram"
    src, srcb = P.d(src_name)
    dst, dstb = P.d(dst_name)
    gate_d, gate_db = P.d("gate_sp")

    for w in ("w_l0a_in", "w_l0a_pw", "w_l0a_win", "w_l0a_wout", "w_l0a_out"):
        P.precast(w)
    v_inb = P.load_vec("v_l0a_inb")
    v_dwb = P.load_vec("v_l0a_dwb")
    v_pwb = P.load_vec("v_l0a_pwb")
    v_outb = P.load_vec("v_l0a_outb")
    v_ns = P.load_vec("v_l0a_ns")
    v_dww = P.load_vec("v_l0a_dww")
    v_ssm = P.load_vec("v_l0a_ssm")

    P.stats_pass(src_name, "lnrms", [0, 1, 2, 3, 4])

    sc = P.sm[:, 0:8]
    scb = P.smb
    ssm_t, ssm_b = v_ssm

    def scol(i):
        return sc[:, i:i + 1]

    k.op("scalar", "activation", scol(0), ssm_t[:, 3:4], AF.Exp, ins=(ssm_b,), outs=(scb,))
    k.op("scalar", "activation", scol(1), ssm_t[:, 0:1], AF.Exp, ins=(ssm_b,), outs=(scb,))
    k.op("vector", "tensor_scalar", scol(1), scol(1), -1.0, None, ALU.mult, ins=(scb,), outs=(scb,))
    k.op("vector", "tensor_tensor", scol(2), scol(1), scol(0), ALU.mult, ins=(scb,), outs=(scb,))
    k.op("scalar", "activation", scol(2), scol(2), AF.Exp, ins=(scb,), outs=(scb,))
    k.op("scalar", "activation", scol(3), scol(1), AF.Abs, ins=(scb,), outs=(scb,))
    k.op("vector", "tensor_scalar", scol(4), scol(3), 1e-6, None, ALU.is_lt, ins=(scb,), outs=(scb,))
    k.op("vector", "tensor_tensor", scol(5), scol(1), scol(4), ALU.mult, ins=(scb,), outs=(scb,))
    k.op("vector", "tensor_tensor", scol(5), scol(1), scol(5), ALU.subtract, ins=(scb,), outs=(scb,))
    k.op("vector", "tensor_tensor", scol(5), scol(5), scol(4), ALU.add, ins=(scb,), outs=(scb,))
    k.op("vector", "reciprocal", scol(6), scol(5), ins=(scb,), outs=(scb,))
    k.op("vector", "tensor_scalar", scol(7), scol(2), -1.0, None, ALU.add, ins=(scb,), outs=(scb,))
    k.op("vector", "tensor_tensor", scol(6), scol(6), scol(7), ALU.mult, ins=(scb,), outs=(scb,))
    k.op("vector", "tensor_tensor", scol(7), scol(6), scol(4), ALU.mult, ins=(scb,), outs=(scb,))
    k.op("vector", "tensor_tensor", scol(6), scol(6), scol(7), ALU.subtract, ins=(scb,), outs=(scb,))
    k.op("vector", "tensor_tensor", scol(7), scol(0), scol(4), ALU.mult, ins=(scb,), outs=(scb,))
    k.op("vector", "tensor_tensor", scol(6), scol(6), scol(7), ALU.add, ins=(scb,), outs=(scb,))
    k.op("vector", "tensor_tensor", scol(7), scol(6), ssm_t[:, 1:2], ALU.mult,
         ins=(scb, ssm_b), outs=(scb,))

    w_in = P.load_w("w_l0a_in", 0)
    tn, tnb = P.s2b, P.s2bb
    uss, ussb = P.f1, P.f1b

    for s in range(2):
        xh, xhb = (P.abig, P.abigb) if s == 0 else (P.bbig, P.bbigb)
        for lt in range(TPS):
            tt = s * TPS + lt
            sl = slice(tt * TT, (tt + 1) * TT)
            rws = P.use_rows([2, 3], tt)
            xbt, xbtb = P.s1b, P.s1bb
            for c in range(NCH):
                k.dma("g", xbt[:, c * TT:(c + 1) * TT],
                      src[c * 128:(c + 1) * 128, sl], ins=(srcb,), **gw(xbtb, c, NCH))
            P.bcast(rws[0][0], rws[0][1], P.pb1, P.pb1b, TT)
            P.bcast(rws[1][0], rws[1][1], P.pb2, P.pb2b, TT)
            for c in range(NCH):
                csl = slice(c * TT, (c + 1) * TT)
                k.op("vector", "tensor_tensor", tn[:, csl], xbt[:, csl],
                     P.pb1[:, 0:TT], ALU.mult, ins=(xbtb, P.pb1b), outs_war=(tnb,))
                k.op("vector", "tensor_tensor", tn[:, csl], tn[:, csl],
                     P.pb2[:, 0:TT], ALU.add, ins=(P.pb2b,))
                k.op("vector", "tensor_scalar", tn[:, csl], tn[:, csl],
                     v_ns[0][:, c:c + 1], None, ALU.mult, ins=(v_ns[1],),
                     finals=(tnb,) if c == NCH - 1 else ())
            for mc in range(16):
                pt_, pb_ = P.bank()
                k.mm(pb_, pt_[:, 0:TT],
                     [(w_in(kc, mc), tn[:, kc * TT:(kc + 1) * TT], [P.wslabb, tnb])
                      for kc in range(NCH)])
                if mc < 8:
                    k.op("vector", "tensor_scalar",
                         xh[:, mc * SEQ + lt * TT: mc * SEQ + (lt + 1) * TT],
                         pt_[:, 0:TT], v_inb[0][:, mc:mc + 1], None, ALU.add,
                         ins=(pb_, v_inb[1]), **gw(xhb, lt * 8 + mc, TPS * 8))
                else:
                    gt = P.smb_t[:, 0:TT]
                    k.op("scalar", "activation", gt, pt_[:, 0:TT], AF.Sigmoid,
                         bias=v_inb[0][:, mc:mc + 1], ins=(pb_, v_inb[1]),
                         outs=(P.smb_b,))
                    k.dma("s", gate_d[(mc - 8) * 128:(mc - 7) * 128, sl], gt,
                          ins=(P.smb_b,), outs=(gate_db,))

    w_pw = P.load_w("w_l0a_pw", 0)
    w_win = P.load_w("w_l0a_win", 8192)

    for s in range(2):
        xh, xhb = (P.abig, P.abigb) if s == 0 else (P.bbig, P.bbigb)
        for lt in range(TPS):
            tt = s * TPS + lt
            cvt, cvtb = P.s1b, P.s1bb
            acc = P.sm[:, 512:1024]
            accb = P.smb
            t0_, t1_ = lt * TT, (lt + 1) * TT
            for c in range(NCH):
                xrow = xh[:, c * SEQ: (c + 1) * SEQ]
                w0 = v_dww[0][:, c * 3 + 0: c * 3 + 1]
                w1 = v_dww[0][:, c * 3 + 1: c * 3 + 2]
                w2 = v_dww[0][:, c * 3 + 2: c * 3 + 3]
                k.op("vector", "tensor_scalar", acc, xrow[:, t0_:t1_],
                     w1, v_dwb[0][:, c:c + 1], ALU.mult, ALU.add,
                     ins=(xhb, v_dww[1], v_dwb[1]), outs=(accb,))
                l0 = 1 if lt == 0 else 0
                k.op("vector", "scalar_tensor_tensor", acc[:, l0:TT],
                     xrow[:, t0_ + l0 - 1:t1_ - 1], w0, acc[:, l0:TT],
                     ALU.mult, ALU.add, ins=(xhb,), outs=(accb,))
                r = TT - 1 if lt == TPS - 1 else TT
                k.op("vector", "scalar_tensor_tensor",
                     cvt[:, c * TT:c * TT + r],
                     xrow[:, t0_ + 1:t0_ + 1 + r], w2, acc[:, 0:r],
                     ALU.mult, ALU.add, ins=(xhb, accb), outs_war=(cvtb,))
                if r < TT:
                    k.op("vector", "tensor_copy", cvt[:, c * TT + r:(c + 1) * TT],
                         acc[:, r:TT], ins=(accb,))
                if c == NCH - 1:
                    k.op("vector", "engine_nop", finals=(cvtb,))
            xcv, xcvb = P.s2b, P.s2bb
            for mc in range(NCH):
                pt_, pb_ = P.bank()
                k.mm(pb_, pt_[:, 0:TT],
                     [(w_pw(kc, mc), cvt[:, kc * TT:(kc + 1) * TT], [P.wslabb, cvtb])
                      for kc in range(NCH)])
                k.op("vector", "tensor_scalar", xcv[:, mc * TT:(mc + 1) * TT],
                     pt_[:, 0:TT], v_pwb[0][:, mc:mc + 1], None, ALU.add,
                     ins=(pb_, v_pwb[1]), **gw(xcvb, mc, NCH))
            pt_, pb_ = P.bank()
            k.mm(pb_, pt_[:, 0:TT],
                 [(w_win(kc, 0), xcv[:, kc * TT:(kc + 1) * TT], [P.wslabb, xcvb])
                  for kc in range(NCH)])
            k.op("vector", "tensor_copy", uss[:, tt * TT:(tt + 1) * TT],
                 pt_[:, 0:TT], ins=(pb_,), **gw(ussb, tt, NTT))

    bst, bstb = P.s2b, P.s2bb
    k.op("vector", "tensor_scalar", bst[:], uss[:], scol(7), None, ALU.mult,
         ins=(ussb, scb), outs=(bstb,))
    abt, abtb = k.sb("l0a_abt", [128, SEQ], F32)
    k.op("vector", "memset", abt[:], 1.0, outs=(abtb,))
    k.op("vector", "tensor_scalar", abt[:], abt[:], scol(2), None, ALU.mult,
         ins=(scb,), outs=(abtb,))
    for s in range(2):
        sl = slice(s * SEQ, (s + 1) * SEQ)
        k.op("vector", "tensor_tensor_scan", uss[0:64, sl], abt[0:64, :],
             bst[0:64, sl], 0.0, ALU.mult, ALU.add, ins=(bstb, abtb), outs=(ussb,))
        k.op("vector", "tensor_tensor_scan", uss[64:128, sl][:, ::-1],
             abt[64:128, :][:, ::-1], bst[64:128, sl][:, ::-1], 0.0,
             ALU.mult, ALU.add, ins=(bstb, abtb), outs=(ussb,))
    ysb, ysbb = P.s1b, P.s1bb
    k.op("vector", "tensor_scalar", ysb[:], uss[:], ssm_t[:, 2:3], None, ALU.mult,
         ins=(ussb, ssm_b), outs=(ysbb,))

    w_wout = P.load_w("w_l0a_wout", 0)
    w_out = P.load_w("w_l0a_out", 1024)
    ygb, ygbb = P.s2b, P.s2bb
    xf, xfb = P.f1, P.f1b
    res = P.sm[:, 1024:1536]
    resb = P.smb
    for tt in range(NTT):
        sl = slice(tt * TT, (tt + 1) * TT)
        rws = P.use_rows([4, 1], tt)
        for c in range(NCH):
            k.dma("g", xf[:, c * TT:(c + 1) * TT], src[c * 128:(c + 1) * 128, sl],
                  ins=(srcb,), **gw(xfb, c, NCH))
        for mc in range(NCH):
            gt = P.smb_t[:, 1024 + (mc % 2) * TT: 1024 + (mc % 2 + 1) * TT]
            k.dma("s", gt, gate_d[mc * 128:(mc + 1) * 128, sl],
                  ins=(gate_db,), outs=(P.smb_b,))
            pt_, pb_ = P.bank()
            k.mm(pb_, pt_[:, 0:TT], [(w_wout(0, mc), ysb[:, sl], [P.wslabb, ysbb])])
            k.op("vector", "tensor_tensor", ygb[:, mc * TT:(mc + 1) * TT],
                 pt_[:, 0:TT], gt, ALU.mult, ins=(pb_, P.smb_b), **gw(ygbb, mc, NCH))
        P.bcast(rws[0][0], rws[0][1], P.pb1, P.pb1b, TT)
        P.bcast(rws[1][0], rws[1][1], P.pb2, P.pb2b, TT)
        for mc in range(NCH):
            pt_, pb_ = P.bank()
            k.mm(pb_, pt_[:, 0:TT],
                 [(w_out(kc, mc), ygb[:, kc * TT:(kc + 1) * TT], [P.wslabb, ygbb])
                  for kc in range(NCH)])
            csl = slice(mc * TT, (mc + 1) * TT)
            k.op("vector", "tensor_tensor", res, xf[:, csl], P.pb1[:, 0:TT],
                 ALU.mult, ins=(xfb, P.pb1b), outs=(resb,))
            k.op("vector", "tensor_tensor", res, res, P.pb2[:, 0:TT],
                 ALU.add, ins=(P.pb2b,))
            k.op("vector", "tensor_tensor", res, res, pt_[:, 0:TT],
                 ALU.add, ins=(pb_,))
            k.op("vector", "tensor_scalar", res, res,
                 v_outb[0][:, mc:mc + 1], None, ALU.add, ins=(v_outb[1],),
                 outs=(resb,))
            k.dma("s", dst[mc * 128:(mc + 1) * 128, sl], res, ins=(resb,),
                  **gw(dstb, tt * NCH + mc, NTT * NCH))




def build_l0f(P):
    k = P.k
    i = P.blocks.index("l0f")
    src_name = "x_in" if i == 0 else "xdram"
    dst_name = "x_out" if i == len(P.blocks) - 1 else "xdram"
    src, srcb = P.d(src_name)
    dst, dstb = P.d(dst_name)
    sdram, sdramb = P.d("sdram")
    ebuf, ebufb = P.d("ebuf")
    ybuf, ybufb = P.d("ybuf")

    for w in ("w_l0f_qw", "w_l0f_kw", "w_l0f_vw", "w_l0f_ow", "w_l0f_gate"):
        P.precast(w)
    P.precast("w_l0f_w1", 8)
    P.precast("w_l0f_w2", 8)
    v_qb = P.load_vec("v_l0f_qb")
    v_kb = P.load_vec("v_l0f_kb")
    v_vb = P.load_vec("v_l0f_vb")
    v_ob = P.load_vec("v_l0f_ob")
    v_n1 = P.load_vec("v_l0f_n1")
    v_n2 = P.load_vec("v_l0f_n2")

    P.stats_pass(src_name, "ln", [0, 1])

    vt, vtb = P.vt, P.vtb
    cs1, cs1b = P.csb["onesc1"]

    for s in range(2):
        w_q = P.load_w("w_l0f_qw", 0)
        w_kk = P.load_w("w_l0f_kw", 8192)
        w_v = P.load_w("w_l0f_vw", 10240)
        ssl = slice(s * SEQ, (s + 1) * SEQ)
        x_in, x_inb = P.abig, P.abigb
        for lt in range(TPS):
            tt = s * TPS + lt
            sl = slice(tt * TT, (tt + 1) * TT)
            rws = P.use_rows([0, 1], tt)
            xbt, xbtb = P.s1b, P.s1bb
            for c in range(NCH):
                k.dma("g", xbt[:, c * TT:(c + 1) * TT],
                      src[c * 128:(c + 1) * 128, sl], ins=(srcb,), **gw(xbtb, c, NCH))
            P.bcast(rws[0][0], rws[0][1], P.pb1, P.pb1b, TT)
            P.bcast(rws[1][0], rws[1][1], P.pb2, P.pb2b, TT)
            for c in range(NCH):
                csl = slice(c * TT, (c + 1) * TT)
                osl = slice(c * SEQ + lt * TT, c * SEQ + (lt + 1) * TT)
                k.op("vector", "tensor_tensor", x_in[:, osl], xbt[:, csl],
                     P.pb1[:, 0:TT], ALU.mult, ins=(xbtb, P.pb1b), outs_war=(x_inb,))
                k.op("vector", "tensor_tensor", x_in[:, osl], x_in[:, osl],
                     P.pb2[:, 0:TT], ALU.add, ins=(P.pb2b,),
                     finals=(x_inb,) if (lt == TPS - 1 and c == NCH - 1) else ())
        ksb, ksbb = P.s1b, P.s1bb
        vsb, vsbb = P.s2b, P.s2bb
        for mc in range(2):
            for lt in range(TPS):
                pt_, pb_ = P.bank()
                k.mm(pb_, pt_[:, 0:TT],
                     [(w_kk(kc, mc), x_in[:, kc * SEQ + lt * TT: kc * SEQ + (lt + 1) * TT],
                       [P.wslabb, x_inb]) for kc in range(NCH)])
                k.op("vector", "tensor_scalar",
                     ksb[:, mc * SEQ + lt * TT: mc * SEQ + (lt + 1) * TT],
                     pt_[:, 0:TT], v_kb[0][:, mc:mc + 1], None, ALU.add,
                     ins=(pb_, v_kb[1]), **gw(ksbb, mc * TPS + lt, 2 * TPS))
                pt_, pb_ = P.bank()
                k.mm(pb_, pt_[:, 0:TT],
                     [(w_v(kc, mc), x_in[:, kc * SEQ + lt * TT: kc * SEQ + (lt + 1) * TT],
                       [P.wslabb, x_inb]) for kc in range(NCH)])
                k.op("vector", "tensor_scalar",
                     vsb[:, mc * SEQ + lt * TT: mc * SEQ + (lt + 1) * TT],
                     pt_[:, 0:TT], v_vb[0][:, mc:mc + 1], None, ALU.add,
                     ins=(pb_, v_vb[1]), **gw(vsbb, mc * TPS + lt, 2 * TPS))
        k.op("vector", "memset", vt[:], 1.0, outs=(vtb,))
        idt, idtb = P.csb["ident"]
        for kvh in range(4):
            vrow = vsb[(kvh % 2) * 64:(kvh % 2) * 64 + 64, :]
            for g2 in range(2):
                for j in range(8):
                    kt = g2 * 8 + j
                    k.op("tensor", "transpose", P.pt[:, j * 128:j * 128 + 64],
                         vsb[(kvh % 2) * 64:(kvh % 2) * 64 + 64,
                             (kvh // 2) * SEQ + kt * 128:(kvh // 2) * SEQ + (kt + 1) * 128],
                         idt[(kvh % 2) * 64:(kvh % 2) * 64 + 64,
                             (kvh % 2) * 64:(kvh % 2) * 64 + 64], ins=(vsbb, idtb),
                         outs_war=(P.ptb,) if j == 0 else (),
                         finals=(P.ptb,) if j == 7 else ())
                k.op("vector", "tensor_copy",
                     vt[:, (kvh * 16 + g2 * 8) * 65:(kvh * 16 + g2 * 8) * 65 + 8 * 65].rearrange(
                         "p (t c) -> p t c", c=65)[:, :, 0:64],
                     P.pt[:, 0:1024].rearrange("p (t c) -> p t c", c=128)[:, :, 0:64],
                     ins=(P.ptb,), outs_war=(vtb,),
                     finals=(vtb,) if (kvh == 3 and g2 == 1) else ())
        asb, asbb = P.bbig, P.bbigb
        qh = P.smb_t
        qhb = P.smb_b
        at0 = P.s2b
        at0b = P.s2bb
        PAIRS = [(0, 4), (1, 5), (2, 6), (3, 7), (8, 12), (9, 13), (10, 14), (11, 15)]
        for j, (ha, hb) in enumerate(PAIRS):
            for lt in range(TPS):
                pt_, pb_ = P.bank()
                k.mm(pb_, pt_[:, 0:TT],
                     [(w_q(kc, mc=j),
                       x_in[:, kc * SEQ + lt * TT: kc * SEQ + (lt + 1) * TT],
                       [P.wslabb, x_inb]) for kc in range(NCH)])
                k.op("vector", "tensor_scalar", qh[:, lt * TT:(lt + 1) * TT],
                     pt_[:, 0:TT], v_qb[0][:, j:j + 1], None, ALU.add,
                     ins=(pb_, v_qb[1]), **gw(qhb, lt, TPS))
            for hh, h in enumerate((ha, hb)):
                qb0 = hh * 64
                kvh = h // 4
                assert (kvh % 2) == hh
                for qc in range(TPS):
                    qsl = slice(qc * TT, (qc + 1) * TT)
                    avb = P.pb2b
                    sbanks = {}

                    def score(kt):
                        pt_, pb_ = P.bank()
                        k.mm(pb_, pt_[:, 0:TT],
                             [(ksb[qb0:qb0 + 64,
                                   (kvh // 2) * SEQ + kt * 128:(kvh // 2) * SEQ + (kt + 1) * 128],
                               qh[qb0:qb0 + 64, qsl], [ksbb, qhb])])
                        sbanks[kt] = (pt_, pb_)
                    score(0)
                    for kt in range(16):
                        if kt + 1 < 16:
                            score(kt + 1)
                        pt_, pb_ = sbanks.pop(kt)
                        atile = at0[:, (kt % 2) * TT:(kt % 2 + 1) * TT]
                        k.op("scalar", "activation", atile, pt_[0:128, 0:TT], AF.Exp,
                             scale=0.125, ins=(pb_,), outs=(at0b,))
                        k.op("tensor", "matmul", P.pb2[0:65, 0:TT],
                             vt[:, (kvh * 16 + kt) * 65:(kvh * 16 + kt + 1) * 65],
                             atile, start=(kt == 0), stop=(kt == 15),
                             ins=(vtb, at0b),
                             outs_war=(avb,) if kt == 0 else (),
                             finals=(avb,) if kt == 15 else ())
                    avs = P.sm[:, 0:TT]
                    k.op("vector", "tensor_copy", avs[0:65, :], P.pb2[0:65, 0:TT],
                         ins=(avb,), outs=(P.smb,))
                    rrow = P.sm[0:1, TT:2 * TT]
                    k.op("vector", "reciprocal", rrow, avs[64:65, :], ins=(P.smb,), outs=(P.smb,))
                    hi = P.rbf[0:1, 0:TT]
                    lo = P.rbf[0:1, TT:2 * TT]
                    k.op("vector", "tensor_copy", hi, rrow, ins=(P.smb,), outs=(P.rbfb,))
                    k.op("vector", "tensor_tensor", lo, rrow, hi, ALU.subtract,
                         ins=(P.smb, P.rbfb), outs=(P.rbfb,))
                    P.bcast(hi, lo, P.pb1, P.pb1b, TT)
                    dst_sl = asb[(h % 2) * 64:(h % 2) * 64 + 64,
                                 (h // 2) * SEQ + qc * TT:(h // 2) * SEQ + (qc + 1) * TT]
                    fin = dict(outs_war=(asbb,),
                               finals=(asbb,) if (j == 7 and hh == 1 and qc == TPS - 1) else ())
                    if h % 2 == 0:
                        k.op("vector", "tensor_tensor", dst_sl,
                             avs[0:64, :], P.pb1[0:64, 0:TT], ALU.mult,
                             ins=(P.smb, P.pb1b), **fin)
                    else:
                        tmp = at0[:, 1024:1024 + TT]
                        k.op("vector", "tensor_tensor", tmp[0:64, :],
                             avs[0:64, :], P.pb1[0:64, 0:TT], ALU.mult,
                             ins=(P.smb, P.pb1b), outs=(at0b,))
                        idtc, idtcb = P.csb["ident"]
                        prl, prlb = P.bank()
                        k.op("tensor", "matmul", prl[64:128, 0:TT],
                             idtc[0:64, 0:64], tmp[0:64, :], start=True, stop=True,
                             ins=(idtcb, at0b), outs=(prlb,))
                        k.op("vector", "tensor_copy", dst_sl, prl[64:128, 0:TT],
                             ins=(prlb,), **fin)
        if DBG and s == 1:
            for nm, t_, b_ in [("vt", vt, vtb), ("asb", P.bbig, P.bbigb),
                               ("ksb", P.s1b, P.s1bb)]:
                dd = P.nc.dram_tensor("dbg_" + nm, list(t_[:].shape), t_[:].dtype,
                                      kind="ExternalOutput")
                P.k.dma("c", dd.ap(), t_[:], ins=(b_,), outs=(Buf("d" + nm),))
        w_o = P.load_w("w_l0f_ow", 0)
        if DBG and s == 1:
            dws = P.nc.dram_tensor("dbg_wslab", [128, 8192], BF16, kind="ExternalOutput")
            P.k.dma("c", dws.ap(), P.wslab[:, 0:8192], ins=(P.wslabb,), outs=(Buf("dw"),))
        s1t, s1tb = P.f1, P.f1b
        sq = P.smb_t
        sqb = P.smb_b
        for lt in range(TPS):
            tt = s * TPS + lt
            sl = slice(tt * TT, (tt + 1) * TT)
            p2, p2b_ = P.pb2, P.pb2b
            for mc in range(NCH):
                pt_, pb_ = P.bank()
                k.mm(pb_, pt_[:, 0:TT],
                     [(w_o(kc, mc), asb[:, kc * SEQ + lt * TT: kc * SEQ + (lt + 1) * TT],
                       [P.wslabb, asbb]) for kc in range(NCH)])
                csl = slice(mc * TT, (mc + 1) * TT)
                k.op("vector", "tensor_scalar", s1t[:, csl], pt_[:, 0:TT],
                     v_ob[0][:, mc:mc + 1], None, ALU.add, ins=(pb_, v_ob[1]),
                     outs_war=(s1tb,))
                k.op("vector", "tensor_tensor", s1t[:, csl], s1t[:, csl],
                     x_in[:, mc * SEQ + lt * TT: mc * SEQ + (lt + 1) * TT],
                     ALU.add, ins=(x_inb,), finals=(s1tb,) if mc == NCH - 1 else ())
            for mc in range(NCH):
                csl = slice(mc * TT, (mc + 1) * TT)
                k.dma("s", sdram[mc * 128:(mc + 1) * 128, sl], s1t[:, csl],
                      ins=(s1tb,), **gw(sdramb, (s * TPS + lt) * NCH + mc, NTT * NCH))
                k.op("scalar", "activation", sq[:, 0:TT], s1t[:, csl], AF.Square,
                     ins=(s1tb,), outs=(sqb,))
                k.op("tensor", "matmul", p2[0:1, 0:TT], cs1[:, 0:1], sq[:, 0:TT],
                     start=(mc == 0), stop=(mc == NCH - 1), ins=(cs1b, sqb),
                     outs_war=(p2b_,) if mc == 0 else (),
                     finals=(p2b_,) if mc == NCH - 1 else ())
            rd, rdb = P.d("rowsd")
            r2row = P.sm[0:1, 0:TT]
            k.op("scalar", "activation", r2row, p2[0:1, 0:TT], AF.Ln,
                 bias=np.float32(1e-6), scale=1.0 / D, ins=(p2b_,), outs=(P.smb,))
            k.op("scalar", "activation", r2row, r2row, AF.Exp, scale=-0.5,
                 ins=(P.smb,), outs=(P.smb,))
            k.dma("s", rd[5:6, sl], r2row, ins=(P.smb,), outs=(rdb,))

    P.dbg_dump("s1", "sdram")
    build_l0f_moe(P, dst, dstb)


def build_l0f_moe(P, dst, dstb):
    k = P.k
    sdram, sdramb = P.d("sdram")
    ebuf, ebufb = P.d("ebuf")
    ybuf, ybufb = P.d("ybuf")
    cc_in, cc_inb = P.d("cc_in")
    cc_out, cc_outb = P.d("cc_out")
    cmask, cmaskb = P.d("core_mask")
    rd, rdb = P.d("rowsd")
    v_n1 = P.load_vec("v_l0f_n1")
    v_n2 = P.load_vec("v_l0f_n2")
    w_g = P.load_w("w_l0f_gate", 0)
    cs1, cs1b = P.csb["onesc1"]
    us, usb = P.csb["ustrict"]
    io8, io8b = P.csb["iota8"]
    io8B, io8Bb = P.csb["iota8B"]
    idt, idtb = P.csb["ident"]
    ones1, ones1b = P.csb["ones1"]

    meta, metab = k.sb("l0f_meta", [128, 6 * 32], F32)
    sidx, sidxb = k.sb("l0f_sidx", [128, 64], I32)
    run_, runb = k.sb("l0f_run", [4, 16], F32)
    runbf, runbfb = k.sb("l0f_runbf", [1, 32], BF16)

    k.op("vector", "memset", run_[0:1, 0:8], 0.0, outs=(runb,))
    k.op("vector", "memset", runbf[:], 0.0, outs=(runbfb,))

    for tt in range(NTT):
        sl = slice(tt * TT, (tt + 1) * TT)
        s1t, s1tb = P.f1, P.f1b
        for c in range(NCH):
            k.dma("g", s1t[:, c * TT:(c + 1) * TT], sdram[c * 128:(c + 1) * 128, sl],
                  ins=(sdramb,), **gw(s1tb, c, NCH))
        rws = P.use_rows([5], tt)
        P.bcast(rws[0][0], rws[0][1], P.pb1, P.pb1b, TT)
        s2t, s2tb = P.s1b, P.s1bb
        for c in range(NCH):
            csl = slice(c * TT, (c + 1) * TT)
            k.op("vector", "tensor_tensor", s2t[:, csl], s1t[:, csl],
                 P.pb1[:, 0:TT], ALU.mult, ins=(s1tb, P.pb1b), outs_war=(s2tb,))
            k.op("vector", "tensor_scalar", s2t[:, csl], s2t[:, csl],
                 v_n1[0][:, c:c + 1], None, ALU.mult, ins=(v_n1[1],),
                 finals=(s2tb,) if c == NCH - 1 else ())
        pg, pgb = P.bank()
        k.mm(pgb, pg[0:8, 0:TT],
             [(w_g(kc, mo=0, width=8), s2t[:, kc * TT:(kc + 1) * TT], [P.wslabb, s2tb])
              for kc in range(NCH)])
        probs = P.sm[:, 0:TT]
        k.op("scalar", "activation", probs[0:8, :], pg[0:8, 0:TT], AF.Sigmoid,
             ins=(pgb,), outs=(P.smb,))
        for st in range(4):
            stt = tt * 4 + st
            pts, ptsb = P.bank()
            idf, idfb = P.csb["identf"]
            k.op("tensor", "transpose", pts[:, 0:8],
                 probs[0:8, st * 128:(st + 1) * 128], idf[0:8, 0:8], ins=(P.smb, idfb),
                 outs=(ptsb,))
            pm = P.sm[:, TT:TT + 8]
            k.op("vector", "tensor_copy", pm, pts[:, 0:8], ins=(ptsb,), outs=(P.smb,))
            w1c = P.sm[:, TT + 8:TT + 9]; w2c = P.sm[:, TT + 9:TT + 10]
            i1c = P.sm[:, TT + 10:TT + 11]; i2c = P.sm[:, TT + 11:TT + 12]
            em = P.sm[:, TT + 16:TT + 24]
            em2 = P.sm[:, TT + 24:TT + 32]
            emsb = P.smb_t[:, 0:8]
            k.op("vector", "tensor_reduce", w1c, pm, mybir.AxisListType.X, ALU.max,
                 ins=(P.smb,), outs=(P.smb,))
            k.op("vector", "tensor_scalar", em, pm, w1c, None, ALU.is_equal,
                 ins=(P.smb,), outs=(P.smb,))
            if DBG and tt == 0 and st == 0:
                dbe2 = P.nc.dram_tensor("dbg_e2", [128, 24], F32, kind="ExternalOutput")
                dt2 = P.k.sb("dbg_e2t", [128, 24], F32)
                k.op("vector", "tensor_copy", dt2[0][:, 0:8], pm, ins=(P.smb,), outs=(dt2[1],))
                k.op("vector", "tensor_copy", dt2[0][:, 8:16], em, ins=(P.smb,), outs=(dt2[1],))
                k.op("vector", "tensor_copy", dt2[0][:, 16:17], w1c, ins=(P.smb,), outs=(dt2[1],))
                k.dma("c", dbe2.ap(), dt2[0][:], ins=(dt2[1],), outs=(Buf("dbe2"),))
            k.op("vector", "scalar_tensor_tensor", em2, em, -65536.0, io8B[:],
                 ALU.mult, ALU.add, ins=(P.smb, io8Bb), outs=(P.smb,))
            k.op("vector", "tensor_reduce", i1c, em2, mybir.AxisListType.X, ALU.min,
                 ins=(P.smb,), outs=(P.smb,))
            k.op("vector", "tensor_scalar", em, io8[:], i1c, None, ALU.is_equal,
                 ins=(P.smb, io8b), outs=(P.smb,))
            k.op("vector", "scalar_tensor_tensor", em2, em, -BIG, pm,
                 ALU.mult, ALU.add, ins=(P.smb,), outs=(P.smb,))
            k.op("vector", "tensor_reduce", w2c, em2, mybir.AxisListType.X, ALU.max,
                 ins=(P.smb,), outs=(P.smb,))
            k.op("vector", "tensor_scalar", em2, em2, w2c, None, ALU.is_equal,
                 ins=(P.smb,), outs=(P.smb,))
            k.op("vector", "scalar_tensor_tensor", em2, em2, -65536.0, io8B[:],
                 ALU.mult, ALU.add, ins=(P.smb, io8Bb), outs=(P.smb,))
            k.op("vector", "tensor_reduce", i2c, em2, mybir.AxisListType.X, ALU.min,
                 ins=(P.smb,), outs=(P.smb,))
            k.op("vector", "tensor_scalar", em2, io8[:], i2c, None, ALU.is_equal,
                 ins=(P.smb, io8b), outs=(P.smb,))
            wsum = P.sm[:, TT + 12:TT + 13]
            k.op("vector", "tensor_scalar", wsum, w1c, 1e-6, None, ALU.add,
                 ins=(P.smb,), outs=(P.smb,))
            k.op("vector", "tensor_tensor", wsum, wsum, w2c, ALU.add,
                 ins=(P.smb,), outs=(P.smb,))
            k.op("vector", "reciprocal", wsum, wsum, ins=(P.smb,), outs=(P.smb,))
            k.op("vector", "tensor_tensor", w1c, w1c, wsum, ALU.mult,
                 ins=(P.smb,), outs=(P.smb,))
            k.op("vector", "tensor_tensor", w2c, w2c, wsum, ALU.mult,
                 ins=(P.smb,), outs=(P.smb,))
            k.op("vector", "tensor_tensor", em, em, em2, ALU.add,
                 ins=(P.smb,), outs=(P.smb,))
            k.op("vector", "tensor_copy", emsb, em, ins=(P.smb,), outs=(P.smb_b,))
            ppre, ppreb = P.bank()
            k.mm(ppreb, ppre[:, 0:8],
                 [(us[:], emsb, [usb, P.smb_b]),
                  (ones1[0:1, :], runbf[0:1, 0:8], [ones1b, runbfb]),
                  (ones1[0:1, :], runbf[0:1, 8:16], [ones1b, runbfb])])
            pct, pctb = P.bank()
            k.mm(pctb, pct[0:1, 0:8], [(cs1[:, 0:1], emsb, [cs1b, P.smb_b])])
            k.op("vector", "tensor_scalar", em, io8[:], i1c, None, ALU.is_equal,
                 ins=(P.smb, io8b), outs=(P.smb,))
            tdot = P.sm[:, TT + 32:TT + 40]
            k.op("vector", "tensor_tensor", tdot, ppre[:, 0:8], em, ALU.mult,
                 ins=(ppreb, P.smb), outs=(P.smb,))
            lr1 = meta[:, stt:stt + 1]
            k.op("vector", "tensor_reduce", lr1, tdot, mybir.AxisListType.X, ALU.add,
                 ins=(P.smb,), outs=(metab,))
            k.op("vector", "tensor_scalar", em2, io8[:], i2c, None, ALU.is_equal,
                 ins=(P.smb, io8b), outs=(P.smb,))
            k.op("vector", "tensor_tensor", tdot, ppre[:, 0:8], em2, ALU.mult,
                 ins=(ppreb, P.smb), outs=(P.smb,))
            lr2 = meta[:, 32 + stt:32 + stt + 1]
            k.op("vector", "tensor_reduce", lr2, tdot, mybir.AxisListType.X, ALU.add,
                 ins=(P.smb,), outs=(metab,))
            k.op("vector", "tensor_copy", meta[:, 64 + stt:64 + stt + 1], i1c,
                 ins=(P.smb,), outs=(metab,))
            k.op("vector", "tensor_copy", meta[:, 96 + stt:96 + stt + 1], i2c,
                 ins=(P.smb,), outs=(metab,))
            k.op("vector", "tensor_copy", meta[:, 128 + stt:128 + stt + 1], w1c,
                 ins=(P.smb,), outs=(metab,))
            k.op("vector", "tensor_copy", meta[:, 160 + stt:160 + stt + 1], w2c,
                 ins=(P.smb,), outs=(metab,))
            if DBG and tt == 0 and st == 0:
                P._dbe = P.nc.dram_tensor("dbg_e1", [128, 64], F32, kind="ExternalOutput")
                P._dt = P.k.sb("dbg_e1t", [128, 64], F32)
                dt_ = P._dt
                k.op("vector", "tensor_copy", dt_[0][:, 0:8], pm, ins=(P.smb,), outs=(dt_[1],))
                k.op("vector", "tensor_copy", dt_[0][:, 8:16], em, ins=(P.smb,), outs=(dt_[1],))
                k.op("vector", "tensor_copy", dt_[0][:, 16:24], em2, ins=(P.smb,), outs=(dt_[1],))
                k.op("vector", "tensor_copy", dt_[0][:, 24:32], ppre[:, 0:8], ins=(ppreb,), outs=(dt_[1],))
                k.op("vector", "tensor_copy", dt_[0][0:1, 32:40], pct[0:1, 0:8], ins=(pctb,), outs=(dt_[1],))
            if DBG and tt == 0 and st == 1:
                dt_ = P._dt
                k.op("vector", "tensor_copy", dt_[0][0:1, 40:48], run_[0:1, 0:8],
                     ins=(runb,), outs=(dt_[1],))
                k.op("vector", "tensor_copy", dt_[0][0:1, 48:56], pct[0:1, 0:8],
                     ins=(pctb,), outs=(dt_[1],))
            k.op("vector", "tensor_tensor", run_[0:1, 0:8], run_[0:1, 0:8],
                 pct[0:1, 0:8], ALU.add, ins=(pctb, runb), outs=(runb,))
            k.op("vector", "tensor_copy", runbf[0:1, 0:8], run_[0:1, 0:8],
                 ins=(runb,), outs=(runbfb,))
            k.op("vector", "tensor_tensor", runbf[0:1, 8:16], run_[0:1, 0:8],
                 runbf[0:1, 0:8], ALU.subtract, ins=(runb, runbfb), outs=(runbfb,))
            for kk2, (ic, lrc, col) in enumerate(
                    [(i1c, lr1, 0), (i2c, lr2, 1)]):
                slc = P.sm[:, TT + 40 + kk2:TT + 41 + kk2]
                inb = P.sm[:, TT + 42 + kk2:TT + 43 + kk2]
                k.op("vector", "scalar_tensor_tensor", slc, ic, float(ECAP), lrc,
                     ALU.mult, ALU.add, ins=(P.smb, metab), outs=(P.smb,))
                k.op("vector", "tensor_scalar", inb, lrc, float(ECAP), None,
                     ALU.is_lt, ins=(metab,), outs=(P.smb,))
                k.op("vector", "tensor_scalar", slc, slc, OOBIDX, None,
                     ALU.subtract, ins=(P.smb,), outs=(P.smb,))
                k.op("vector", "tensor_tensor", slc, slc, inb, ALU.mult,
                     ins=(P.smb,), outs=(P.smb,))
                k.op("vector", "tensor_scalar", slc, slc, OOBIDX, None,
                     ALU.add, ins=(P.smb,), outs=(P.smb,))
                k.op("vector", "tensor_copy", sidx[:, stt * 2 + kk2:stt * 2 + kk2 + 1],
                     slc, ins=(P.smb,), outs=(sidxb,))
            for c in range(NCH):
                k.op("tensor", "transpose", P.pt[:, c * 128:(c + 1) * 128],
                     s2t[:, c * TT + st * 128: c * TT + (st + 1) * 128], idt[:],
                     ins=(s2tb, idtb), **gw(P.ptb, c, NCH))
            stm = P.smb_t[:, 0:1024]
            k.op("vector", "tensor_copy", stm, P.pt[:, 0:1024], ins=(P.ptb,),
                 outs=(P.smb_b,))
            for kk2 in range(2):
                k.idma(ebuf[:],
                       IndirectOffsetOnAxis(ap=sidx[:, stt * 2 + kk2:stt * 2 + kk2 + 1], axis=0),
                       stm, None, 8 * ECAP - 1,
                       ins=(P.smb_b, sidxb), **gw(ebufb, stt * 2 + kk2, 64))
    if DBG:
        k.op("vector", "tensor_copy", P._dt[0][0:1, 56:64], run_[0:1, 0:8],
             ins=(runb,), outs=(P._dt[1],))
        k.dma("c", P._dbe.ap(), P._dt[0][:], ins=(P._dt[1],), outs=(Buf("dbe"),))
    k.dma("s", cc_in[:], run_[0:1, 0:8], ins=(runb,), outs=(cc_inb,))
    k.op("gpsimd", "collective_compute", "AllGather", ALU.bypass,
         [[0, 1, 2, 3, 4, 5, 6, 7]], [cc_in.opt()], [cc_out.opt()],
         sem="cc", inc=1, ins=(cc_inb,), outs=(cc_outb,))
    call, callb = k.sb("l0f_call", [8, 8], F32)
    k.dma("s", call[:], cc_out[:], ins=(cc_outb,), outs=(callb,))
    cbf, cbfb = k.sb("l0f_cbf", [8, 17], BF16)
    k.op("vector", "tensor_copy", cbf[0:8, 0:8], call[:], ins=(callb,), outs=(cbfb,))
    k.op("vector", "tensor_tensor", call[:], call[:], cbf[0:8, 0:8], ALU.subtract,
         ins=(callb, cbfb), outs=(callb,))
    k.op("vector", "tensor_copy", cbf[0:8, 8:16], call[:], ins=(callb,), outs=(cbfb,))
    cmb, cmbb = k.sb("l0f_cmb", [8, 1], BF16)
    cmf, cmfb = k.sb("l0f_cmf", [8, 1], F32)
    k.dma("s", cmf[:], cmask[:], ins=(cmaskb,), outs=(cmfb,))
    k.op("vector", "tensor_copy", cmb[:], cmf[:], ins=(cmfb,), outs=(cmbb,))
    pco, pcob = P.bank()
    k.mm(pcob, pco[0:1, 0:8],
         [(cmb[:], cbf[0:8, 0:8], [cmbb, cbfb]),
          (cmb[:], cbf[0:8, 8:16], [cmbb, cbfb])])
    coffr = P.sm[0:1, 0:8]
    k.op("vector", "tensor_copy", coffr, pco[0:1, 0:8], ins=(pcob,), outs=(P.smb,))
    chi = P.rbf[0:1, 0:8]; clo = P.rbf[0:1, 8:16]
    k.op("vector", "tensor_copy", chi, coffr, ins=(P.smb,), outs=(P.rbfb,))
    k.op("vector", "tensor_tensor", clo, coffr, chi, ALU.subtract,
         ins=(P.smb, P.rbfb), outs=(P.rbfb,))
    P.bcast(chi, clo, P.pb1, P.pb1b, 8)
    cofft, cofftb = k.sb("l0f_cofft", [128, 8], F32)
    k.op("vector", "tensor_copy", cofft[:], P.pb1[:, 0:8], ins=(P.pb1b,), outs=(cofftb,))
    io8_t, io8_tb = P.csb["iota8"]
    for stt in range(32):
        em = P.sm[:, 8:16]
        td = P.sm[:, 16:24]
        for kk2 in range(2):
            ic = meta[:, 64 + 32 * kk2 + stt:64 + 32 * kk2 + stt + 1]
            lrc = meta[:, 32 * kk2 + stt:32 * kk2 + stt + 1]
            wnc = meta[:, 128 + 32 * kk2 + stt:128 + 32 * kk2 + stt + 1]
            k.op("vector", "tensor_scalar", em, io8_t[:], ic, None, ALU.is_equal,
                 ins=(io8_tb, metab), outs=(P.smb,))
            k.op("vector", "tensor_tensor", td, em, cofft[:], ALU.mult,
                 ins=(P.smb, cofftb), outs=(P.smb,))
            gr = P.sm[:, 24:25]
            k.op("vector", "tensor_reduce", gr, td, mybir.AxisListType.X, ALU.add,
                 ins=(P.smb,), outs=(P.smb,))
            k.op("vector", "tensor_tensor", gr, gr, lrc, ALU.add,
                 ins=(P.smb, metab), outs=(P.smb,))
            k.op("vector", "tensor_scalar", gr, gr, float(GCAP), None, ALU.is_lt,
                 ins=(P.smb,), outs=(P.smb,))
            k.op("vector", "tensor_tensor", wnc, wnc, gr, ALU.mult,
                 ins=(P.smb, metab), outs=(metab,))
            k.op("vector", "tensor_scalar", gr, lrc, float(ECAP), None, ALU.is_lt,
                 ins=(metab,), outs=(P.smb,))
            k.op("vector", "tensor_tensor", wnc, wnc, gr, ALU.mult,
                 ins=(P.smb, metab), outs=(metab,))
    h_sb, h_sbb = P.s1b, P.s1bb
    for e in range(8):
        for half in range(2):
            w2h = P.load_w("w_l0f_w2", 0, expert=e, koff=half * 1024, kdim=1024)
            w1h = P.load_w("w_l0f_w1", 8192, expert=e, moff=half * 1024, mdim=1024)
            for sc_ in range(4):
                xe, xeb = P.abig, P.abigb
                for c in range(NCH):
                    k.dma("s", xe[:, c * TT:(c + 1) * TT],
                          ebuf[e * ECAP + sc_ * TT:e * ECAP + (sc_ + 1) * TT,
                               c * 128:(c + 1) * 128],
                          transpose=True, ins=(ebufb,), **gw(xeb, c, NCH))
                for mc in range(NCH):
                    pt_, pb_ = P.bank()
                    k.mm(pb_, pt_[:, 0:TT],
                         [(w1h(kc, mc), xe[:, kc * TT:(kc + 1) * TT],
                           [P.wslabb, xeb]) for kc in range(NCH)])
                    k.op("scalar", "activation", h_sb[:, mc * TT:(mc + 1) * TT],
                         pt_[:, 0:TT], AF.Gelu, ins=(pb_,), **gw(h_sbb, mc, NCH))
                for st2 in range(4):
                    ytmp = P.sm[:, 0:1024]
                    for n2 in range(2):
                        pt_, pb_ = P.bank()
                        k.mm(pb_, pt_[:, 0:TT],
                             [(h_sb[:, hc * TT + st2 * 128: hc * TT + (st2 + 1) * 128],
                               w2h(hc, mo=n2 * 512, width=512),
                               [h_sbb, P.wslabb]) for hc in range(NCH)])
                        k.op("vector", "tensor_copy", ytmp[:, n2 * 512:(n2 + 1) * 512],
                             pt_[:, 0:TT], ins=(pb_,),
                             outs_war=(P.smb,), finals=(P.smb,) if n2 == 1 else ())
                    rows = ybuf[e * ECAP + sc_ * TT + st2 * 128:
                                e * ECAP + sc_ * TT + (st2 + 1) * 128, :]
                    if half == 0:
                        k.dma("s", rows, ytmp, ins=(P.smb,),
                              **gw(ybufb, ((e * 2 + half) * 4 + sc_) * 4 + st2, 256))
                    else:
                        k.dma("g", rows, ytmp, accum_op=ALU.add, ins=(P.smb,),
                              **gw(ybufb, ((e * 2 + half) * 4 + sc_) * 4 + st2, 256))
    P.dbg_dump("ebuf", "ebuf")
    P.dbg_dump("ybuf", "ybuf")
    if DBG:
        mdump = P.nc.dram_tensor("dbg_meta", [128, 192], F32, kind="ExternalOutput")
        P.k.dma("c", mdump.ap(), meta[:], ins=(metab,), outs=(Buf("dm"),))
        sdump = P.nc.dram_tensor("dbg_sidx", [128, 64], I32, kind="ExternalOutput")
        P.k.dma("c", sdump.ap(), sidx[:], ins=(sidxb,), outs=(Buf("ds"),))
        cdump = P.nc.dram_tensor("dbg_call", [8, 8], F32, kind="ExternalOutput")
        P.k.dma("c", cdump.ap(), call[:], ins=(callb,), outs=(Buf("dc"),))
    for tt in range(NTT):
        sl = slice(tt * TT, (tt + 1) * TT)
        s1t, s1tb = P.f1, P.f1b
        for c in range(NCH):
            k.dma("g", s1t[:, c * TT:(c + 1) * TT], sdram[c * 128:(c + 1) * 128, sl],
                  ins=(sdramb,), **gw(s1tb, c, NCH))
        rws = P.use_rows([5], tt)
        P.bcast(rws[0][0], rws[0][1], P.pb1, P.pb1b, TT)
        for c in range(NCH):
            csl = slice(c * TT, (c + 1) * TT)
            k.op("vector", "tensor_tensor", s1t[:, csl], s1t[:, csl],
                 P.pb1[:, 0:TT], ALU.mult, ins=(P.pb1b,), outs_war=(s1tb,))
            k.op("vector", "tensor_scalar", s1t[:, csl], s1t[:, csl],
                 v_n1[0][:, c:c + 1], None, ALU.mult, ins=(v_n1[1],),
                 finals=(s1tb,) if c == NCH - 1 else ())
        for st in range(4):
            stt = tt * 4 + st
            g1 = P.sm[:, 0:1024]
            g2 = P.sm[:, 1024:2048]
            k.idma(g1, None, ybuf[:],
                   IndirectOffsetOnAxis(ap=sidx[:, stt * 2:stt * 2 + 1], axis=0),
                   8 * ECAP - 1, ins=(ybufb, sidxb), outs=(P.smb,))
            k.idma(g2, None, ybuf[:],
                   IndirectOffsetOnAxis(ap=sidx[:, stt * 2 + 1:stt * 2 + 2], axis=0),
                   8 * ECAP - 1, ins=(ybufb, sidxb), outs=(P.smb,))
            k.op("vector", "tensor_scalar", g1, g1,
                 meta[:, 128 + stt:128 + stt + 1], None, ALU.mult,
                 ins=(metab,), outs=(P.smb,))
            k.op("vector", "scalar_tensor_tensor", g1, g2,
                 meta[:, 160 + stt:160 + stt + 1], g1, ALU.mult, ALU.add,
                 ins=(metab,), outs=(P.smb,))
            mb = P.smb_t[:, 0:1024]
            k.op("vector", "tensor_copy", mb, g1, ins=(P.smb,), outs=(P.smb_b,))
            idt2, idt2b = P.csb["ident"]
            for c in range(NCH):
                k.op("tensor", "transpose", P.pt[:, c * 128:(c + 1) * 128],
                     mb[:, c * 128:(c + 1) * 128], idt2[:], ins=(P.smb_b, idt2b),
                     **gw(P.ptb, c, NCH))
            for c in range(NCH):
                k.op("vector", "tensor_tensor",
                     s1t[:, c * TT + st * 128: c * TT + (st + 1) * 128],
                     s1t[:, c * TT + st * 128: c * TT + (st + 1) * 128],
                     P.pt[:, c * 128:(c + 1) * 128], ALU.add,
                     ins=(P.ptb,), outs=(s1tb,))
        sq = P.smb_t
        cs1_, cs1_b = P.csb["onesc1"]
        p2, p2b_ = P.pb2, P.pb2b
        for c in range(NCH):
            csl = slice(c * TT, (c + 1) * TT)
            k.dma("s", sdram[c * 128:(c + 1) * 128, sl], s1t[:, csl],
                  ins=(s1tb,), **gw(sdramb, tt * NCH + c, NTT * NCH))
            k.op("scalar", "activation", sq[:, 0:TT], s1t[:, csl], AF.Square,
                 ins=(s1tb,), outs=(P.smb_b,))
            k.op("tensor", "matmul", p2[0:1, 0:TT], cs1_[:, 0:1], sq[:, 0:TT],
                 start=(c == 0), stop=(c == NCH - 1), ins=(cs1_b, P.smb_b),
                 outs_war=(p2b_,) if c == 0 else (),
                 finals=(p2b_,) if c == NCH - 1 else ())
        r2row = P.sm[0:1, 0:TT]
        k.op("scalar", "activation", r2row, p2[0:1, 0:TT], AF.Ln,
             bias=np.float32(1e-6), scale=1.0 / D, ins=(p2b_,), outs=(P.smb,))
        k.op("scalar", "activation", r2row, r2row, AF.Exp, scale=-0.5,
             ins=(P.smb,), outs=(P.smb,))
        k.dma("s", rd[6:7, sl], r2row, ins=(P.smb,), outs=(rdb,))
    P.dbg_dump("s3", "sdram")
    src, srcb = P.d("x_in" if P.blocks.index("l0f") == 0 else "xdram")
    HT = 256
    for ht in range(RPC // HT):
        sl = slice(ht * HT, (ht + 1) * HT)
        s3t, s3tb = P.f1, P.f1b
        for c in range(NCH):
            k.dma("g", s3t[:, c * HT:(c + 1) * HT], sdram[c * 128:(c + 1) * 128, sl],
                  ins=(sdramb,), **gw(s3tb, c, 2 * NCH))
            k.dma("g", s3t[:, 2048 + c * HT:2048 + (c + 1) * HT],
                  src[c * 128:(c + 1) * 128, sl], ins=(srcb,),
                  **gw(s3tb, NCH + c, 2 * NCH))
        rd2, rd2b = P.d("rowsd")
        fsl = P.rsc[0:1, 0:HT]
        k.dma("s", fsl, rd2[6:7, sl], ins=(rd2b,), outs=(P.rscb,))
        rhi = P.rbf[0:1, 0:HT]
        rlo = P.rbf[0:1, HT:2 * HT]
        k.op("vector", "tensor_copy", rhi, fsl, ins=(P.rscb,), outs=(P.rbfb,))
        k.op("vector", "tensor_tensor", rlo, fsl, rhi, ALU.subtract,
             ins=(P.rscb, P.rbfb), outs=(P.rbfb,))
        P.bcast(rhi, rlo, P.pb1, P.pb1b, HT)
        res = P.sm[:, 0:HT]
        for c in range(NCH):
            k.op("vector", "tensor_tensor", res, s3t[:, c * HT:(c + 1) * HT],
                 P.pb1[:, 0:HT], ALU.mult, ins=(s3tb, P.pb1b), outs=(P.smb,))
            k.op("vector", "tensor_scalar", res, res, v_n2[0][:, c:c + 1], None,
                 ALU.mult, ins=(v_n2[1],), outs=(P.smb,))
            k.op("vector", "tensor_tensor", res, res,
                 s3t[:, 2048 + c * HT:2048 + (c + 1) * HT], ALU.add,
                 ins=(s3tb,), outs=(P.smb,))
            k.dma("s", dst[c * 128:(c + 1) * 128, sl], res, ins=(P.smb,),
                  **gw(dstb, ht * NCH + c, (RPC // HT) * NCH))




def build_l1a(P):
    k = P.k
    i = P.blocks.index("l1a")
    src, srcb = P.d("x_in" if i == 0 else "xdram")
    dst, dstb = P.d("x_out" if i == len(P.blocks) - 1 else "xdram")

    for w in ("w_l1a_qk", "w_l1a_v", "w_l1a_out"):
        P.precast(w)
    v_qkb = P.load_vec("v_l1a_qkb")
    v_vb = P.load_vec("v_l1a_vb2")
    v_gb = P.load_vec("v_l1a_gb")
    v_ob = P.load_vec("v_l1a_outb")

    P.stats_pass("x_in" if i == 0 else "xdram", "ln", [0, 1])
    vt, vtb = P.vt, P.vtb
    idt, idtb = P.csb["ident"]
    oh0, oh0b = P.csb["onesc_h0"]
    oh1, oh1b = P.csb["onesc_h1"]
    masks = {0: P.csb["m_diag"], 4: P.csb["m_win"], -1: P.csb["m_full"]}
    w_out = P.load_w("w_l1a_out", 2048)

    def hilo2(rA, rB, psum_t, psum_b, ins):
        hA = P.rbf[0:1, 0:TT]; lA = P.rbf[0:1, TT:2 * TT]
        hB = P.rbf[0:1, 2 * TT:3 * TT]; lB = P.rbf[0:1, 3 * TT:4 * TT]
        k.op("vector", "tensor_copy", hA, rA, ins=ins, outs=(P.rbfb,))
        k.op("vector", "tensor_tensor", lA, rA, hA, ALU.subtract,
             ins=tuple(ins) + (P.rbfb,), outs=(P.rbfb,))
        k.op("vector", "tensor_copy", hB, rB, ins=ins, outs=(P.rbfb,))
        k.op("vector", "tensor_tensor", lB, rB, hB, ALU.subtract,
             ins=tuple(ins) + (P.rbfb,), outs=(P.rbfb,))
        P.bcast(hA, lA, psum_t, psum_b, TT, pair2=(hB, lB))

    for s in range(2):
        x_in, x_inb = P.abig, P.abigb
        for lt in range(TPS):
            tt = s * TPS + lt
            sl = slice(tt * TT, (tt + 1) * TT)
            rws = P.use_rows([0, 1], tt)
            xbt, xbtb = P.s1b, P.s1bb
            for c in range(NCH):
                k.dma("g", xbt[:, c * TT:(c + 1) * TT],
                      src[c * 128:(c + 1) * 128, sl], ins=(srcb,), **gw(xbtb, c, NCH))
            P.bcast(rws[0][0], rws[0][1], P.pb1, P.pb1b, TT)
            P.bcast(rws[1][0], rws[1][1], P.pb2, P.pb2b, TT)
            for c in range(NCH):
                csl = slice(c * TT, (c + 1) * TT)
                osl = slice(c * SEQ + lt * TT, c * SEQ + (lt + 1) * TT)
                k.op("vector", "tensor_tensor", x_in[:, osl], xbt[:, csl],
                     P.pb1[:, 0:TT], ALU.mult, ins=(xbtb, P.pb1b), outs_war=(x_inb,))
                k.op("vector", "tensor_tensor", x_in[:, osl], x_in[:, osl],
                     P.pb2[:, 0:TT], ALU.add, ins=(P.pb2b,),
                     finals=(x_inb,) if (lt == TPS - 1 and c == NCH - 1) else ())
        asb, asbb = P.bbig, P.bbigb
        qkt, qktb = P.f1, P.f1b
        for h in range(16):
            w_qk = P.load_w("w_l1a_qk", 0, moff=h * 128, mdim=128)
            w_v = P.load_w("w_l1a_v", 1024, moff=h * 64, mdim=64)
            for lt in range(TPS):
                pt_, pb_ = P.bank()
                k.mm(pb_, pt_[:, 0:TT],
                     [(w_qk(kc), x_in[:, kc * SEQ + lt * TT: kc * SEQ + (lt + 1) * TT],
                       [P.wslabb, x_inb]) for kc in range(NCH)])
                k.op("vector", "tensor_scalar", qkt[:, lt * TT:(lt + 1) * TT],
                     pt_[:, 0:TT], v_qkb[0][:, h:h + 1], None, ALU.add,
                     ins=(pb_, v_qkb[1]), **gw(qktb, lt, TPS))
                pt_, pb_ = P.bank()
                k.mm(pb_, pt_[0:64, 0:TT],
                     [(w_v(kc, width=64), x_in[:, kc * SEQ + lt * TT: kc * SEQ + (lt + 1) * TT],
                       [P.wslabb, x_inb]) for kc in range(NCH)])
                k.op("vector", "tensor_scalar", P.s2b[0:64, lt * TT:(lt + 1) * TT],
                     pt_[0:64, 0:TT], v_vb[0][0:64, h:h + 1], None, ALU.add,
                     ins=(pb_, v_vb[1]), **gw(P.s2bb, lt, TPS))
            k.op("vector", "memset", vt[:, 0:16 * 65], 1.0, outs=(vtb,))
            for g2 in range(2):
                for j in range(8):
                    kt = g2 * 8 + j
                    k.op("tensor", "transpose", P.pt[:, j * 128:j * 128 + 64],
                         P.s2b[0:64, kt * 128:(kt + 1) * 128], idt[0:64, 0:64],
                         ins=(P.s2bb, idtb), **gw(P.ptb, j, 8))
                k.op("vector", "tensor_copy",
                     vt[:, (g2 * 8) * 65:(g2 * 8) * 65 + 8 * 65].rearrange(
                         "p (t c) -> p t c", c=65)[:, :, 0:64],
                     P.pt[:, 0:1024].rearrange("p (t c) -> p t c", c=128)[:, :, 0:64],
                     ins=(P.ptb,), outs_war=(vtb,), finals=(vtb,) if g2 == 1 else ())
            for ch in range(TPS):
                csl = slice(ch * TT, (ch + 1) * TT)
                qkb_t = P.smb_t[:, 0:TT]
                sqb_t = P.smb_t[:, TT:2 * TT]
                k.op("scalar", "activation", qkb_t, qkt[:, csl], AF.Copy,
                     ins=(qktb,), outs=(P.smb_b,))
                k.op("scalar", "activation", sqb_t, qkt[:, csl], AF.Square,
                     ins=(qktb,), outs=(P.smb_b,))
                pa, pab = P.bank(); pb, pbb = P.bank()
                pc, pcb = P.bank(); pd, pdb = P.bank()
                k.mm(pab, pa[0:1, 0:TT], [(oh0[:], qkb_t, [oh0b, P.smb_b])])
                k.mm(pbb, pb[0:1, 0:TT], [(oh1[:], qkb_t, [oh1b, P.smb_b])])
                k.mm(pcb, pc[0:1, 0:TT], [(oh0[:], sqb_t, [oh0b, P.smb_b])])
                k.mm(pdb, pd[0:1, 0:TT], [(oh1[:], sqb_t, [oh1b, P.smb_b])])
                R, Rb = P.rsc, P.rscb
                muq = R[0:1, 0:TT]; muk = R[0:1, TT:2 * TT]
                Aq = R[0:1, 2 * TT:3 * TT]; Ak = R[0:1, 3 * TT:4 * TT]
                k.op("vector", "tensor_scalar", muq, pa[0:1, 0:TT], 1.0 / 64, None,
                     ALU.mult, ins=(pab,), outs=(Rb,))
                k.op("vector", "tensor_tensor", Aq, muq, muq, ALU.mult, ins=(Rb,), outs=(Rb,))
                k.op("vector", "scalar_tensor_tensor", Aq, pc[0:1, 0:TT], 1.0 / 64,
                     Aq, ALU.mult, ALU.subtract, ins=(pcb, Rb), outs=(Rb,))
                k.op("scalar", "activation", Aq, Aq, AF.Ln, bias=np.float32(1e-5),
                     ins=(Rb,), outs=(Rb,))
                k.op("scalar", "activation", Aq, Aq, AF.Exp, scale=-0.5,
                     ins=(Rb,), outs=(Rb,))
                k.op("vector", "tensor_scalar", muk, pb[0:1, 0:TT], 1.0 / 64, None,
                     ALU.mult, ins=(pbb,), outs=(Rb,))
                k.op("vector", "tensor_tensor", Ak, muk, muk, ALU.mult, ins=(Rb,), outs=(Rb,))
                k.op("vector", "scalar_tensor_tensor", Ak, pd[0:1, 0:TT], 1.0 / 64,
                     Ak, ALU.mult, ALU.subtract, ins=(pdb, Rb), outs=(Rb,))
                k.op("scalar", "activation", Ak, Ak, AF.Ln, bias=np.float32(1e-5),
                     ins=(Rb,), outs=(Rb,))
                k.op("scalar", "activation", Ak, Ak, AF.Exp, scale=-0.5,
                     ins=(Rb,), outs=(Rb,))
                hilo2(Aq, Ak, P.pb1, P.pb1b, (Rb,))
                k.op("vector", "tensor_tensor", muq, muq, Aq, ALU.mult, ins=(Rb,), outs=(Rb,))
                k.op("vector", "tensor_scalar", muq, muq, -1.0, None, ALU.mult,
                     ins=(Rb,), outs=(Rb,))
                k.op("vector", "tensor_tensor", muk, muk, Ak, ALU.mult, ins=(Rb,), outs=(Rb,))
                k.op("vector", "tensor_scalar", muk, muk, -1.0, None, ALU.mult,
                     ins=(Rb,), outs=(Rb,))
                hilo2(muq, muk, P.pb2, P.pb2b, (Rb,))
                t1 = P.sm[:, 0:TT]
                k.op("vector", "tensor_tensor", t1, qkt[:, csl], P.pb1[:, 0:TT],
                     ALU.mult, ins=(qktb, P.pb1b), outs=(P.smb,))
                k.op("vector", "tensor_tensor", t1, t1, P.pb2[:, 0:TT],
                     ALU.add, ins=(P.pb2b,), outs=(P.smb,))
                qkh = P.smb_t[:, 0:TT]
                k.op("vector", "tensor_scalar", qkh, t1, v_gb[0][:, 0:1],
                     v_gb[0][:, 1:2], ALU.mult, ALU.add, ins=(P.smb, v_gb[1]),
                     outs=(P.smb_b,))
                k.op("vector", "tensor_copy", P.s2b[0:64, csl], qkh[0:64, :],
                     ins=(P.smb_b,), outs=(P.s2bb,))
                prl, prlb = P.bank()
                k.op("tensor", "matmul", prl[0:64, 0:TT], idt[64:128, 64:128],
                     qkh[64:128, :], start=True, stop=True,
                     ins=(idtb, P.smb_b), outs=(prlb,))
                k.op("vector", "tensor_copy", P.s2b[0:64, 2048 + ch * TT:2048 + (ch + 1) * TT],
                     prl[0:64, 0:TT], ins=(prlb,), outs=(P.s2bb,))
            for qc in range(TPS):
                qsl = slice(qc * TT, (qc + 1) * TT)
                avb = P.pb2b
                ktlo = max(0, 4 * qc - 4)
                kts = list(range(ktlo, 4 * qc + 4))
                sbanks = {}

                def score(ki, kt):
                    pt_, pb_ = P.bank()
                    k.mm(pb_, pt_[:, 0:TT],
                         [(P.s2b[0:64, 2048 + kt * 128:2048 + (kt + 1) * 128],
                           P.s2b[0:64, qsl], [P.s2bb])])
                    for jt in range(4):
                        diff = (4 * qc + jt) - kt
                        if diff < 0 or diff == 0 or diff == 4:
                            mk_t, mk_b = masks[0 if diff == 0 else (4 if diff == 4 else -1)]
                            k.op("vector", "tensor_tensor",
                                 pt_[:, jt * 128:(jt + 1) * 128],
                                 pt_[:, jt * 128:(jt + 1) * 128], mk_t[:],
                                 ALU.add, ins=(pb_, mk_b), outs=(pb_,))
                    sbanks[ki] = (pt_, pb_)
                score(0, kts[0])
                for ki, kt in enumerate(kts):
                    if ki + 1 < len(kts):
                        score(ki + 1, kts[ki + 1])
                    pt_, pb_ = sbanks.pop(ki)
                    atile = P.smb_t[:, (ki % 2) * TT:(ki % 2 + 1) * TT]
                    k.op("scalar", "activation", atile, pt_[:, 0:TT], AF.Exp,
                         scale=0.125, ins=(pb_,), outs=(P.smb_b,))
                    k.op("tensor", "matmul", P.pb2[0:65, 0:TT],
                         vt[:, kt * 65:(kt + 1) * 65], atile,
                         start=(ki == 0), stop=(ki == len(kts) - 1),
                         ins=(vtb, P.smb_b),
                         outs_war=(avb,) if ki == 0 else (),
                         finals=(avb,) if ki == len(kts) - 1 else ())
                avs = P.sm[:, 0:TT]
                k.op("vector", "tensor_copy", avs[0:65, :], P.pb2[0:65, 0:TT],
                     ins=(avb,), outs=(P.smb,))
                rrow = P.sm[0:1, TT:2 * TT]
                k.op("vector", "reciprocal", rrow, avs[64:65, :], ins=(P.smb,), outs=(P.smb,))
                hi = P.rbf[0:1, 0:TT]; lo = P.rbf[0:1, TT:2 * TT]
                k.op("vector", "tensor_copy", hi, rrow, ins=(P.smb,), outs=(P.rbfb,))
                k.op("vector", "tensor_tensor", lo, rrow, hi, ALU.subtract,
                     ins=(P.smb, P.rbfb), outs=(P.rbfb,))
                P.bcast(hi, lo, P.pb1, P.pb1b, TT)
                dst_sl = asb[(h % 2) * 64:(h % 2) * 64 + 64,
                             (h // 2) * SEQ + qc * TT:(h // 2) * SEQ + (qc + 1) * TT]
                fin = dict(outs_war=(asbb,),
                           finals=(asbb,) if (h == 15 and qc == TPS - 1) else ())
                if h % 2 == 0:
                    k.op("vector", "tensor_tensor", dst_sl, avs[0:64, :],
                         P.pb1[0:64, 0:TT], ALU.mult, ins=(P.smb, P.pb1b), **fin)
                else:
                    tmp = P.smb_t[:, 1024:1024 + TT]
                    k.op("vector", "tensor_tensor", tmp[0:64, :], avs[0:64, :],
                         P.pb1[0:64, 0:TT], ALU.mult, ins=(P.smb, P.pb1b),
                         outs=(P.smb_b,))
                    prl, prlb = P.bank()
                    k.op("tensor", "matmul", prl[64:128, 0:TT],
                         idt[0:64, 0:64], tmp[0:64, :], start=True, stop=True,
                         ins=(idtb, P.smb_b), outs=(prlb,))
                    k.op("vector", "tensor_copy", dst_sl, prl[64:128, 0:TT],
                         ins=(prlb,), **fin)
        xf, xfb = P.f1, P.f1b
        for lt in range(TPS):
            tt = s * TPS + lt
            sl = slice(tt * TT, (tt + 1) * TT)
            for c in range(NCH):
                k.dma("g", xf[:, c * TT:(c + 1) * TT], src[c * 128:(c + 1) * 128, sl],
                      ins=(srcb,), **gw(xfb, c, NCH))
            for mc in range(NCH):
                pt_, pb_ = P.bank()
                k.mm(pb_, pt_[:, 0:TT],
                     [(w_out(kc, mc), asb[:, kc * SEQ + lt * TT: kc * SEQ + (lt + 1) * TT],
                       [P.wslabb, asbb]) for kc in range(NCH)])
                res = P.sm[:, 0:TT]
                k.op("vector", "tensor_scalar", res, pt_[:, 0:TT],
                     v_ob[0][:, mc:mc + 1], None, ALU.add, ins=(pb_, v_ob[1]),
                     outs=(P.smb,))
                k.op("vector", "tensor_tensor", res, res, xf[:, mc * TT:(mc + 1) * TT],
                     ALU.add, ins=(xfb,), outs=(P.smb,))
                k.dma("s", dst[mc * 128:(mc + 1) * 128, sl], res, ins=(P.smb,),
                      **gw(dstb, (s * TPS + lt) * NCH + mc, NTT * NCH))




def build_l1f(P):
    k = P.k
    i = P.blocks.index("l1f")
    src, srcb = P.d("x_in" if i == 0 else "xdram")
    dst, dstb = P.d("x_out" if i == len(P.blocks) - 1 else "xdram")
    fout, foutb = P.d("fout")
    hg, hgb = P.d("hg")
    cmbd, cmbdb = P.d("cmbd")
    rd, rdb = P.d("rowsd")

    for w in ("w_l1f_router", "w_l1f_sg", "w_l1f_su", "w_l1f_sd"):
        P.precast(w)
    P.precast("w_l1f_g", 4)
    P.precast("w_l1f_u", 4)
    P.precast("w_l1f_d", 4)
    corr_ap, corrb_ = P.d("v_l1f_corr")

    P.stats_pass("x_in" if i == 0 else "xdram", "ln", [0, 1])
    io16, io16b = P.csb["iota16"]
    io16B, io16Bb = P.csb["iota16B"]
    g16, g16b = P.csb["grp16"]
    idf, idfb = P.csb["identf"]

    corrt, corrtb = k.sb("l1f_corrt", [128, 16], F32)
    k.dma("s", P.rsc[0:1, 0:16], corr_ap[:], ins=(corrb_,), outs=(P.rscb,))
    hi = P.rbf[0:1, 0:16]; lo = P.rbf[0:1, 16:32]
    k.op("vector", "tensor_copy", hi, P.rsc[0:1, 0:16], ins=(P.rscb,), outs=(P.rbfb,))
    k.op("vector", "tensor_tensor", lo, P.rsc[0:1, 0:16], hi, ALU.subtract,
         ins=(P.rscb, P.rbfb), outs=(P.rbfb,))
    P.bcast(hi, lo, P.pb1, P.pb1b, 16)
    k.op("vector", "tensor_copy", corrt[:], P.pb1[:, 0:16], ins=(P.pb1b,), outs=(corrtb,))

    def xin_tile(tt, dstt, dsttb):
        sl = slice(tt * TT, (tt + 1) * TT)
        rws = P.use_rows([0, 1], tt)
        xbt, xbtb = P.smb_t, P.smb_b
        P.bcast(rws[0][0], rws[0][1], P.pb1, P.pb1b, TT)
        P.bcast(rws[1][0], rws[1][1], P.pb2, P.pb2b, TT)
        for c in range(NCH):
            k.dma("g", xbt[:, 0:TT] if False else xbt[:, (c % 2) * TT:(c % 2) * TT + TT],
                  src[c * 128:(c + 1) * 128, sl], ins=(srcb,), outs=(xbtb,))
            csl = slice(c * TT, (c + 1) * TT)
            k.op("vector", "tensor_tensor", dstt[:, csl],
                 xbt[:, (c % 2) * TT:(c % 2) * TT + TT],
                 P.pb1[:, 0:TT], ALU.mult, ins=(xbtb, P.pb1b), outs_war=(dsttb,))
            k.op("vector", "tensor_tensor", dstt[:, csl], dstt[:, csl],
                 P.pb2[:, 0:TT], ALU.add, ins=(P.pb2b,),
                 finals=(dsttb,) if c == NCH - 1 else ())

    w_r = P.load_w("w_l1f_router", 16256)
    xint, xintb = P.s1b, P.s1bb

    for tt in range(NTT):
        sl = slice(tt * TT, (tt + 1) * TT)
        xin_tile(tt, xint, xintb)
        pg, pgb = P.bank()
        k.mm(pgb, pg[0:16, 0:TT],
             [(w_r(kc, mo=0, width=16), xint[:, kc * TT:(kc + 1) * TT],
               [P.wslabb, xintb]) for kc in range(NCH)])
        scf = P.sm[0:16, 1024:1536]
        k.op("scalar", "activation", scf, pg[0:16, 0:TT], AF.Sigmoid,
             ins=(pgb,), outs=(P.smb,))
        for st in range(4):
            pts, ptsb = P.bank()
            k.op("tensor", "transpose", pts[:, 0:16],
                 scf[:, st * 128:(st + 1) * 128], idf[0:16, 0:16],
                 ins=(P.smb, idfb), outs=(ptsb,))
            sc_tm = P.sm[:, 0:16]
            sb_ = P.sm[:, 16:32]
            msk = P.sm[:, 32:48]
            gst = P.sm[:, 48:52]
            t1c = P.sm[:, 52:53]; t2c = P.sm[:, 53:54]
            em = P.sm[:, 64:80]
            em2 = P.sm[:, 80:96]
            cmb = P.sm[:, 96:112]
            wk = P.sm[:, 112:116]
            k.op("vector", "tensor_copy", sc_tm, pts[:, 0:16], ins=(ptsb,), outs=(P.smb,))
            k.op("vector", "tensor_tensor", sb_, sc_tm, corrt[:], ALU.add,
                 ins=(corrtb,), outs=(P.smb,))
            for gi in range(4):
                gsl = slice(16 + gi * 4, 16 + (gi + 1) * 4)
                gs_sl = P.sm[:, gsl] if False else sb_[:, gi * 4:(gi + 1) * 4]
                k.op("vector", "tensor_reduce", t1c, gs_sl, mybir.AxisListType.X,
                     ALU.max, ins=(P.smb,), outs=(P.smb,))
                k.op("vector", "tensor_scalar", em[:, 0:4], gs_sl, t1c, None,
                     ALU.is_equal, ins=(P.smb,), outs=(P.smb,))
                k.op("vector", "scalar_tensor_tensor", em2[:, 0:4], em[:, 0:4],
                     -65536.0, io16B[:, gi * 4:(gi + 1) * 4], ALU.mult, ALU.add,
                     ins=(P.smb, io16Bb), outs=(P.smb,))
                k.op("vector", "tensor_reduce", t2c, em2[:, 0:4], mybir.AxisListType.X,
                     ALU.min, ins=(P.smb,), outs=(P.smb,))
                k.op("vector", "tensor_scalar", em[:, 0:4], io16[:, gi * 4:(gi + 1) * 4],
                     t2c, None, ALU.is_equal, ins=(P.smb, io16b), outs=(P.smb,))
                k.op("vector", "scalar_tensor_tensor", em2[:, 0:4], em[:, 0:4],
                     -BIG, gs_sl, ALU.mult, ALU.add, ins=(P.smb,), outs=(P.smb,))
                k.op("vector", "tensor_reduce", t2c, em2[:, 0:4], mybir.AxisListType.X,
                     ALU.max, ins=(P.smb,), outs=(P.smb,))
                k.op("vector", "tensor_tensor", gst[:, gi:gi + 1], t1c, t2c, ALU.add,
                     ins=(P.smb,), outs=(P.smb,))
            k.op("vector", "tensor_reduce", t1c, gst, mybir.AxisListType.X, ALU.max,
                 ins=(P.smb,), outs=(P.smb,))
            k.op("vector", "tensor_scalar", em[:, 0:4], gst, t1c, None, ALU.is_equal,
                 ins=(P.smb,), outs=(P.smb,))
            k.op("vector", "scalar_tensor_tensor", em2[:, 0:4], em[:, 0:4], -65536.0,
                 io16B[:, 0:4], ALU.mult, ALU.add, ins=(P.smb, io16Bb), outs=(P.smb,))
            k.op("vector", "tensor_reduce", t1c, em2[:, 0:4], mybir.AxisListType.X,
                 ALU.min, ins=(P.smb,), outs=(P.smb,))
            k.op("vector", "tensor_scalar", em[:, 0:4], io16[:, 0:4], t1c, None,
                 ALU.is_equal, ins=(P.smb, io16b), outs=(P.smb,))
            k.op("vector", "scalar_tensor_tensor", em2[:, 0:4], em[:, 0:4], -BIG,
                 gst, ALU.mult, ALU.add, ins=(P.smb,), outs=(P.smb,))
            k.op("vector", "tensor_reduce", t2c, em2[:, 0:4], mybir.AxisListType.X,
                 ALU.max, ins=(P.smb,), outs=(P.smb,))
            k.op("vector", "tensor_scalar", em2[:, 0:4], em2[:, 0:4], t2c, None,
                 ALU.is_equal, ins=(P.smb,), outs=(P.smb,))
            k.op("vector", "scalar_tensor_tensor", em2[:, 0:4], em2[:, 0:4], -65536.0,
                 io16B[:, 0:4], ALU.mult, ALU.add, ins=(P.smb, io16Bb), outs=(P.smb,))
            k.op("vector", "tensor_reduce", t2c, em2[:, 0:4], mybir.AxisListType.X,
                 ALU.min, ins=(P.smb,), outs=(P.smb,))
            k.op("vector", "tensor_scalar", msk, g16[:], t1c, None, ALU.is_equal,
                 ins=(P.smb, g16b), outs=(P.smb,))
            k.op("vector", "tensor_scalar", em, g16[:], t2c, None, ALU.is_equal,
                 ins=(P.smb, g16b), outs=(P.smb,))
            k.op("vector", "tensor_tensor", msk, msk, em, ALU.add, ins=(P.smb,), outs=(P.smb,))
            if DBG and tt == 1 and st == 3:
                dbr_t = P.nc.dram_tensor("dbg_rt", [128, 48], F32, kind="ExternalOutput")
                drt = P.k.sb("dbg_rtt", [128, 48], F32)
                k.op("vector", "tensor_copy", drt[0][:, 0:4], gst, ins=(P.smb,), outs=(drt[1],))
                k.op("vector", "tensor_copy", drt[0][:, 4:5], t1c, ins=(P.smb,), outs=(drt[1],))
                k.op("vector", "tensor_copy", drt[0][:, 5:6], t2c, ins=(P.smb,), outs=(drt[1],))
                k.op("vector", "tensor_copy", drt[0][:, 6:22], msk, ins=(P.smb,), outs=(drt[1],))
                k.op("vector", "tensor_copy", drt[0][:, 22:38], sb_, ins=(P.smb,), outs=(drt[1],))
                k.dma("c", dbr_t.ap(), drt[0][:], ins=(drt[1],), outs=(Buf("drt"),))
            k.op("vector", "tensor_scalar", em2, sb_, 1000.0, None, ALU.subtract,
                 ins=(P.smb,), outs=(P.smb,))
            k.op("vector", "scalar_tensor_tensor", msk, msk, 1000.0, em2,
                 ALU.mult, ALU.add, ins=(P.smb,), outs=(P.smb,))
            k.op("vector", "memset", cmb, 0.0, outs=(P.smb,))
            k.op("vector", "memset", wk, 0.0, outs=(P.smb,))
            for kk2 in range(4):
                k.op("vector", "tensor_reduce", t1c, msk, mybir.AxisListType.X,
                     ALU.max, ins=(P.smb,), outs=(P.smb,))
                k.op("vector", "tensor_scalar", em, msk, t1c, None, ALU.is_equal,
                     ins=(P.smb,), outs=(P.smb,))
                k.op("vector", "scalar_tensor_tensor", em2, em, -65536.0, io16B[:],
                     ALU.mult, ALU.add, ins=(P.smb, io16Bb), outs=(P.smb,))
                k.op("vector", "tensor_reduce", t1c, em2, mybir.AxisListType.X,
                     ALU.min, ins=(P.smb,), outs=(P.smb,))
                k.op("vector", "tensor_scalar", em, io16[:], t1c, None, ALU.is_equal,
                     ins=(P.smb, io16b), outs=(P.smb,))
                k.op("vector", "scalar_tensor_tensor", msk, em, -BIG, msk,
                     ALU.mult, ALU.add, ins=(P.smb,), outs=(P.smb,))
                k.op("vector", "tensor_tensor", em2, em, sc_tm, ALU.mult,
                     ins=(P.smb,), outs=(P.smb,))
                k.op("vector", "tensor_reduce", wk[:, kk2:kk2 + 1], em2,
                     mybir.AxisListType.X, ALU.add, ins=(P.smb,), outs=(P.smb,))
                k.op("vector", "scalar_tensor_tensor", cmb, em, 1.0, cmb,
                     ALU.mult, ALU.add, ins=(P.smb,), outs=(P.smb,))
            k.op("vector", "tensor_reduce", t1c, wk, mybir.AxisListType.X, ALU.add,
                 ins=(P.smb,), outs=(P.smb,))
            k.op("vector", "tensor_scalar", t1c, t1c, 1e-20, None, ALU.add,
                 ins=(P.smb,), outs=(P.smb,))
            k.op("vector", "reciprocal", t1c, t1c, ins=(P.smb,), outs=(P.smb,))
            k.op("vector", "tensor_tensor", cmb, cmb, sc_tm, ALU.mult,
                 ins=(P.smb,), outs=(P.smb,))
            k.op("vector", "tensor_scalar", cmb, cmb, t1c, None, ALU.mult,
                 ins=(P.smb,), outs=(P.smb,))
            pcm, pcmb = P.bank()
            k.op("tensor", "transpose", pcm[0:16, 0:128], cmb, idf[:],
                 ins=(P.smb, idfb), outs=(pcmb,))
            cm_s = P.sm[0:16, 1536:1664]
            k.op("vector", "tensor_copy", cm_s, pcm[0:16, 0:128], ins=(pcmb,), outs=(P.smb,))
            k.dma("s", cmbd[0:16, tt * TT + st * 128: tt * TT + (st + 1) * 128],
                  cm_s, ins=(P.smb,), outs=(cmbdb,))

    P.dbg_dump("cmbd", "cmbd")
    for phase, wname in (("sg", "w_l1f_sg"), ("su", "w_l1f_su")):
        w_s = P.load_w(wname, 0)
        for tt in range(NTT):
            sl = slice(tt * TT, (tt + 1) * TT)
            xin_tile(tt, xint, xintb)
            for mc in range(16):
                pt_, pb_ = P.bank()
                k.mm(pb_, pt_[:, 0:TT],
                     [(w_s(kc, mc), xint[:, kc * TT:(kc + 1) * TT],
                       [P.wslabb, xintb]) for kc in range(NCH)])
                ht = P.smb_t[:, 1024:1024 + TT]
                if phase == "sg":
                    k.op("scalar", "activation", ht, pt_[:, 0:TT], AF.Silu,
                         ins=(pb_,), outs=(P.smb_b,))
                else:
                    gt2 = P.smb_t[:, 1536:1536 + TT]
                    k.dma("g", gt2, hg[mc * 128:(mc + 1) * 128, sl],
                          ins=(hgb,), outs=(P.smb_b,))
                    k.op("vector", "tensor_tensor", ht, gt2, pt_[:, 0:TT],
                         ALU.mult, ins=(P.smb_b, pb_), outs=(P.smb_b,))
                k.dma("s", hg[mc * 128:(mc + 1) * 128, sl], ht, ins=(P.smb_b,),
                      outs=(hgb,))
    w_sd = P.load_w("w_l1f_sd", 0)
    for tt in range(NTT):
        sl = slice(tt * TT, (tt + 1) * TT)
        hb1, hb1b = P.s1b, P.s1bb
        hb2, hb2b = P.s2b, P.s2bb
        for c in range(16):
            t_, b_ = (hb1, hb1b) if c < 8 else (hb2, hb2b)
            k.dma("g", t_[:, (c % 8) * TT:(c % 8 + 1) * TT],
                  hg[c * 128:(c + 1) * 128, sl], ins=(hgb,), **gw(b_, c % 8, 8))
        for mc in range(NCH):
            pt_, pb_ = P.bank()
            k.mm(pb_, pt_[:, 0:TT],
                 [((w_sd(kc, mc)), (hb1 if kc < 8 else hb2)[:, (kc % 8) * TT:(kc % 8 + 1) * TT],
                   [P.wslabb, hb1b if kc < 8 else hb2b]) for kc in range(16)])
            fo = P.sm[:, 0:TT]
            k.op("vector", "tensor_copy", fo, pt_[:, 0:TT], ins=(pb_,), outs=(P.smb,))
            k.dma("s", fout[mc * 128:(mc + 1) * 128, sl], fo, ins=(P.smb,),
                  **gw(foutb, tt * NCH + mc, NTT * NCH))

    racc, raccb = P.f1, P.f1b
    h_all, h_allb = P.bbig, P.bbigb
    for tt in range(NTT):
        sl = slice(tt * TT, (tt + 1) * TT)
        xin_tile(tt, xint, xintb)
        for e in range(16):
            w_ge = P.load_w("w_l1f_g", (e % 2) * 4096, expert=e)
            w_ue = P.load_w("w_l1f_u", (e % 2) * 4096 + 2048, expert=e)
            fsl = P.rsc[0:1, 0:TT]
            k.dma("s", fsl, cmbd[e:e + 1, sl], ins=(cmbdb,), outs=(P.rscb,))
            chi = P.rbf[0:1, 0:TT]; clo = P.rbf[0:1, TT:2 * TT]
            k.op("vector", "tensor_copy", chi, fsl, ins=(P.rscb,), outs=(P.rbfb,))
            k.op("vector", "tensor_tensor", clo, fsl, chi, ALU.subtract,
                 ins=(P.rscb, P.rbfb), outs=(P.rbfb,))
            P.bcast(chi, clo, P.pb1, P.pb1b, TT)
            for m2 in range(2):
                pt_, pb_ = P.bank()
                k.mm(pb_, pt_[:, 0:TT],
                     [(w_ge(kc, m2), xint[:, kc * TT:(kc + 1) * TT],
                       [P.wslabb, xintb]) for kc in range(NCH)])
                gt2 = P.smb_t[:, 0:TT]
                k.op("scalar", "activation", gt2, pt_[:, 0:TT], AF.Silu,
                     ins=(pb_,), outs=(P.smb_b,))
                pt2_, pb2_ = P.bank()
                k.mm(pb2_, pt2_[:, 0:TT],
                     [(w_ue(kc, m2), xint[:, kc * TT:(kc + 1) * TT],
                       [P.wslabb, xintb]) for kc in range(NCH)])
                hcol = (e * 2 + m2) * TT
                k.op("vector", "tensor_tensor", h_all[:, hcol:hcol + TT], gt2,
                     pt2_[:, 0:TT], ALU.mult, ins=(P.smb_b, pb2_),
                     outs_war=(h_allb,))
                k.op("vector", "tensor_tensor", h_all[:, hcol:hcol + TT],
                     h_all[:, hcol:hcol + TT], P.pb1[:, 0:TT], ALU.mult,
                     ins=(P.pb1b,),
                     finals=(h_allb,) if (e == 15 and m2 == 1) else ())
        for eg in range(2):
            wd, wdb = P.abig, P.abigb
            ap_, dbuf_ = P.d("w_l1f_d_bf")
            for ei in range(8):
                k.dma("s", wd[:, ei * 2048:(ei + 1) * 2048].rearrange(
                          "p (kc m) -> p kc m", kc=2),
                      ap_[eg * 8 + ei].rearrange("(kc p) m -> p kc m", p=128),
                      ins=(dbuf_,), **gw(wdb, ei, 8))
            for mc in range(NCH):
                pt_, pb_ = P.bank()
                parts = []
                for ei in range(8):
                    for kc in range(2):
                        parts.append((wd[:, (ei * 2 + kc) * 1024 + mc * 128:
                                         (ei * 2 + kc) * 1024 + mc * 128 + 128],
                                      h_all[:, ((eg * 8 + ei) * 2 + kc) * TT:
                                            ((eg * 8 + ei) * 2 + kc + 1) * TT],
                                      [wdb, h_allb]))
                k.mm(pb_, pt_[:, 0:TT], parts)
                if eg == 0:
                    k.op("vector", "tensor_copy", racc[:, mc * TT:(mc + 1) * TT],
                         pt_[:, 0:TT], ins=(pb_,), outs_war=(raccb,))
                else:
                    k.op("vector", "tensor_tensor", racc[:, mc * TT:(mc + 1) * TT],
                         racc[:, mc * TT:(mc + 1) * TT], pt_[:, 0:TT], ALU.add,
                         ins=(pb_,), finals=(raccb,) if mc == NCH - 1 else ())
        for mc in range(NCH):
            fo = P.sm[:, 0:TT]
            x3t = P.sm[:, TT:2 * TT]
            k.dma("s", fo, fout[mc * 128:(mc + 1) * 128, sl], ins=(foutb,), outs=(P.smb,))
            k.dma("g", x3t, src[mc * 128:(mc + 1) * 128, sl], ins=(srcb,), outs=(P.smb,))
            k.op("vector", "tensor_tensor", fo, fo, racc[:, mc * TT:(mc + 1) * TT],
                 ALU.add, ins=(raccb, P.smb), outs=(P.smb,))
            k.op("vector", "tensor_tensor", fo, fo, x3t, ALU.add, ins=(P.smb,), outs=(P.smb,))
            k.dma("s", dst[mc * 128:(mc + 1) * 128, sl], fo, ins=(P.smb,),
                  **gw(dstb, tt * NCH + mc, NTT * NCH))


BLOCK_BUILDERS = {}


BLOCK_BUILDERS["l0a"] = build_l0a
BLOCK_BUILDERS["l0f"] = build_l0f
BLOCK_BUILDERS["l1a"] = build_l1a
BLOCK_BUILDERS["l1f"] = build_l1f


def run_blocks(blocks, g, xs, cms, trace=False):
    P = Prog(blocks, g)
    nc = P.build()
    in_maps = []
    for c in range(NCORES):
        m = {n: g[n] for n in P.used_inputs if n in g}
        if "x_in" in P.used_inputs:
            m["x_in"] = xs[c]
        if "core_mask" in P.used_inputs:
            m["core_mask"] = cms[c]
        in_maps.append(m)
    return run_bass_kernel_spmd(nc, in_maps, core_ids=list(range(NCORES)), trace=trace)


def kernel(**inputs):
    g, xs, cms = stage_inputs(inputs)
    res = run_blocks(["l0a", "l0f", "l1a", "l1f"], g, xs, cms)
    outs = []
    for c in range(NCORES):
        o = res.results[c]["x_out"]
        outs.append(np.ascontiguousarray(o.T).reshape(2, SEQ, D))
    return np.concatenate(outs, axis=0)


# version 05
# speedup vs baseline: 1.1323x; 1.1323x over previous
import numpy as np
import ml_dtypes
from contextlib import ExitStack

import concourse.bass as bass
import concourse.mybir as mybir
from concourse.bass import IndirectOffsetOnAxis
from concourse.bass_utils import run_bass_kernel_spmd

F32 = mybir.dt.float32
BF16 = mybir.dt.bfloat16
I32 = mybir.dt.int32
AF = mybir.ActivationFunctionType
ALU = mybir.AluOpType
BF = ml_dtypes.bfloat16

NCORES = 8
D = 1024
SEQ = 2048
RPC = 4096
NCH = 8
TT = 512
NTT = RPC // TT
TPS = SEQ // TT
BIG = 1.0e9
OOBIDX = float(8 * 2048)
DBG = False

ECAP = 2048
GCAP = 10240

ENG_SEM = {"tensor": "pe", "vector": "vv", "scalar": "aa"}
DMA_Q = {"s": ("sync", "dms"), "g": ("gpsimd", "dmg"), "c": ("scalar", "dmc")}


def _merge(d, sem, cnt):
    if d.get(sem, 0) < cnt:
        d[sem] = cnt


class Buf:
    __slots__ = ("w", "r", "name", "wbig", "rsmall")

    def __init__(self, name=""):
        self.name = name
        self.w = {}
        self.r = {}
        self.wbig = True
        self.rsmall = False


def gw(buf, i, n):
    return dict(outs_war=(buf,), finals=(buf,) if i == n - 1 else ())


class Kb:
    def __init__(self, nc, es):
        self.nc = nc
        self.es = es
        self.streams = {e: [] for e in ("tensor", "vector", "scalar", "sync", "gpsimd")}
        self.sems = {}
        self.cnt = {}
        self.wm = {e: {} for e in self.streams}

    def sem(self, name):
        if name not in self.sems:
            self.sems[name] = self.es.enter_context(self.nc.semaphore(name))
            self.cnt[name] = 0
        return self.sems[name]

    def op(self, eng, meth, *args, ins=(), outs=(), outs_war=(), finals=(),
           sem=None, inc=1, **kw):
        if sem is None:
            sem = ENG_SEM[eng]
        try:
            big = args and hasattr(args[0], "free_size") and args[0].free_size() >= 384
        except Exception:
            big = False
        deps = {}
        dsmall = {}
        for b in ins:
            for s, c in b.w.items():
                _merge(deps, s, c)
                if not b.wbig:
                    dsmall[s] = True
        for b in tuple(outs) + tuple(outs_war):
            for s, c in b.w.items():
                _merge(deps, s, c)
                if not b.wbig:
                    dsmall[s] = True
            for s, c in b.r.items():
                _merge(deps, s, c)
                if b.rsmall:
                    dsmall[s] = True
        self.sem(sem)
        self.cnt[sem] += inc
        mark = (sem, self.cnt[sem])
        own = ENG_SEM.get(eng)
        waits = []
        for s, c in deps.items():
            if c <= 0:
                continue
            if s == own:
                if eng == "tensor":
                    continue
                if eng == "vector" and big and not dsmall.get(s):
                    continue
            if self.wm[eng].get(s, 0) >= c:
                continue
            waits.append((self.sems[s], c))
            self.wm[eng][s] = c
        semh = self.sems[sem]

        def fn(e, waits=waits, meth=meth, args=args, kw=kw, semh=semh, inc=inc):
            for h, c in waits:
                e.wait_ge(h, c)
            getattr(e, meth)(*args, **kw).then_inc(semh, inc)

        self.streams[eng].append(fn)
        for b in ins:
            _merge(b.r, *mark)
            if sem == ENG_SEM.get(eng) and not big:
                b.rsmall = True
        for b in tuple(outs) + tuple(finals):
            b.w = {mark[0]: mark[1]}
            b.r = {}
            b.wbig = big or eng not in ("vector", "scalar")
            b.rsmall = False
        return mark

    def dma(self, q, out, in_, **kw):
        eng, sem = DMA_Q[q]
        return self.op(eng, "dma_start", sem=sem, inc=16, out=out, in_=in_, **kw)

    def idma(self, out, out_offset, in_, in_offset, nmax=None, compute=ALU.bypass, **kw):
        return self.op("gpsimd", "indirect_dma_start", out, out_offset, in_,
                       in_offset, sem="dmg", inc=16, compute_op=compute, **kw)

    def mm(self, bank, out_ap, parts):
        n = len(parts)
        for j, (lh, rh, ib) in enumerate(parts):
            self.op("tensor", "matmul", out_ap, lh, rh,
                    start=(j == 0), stop=(j == n - 1), ins=ib,
                    outs_war=(bank,) if j == 0 else (),
                    finals=(bank,) if j == n - 1 else ())

    def sb(self, name, shape, dtype):
        t = self.es.enter_context(self.nc.sbuf_tensor(name, list(shape), dtype))
        return t, Buf(name)

    def ps(self, name, shape, dtype=F32):
        t = self.es.enter_context(self.nc.psum_tensor(name, list(shape), dtype))
        return t, Buf(name)



def _vec128(v):
    v = np.asarray(v, np.float32)
    m = v.shape[0]
    if m < 128:
        v = np.concatenate([v, np.zeros(128 - m, np.float32)])
        return np.ascontiguousarray(v.reshape(1, 128).transpose())
    return np.ascontiguousarray(v.reshape(m // 128, 128).transpose())


def stage_inputs(inp):
    g = {}
    g["w_l0a_in"] = np.ascontiguousarray(np.asarray(inp["L0_attn_in_w"], np.float32).T)
    g["w_l0a_pw"] = np.ascontiguousarray(np.asarray(inp["L0_attn_pw_w"], np.float32).T)
    g["w_l0a_win"] = np.ascontiguousarray(np.concatenate(
        [np.asarray(inp["L0_attn_f_win"], np.float32).T,
         np.asarray(inp["L0_attn_b_win"], np.float32).T], axis=1))
    g["w_l0a_wout"] = np.ascontiguousarray(np.concatenate(
        [np.asarray(inp["L0_attn_f_wout"], np.float32).T,
         np.asarray(inp["L0_attn_b_wout"], np.float32).T], axis=0))
    g["w_l0a_out"] = np.ascontiguousarray(np.asarray(inp["L0_attn_out_w"], np.float32).T)
    g["v_l0a_inb"] = _vec128(inp["L0_attn_in_b"])
    g["v_l0a_dwb"] = _vec128(inp["L0_attn_dw_b"])
    g["v_l0a_pwb"] = _vec128(inp["L0_attn_pw_b"])
    g["v_l0a_outb"] = _vec128(inp["L0_attn_out_b"])
    g["v_l0a_ns"] = _vec128(inp["L0_attn_norm_scale"])
    dw = np.asarray(inp["L0_attn_dw_w"], np.float32)[:, 0, :]
    g["v_l0a_dww"] = np.ascontiguousarray(
        dw.reshape(8, 128, 3).transpose(1, 0, 2).reshape(128, 24))
    ssm = np.zeros((128, 4), np.float32)
    ssm[:64, 0] = inp["L0_attn_f_logA"]; ssm[64:, 0] = inp["L0_attn_b_logA"]
    ssm[:64, 1] = inp["L0_attn_f_B"];    ssm[64:, 1] = inp["L0_attn_b_B"]
    ssm[:64, 2] = inp["L0_attn_f_C"];    ssm[64:, 2] = inp["L0_attn_b_C"]
    ssm[:64, 3] = float(inp["L0_attn_f_logdt"]); ssm[64:, 3] = float(inp["L0_attn_b_logdt"])
    g["v_l0a_ssm"] = ssm

    qw_ = np.asarray(inp["L0_ffn_qw"], np.float32)
    qb_ = np.asarray(inp["L0_ffn_qb"], np.float32)
    pairs = [(0, 4), (1, 5), (2, 6), (3, 7), (8, 12), (9, 13), (10, 14), (11, 15)]
    qw_cols = []
    qb_cols = np.zeros((128, 8), np.float32)
    for j, (ha, hb) in enumerate(pairs):
        qw_cols.append(qw_[:, ha * 64:(ha + 1) * 64])
        qw_cols.append(qw_[:, hb * 64:(hb + 1) * 64])
        qb_cols[:64, j] = qb_[ha * 64:(ha + 1) * 64]
        qb_cols[64:, j] = qb_[hb * 64:(hb + 1) * 64]
    g["w_l0f_qw"] = np.ascontiguousarray(np.concatenate(qw_cols, axis=1))
    g["w_l0f_kw"] = np.ascontiguousarray(np.asarray(inp["L0_ffn_kw"], np.float32))
    g["w_l0f_vw"] = np.ascontiguousarray(np.asarray(inp["L0_ffn_vw"], np.float32))
    g["w_l0f_ow"] = np.ascontiguousarray(np.asarray(inp["L0_ffn_ow"], np.float32))
    g["w_l0f_gate"] = np.ascontiguousarray(np.asarray(inp["L0_ffn_gate_w"], np.float32))
    g["w_l0f_w1"] = np.ascontiguousarray(np.asarray(inp["L0_ffn_w1"], np.float32))
    g["w_l0f_w2"] = np.ascontiguousarray(np.asarray(inp["L0_ffn_w2"], np.float32))
    g["v_l0f_qb"] = qb_cols
    g["v_l0f_kb"] = _vec128(inp["L0_ffn_kb"])
    g["v_l0f_vb"] = _vec128(inp["L0_ffn_vb"])
    g["v_l0f_ob"] = _vec128(inp["L0_ffn_ob"])
    g["v_l0f_n1"] = _vec128(inp["L0_ffn_norm1_w"])
    g["v_l0f_n2"] = _vec128(inp["L0_ffn_norm2_w"])

    Wqkv = np.asarray(inp["L1_attn_W_qkv"], np.float32)
    bqkv = np.asarray(inp["L1_attn_b_qkv"], np.float32)
    qk_cols = []
    qkb_cols = np.zeros((128, 16), np.float32)
    vb_cols = np.zeros((64, 16), np.float32)
    for h in range(16):
        qk_cols.append(Wqkv[:, h * 64:(h + 1) * 64])
        qk_cols.append(Wqkv[:, 1024 + h * 64:1024 + (h + 1) * 64])
        qkb_cols[:64, h] = bqkv[h * 64:(h + 1) * 64]
        qkb_cols[64:, h] = bqkv[1024 + h * 64:1024 + (h + 1) * 64]
        vb_cols[:, h] = bqkv[2048 + h * 64:2048 + (h + 1) * 64]
    g["w_l1a_qk"] = np.ascontiguousarray(np.concatenate(qk_cols, axis=1))
    g["w_l1a_v"] = np.ascontiguousarray(Wqkv[:, 2048:])
    g["v_l1a_qkb"] = qkb_cols
    g["v_l1a_vb2"] = np.ascontiguousarray(np.vstack([vb_cols, vb_cols]))
    g["w_l1a_out"] = np.ascontiguousarray(np.asarray(inp["L1_attn_W_out"], np.float32))
    g["v_l1a_outb"] = _vec128(inp["L1_attn_b_out"])
    gb = np.zeros((128, 2), np.float32)
    gb[:64, 0] = inp["L1_attn_q_gamma"]; gb[64:, 0] = inp["L1_attn_k_gamma"]
    gb[:64, 1] = inp["L1_attn_q_beta"];  gb[64:, 1] = inp["L1_attn_k_beta"]
    g["v_l1a_gb"] = gb

    g["w_l1f_router"] = np.ascontiguousarray(np.asarray(inp["L1_ffn_router_w"], np.float32).T)
    g["v_l1f_corr"] = np.ascontiguousarray(
        np.asarray(inp["L1_ffn_correction_bias"], np.float32).reshape(1, 16))
    g["w_l1f_g"] = np.ascontiguousarray(np.asarray(inp["L1_ffn_gate_w"], np.float32))
    g["w_l1f_u"] = np.ascontiguousarray(np.asarray(inp["L1_ffn_up_w"], np.float32))
    g["w_l1f_d"] = np.ascontiguousarray(np.asarray(inp["L1_ffn_down_w"], np.float32))
    g["w_l1f_sg"] = np.ascontiguousarray(np.asarray(inp["L1_ffn_shared_gate_w"], np.float32).T)
    g["w_l1f_su"] = np.ascontiguousarray(np.asarray(inp["L1_ffn_shared_up_w"], np.float32).T)
    g["w_l1f_sd"] = np.ascontiguousarray(np.asarray(inp["L1_ffn_shared_down_w"], np.float32).T)

    x = np.asarray(inp["L0_attn_T"], np.float32)
    xs = [np.ascontiguousarray(x[2 * c:2 * c + 2].reshape(RPC, D).T) for c in range(NCORES)]
    cms = [np.asarray([[1.0 if r < c else 0.0] for r in range(8)], np.float32)
           for c in range(NCORES)]
    return g, xs, cms



class Prog:
    def __init__(self, blocks, g):
        self.blocks = blocks
        self.gshapes = {n: v.shape for n, v in g.items()}
        nc = bass.Bass("TRN2", target_bir_lowering=False,
                       dynamic_dma_scratch_size=49152)
        self.nc = nc
        self.es = ExitStack()
        self.k = Kb(nc, self.es)
        self.dram = {}
        self.dbuf = {}
        self.used_inputs = set()
        self._internal("xdram", [D, RPC], F32)
        self._internal("gate_sp", [D, RPC], BF16)
        self._internal("sdram", [D, RPC], F32)
        self._internal("fout", [D, RPC], F32)
        self._internal("hg", [2048, RPC], BF16)
        self._internal("ebuf", [8 * ECAP + 1, D], BF16)
        self._internal("ybuf", [8 * ECAP + 1, D], F32)
        self._internal("cc_in", [1, 8], F32)
        self._internal("cc_out", [8, 8], F32)
        self._internal("rowsd", [16, RPC], F32)
        self._internal("cmbd", [16, RPC], F32)
        self.dram["x_out"] = nc.dram_tensor("x_out", [D, RPC], F32, kind="ExternalOutput")
        self.dbuf["x_out"] = Buf("x_out")
        self.const = {}
        self.cadd("ident", np.eye(128, dtype=BF))
        self.cadd("identf", np.eye(128, dtype=np.float32))
        self.cadd("iota8B", (np.broadcast_to(np.arange(8, dtype=np.float32), (128, 8)) + 65536.0).copy())
        self.cadd("ustrict", np.ascontiguousarray(np.triu(np.ones((128, 128), BF), 1)))
        self.cadd("ones1", np.ones((1, 128), BF))
        self.cadd("ones1h", np.ones((1, 64), BF))
        self.cadd("onesc1", np.ones((128, 1), BF))
        oh0 = np.zeros((128, 1), BF); oh0[:64] = 1
        oh1 = np.zeros((128, 1), BF); oh1[64:] = 1
        self.cadd("onesc_h0", oh0)
        self.cadd("onesc_h1", oh1)
        self.cadd("iota8", np.broadcast_to(np.arange(8, dtype=np.float32), (128, 8)).copy())
        self.cadd("iota16", np.broadcast_to(np.arange(16, dtype=np.float32), (128, 16)).copy())
        self.cadd("iota16B", (np.broadcast_to(np.arange(16, dtype=np.float32), (128, 16)) + 65536.0).copy())
        self.cadd("grp16", np.broadcast_to((np.arange(16) // 4).astype(np.float32), (128, 16)).copy())
        kk = np.arange(128)[:, None]; qq = np.arange(128)[None, :]
        self.cadd("m_diag", np.where(qq >= kk, 0.0, -BIG).astype(np.float32))
        self.cadd("m_win", np.where(qq < kk, 0.0, -BIG).astype(np.float32))
        self.cadd("m_full", np.full((128, 128), -BIG, np.float32))

    def cadd(self, name, arr):
        self.const[name] = (self.nc.inline_tensor(np.ascontiguousarray(arr), name="c_" + name),
                            arr.dtype, arr.shape)

    def _internal(self, name, shape, dtype):
        self.dram[name] = self.nc.dram_tensor(name, list(shape), dtype, kind="Internal")
        self.dbuf[name] = Buf(name)

    def d(self, name):
        if name not in self.dram:
            if name.endswith("_bf"):
                base = name[:-3]
                self._internal(name, list(self.gshapes[base]), BF16)
            else:
                self.dram[name] = self.nc.dram_tensor(
                    name, list(self.gshapes[name]) if name in self.gshapes else
                    ([D, RPC] if name == "x_in" else [8, 1]), F32, kind="ExternalInput")
                self.dbuf[name] = Buf(name)
                self.used_inputs.add(name)
        return self.dram[name].ap(), self.dbuf[name]

    def build(self):
        k = self.k
        with self.es:
            self.pool = []
            for i in range(5):
                t, b = k.ps(f"p{i}", [128, 512], F32)
                self.pool.append((t, b))
            self.pt, self.ptb = k.ps("pt", [128, 1024], BF16)
            self.pb1, self.pb1b = k.ps("pb1", [128, 512], F32)
            self.pb2, self.pb2b = k.ps("pb2", [128, 512], F32)
            self._pool_i = 0
            self.csb = {}
            for n, (t, dt, shp) in self.const.items():
                sdt = BF16 if dt == BF else F32
                st, sbf = k.sb("cs_" + n, list(shp), sdt)
                k.dma("s", st[:], t.ap(), outs=(sbf,))
                self.csb[n] = (st, sbf)
            self.cflt, self.cfltb = k.sb("cflt", [128, 8], F32)
            self._fconsts = {}
            self.wslab, self.wslabb = k.sb("wslab", [128, 16384], BF16)
            self.abig, self.abigb = k.sb("abig", [128, 16384], BF16)
            self.bbig, self.bbigb = k.sb("bbig", [128, 16384], BF16)
            self.f1, self.f1b = k.sb("f1", [128, 4096], F32)
            self.s1b, self.s1bb = k.sb("s1b", [128, 4096], BF16)
            self.s2b, self.s2bb = k.sb("s2b", [128, 4096], BF16)
            self.sm, self.smb = k.sb("sm", [128, 2048], F32)
            self.smb_t, self.smb_b = k.sb("smb", [128, 2048], BF16)
            self.rsc, self.rscb = k.sb("rsc", [1, 2048], F32)
            self.vt, self.vtb = k.sb("vt", [128, 4 * 16 * 65], BF16)
            self.rbf, self.rbfb = k.sb("rbf", [1, 2048], BF16)
            for blk in self.blocks:
                BLOCK_BUILDERS[blk](self)
            with self.nc.Block() as block:
                for eng in ("sync", "gpsimd", "tensor", "vector", "scalar"):
                    fns = self.k.streams[eng]
                    getattr(block, eng)(lambda e, fns=fns: [f(e) for f in fns])
        return self.nc

    def fconst(self, val, rows=1):
        if val not in self._fconsts:
            i = len(self._fconsts)
            self.k.op("vector", "memset", self.cflt[:, i:i + 1], float(val),
                      outs=(self.cfltb,))
            self._fconsts[val] = i
        i = self._fconsts[val]
        return self.cflt[0:rows, i:i + 1]

    def bank(self):
        t, b = self.pool[self._pool_i % len(self.pool)]
        self._pool_i += 1
        return t, b

    def precast(self, name, nsplit=1):
        k = self.k
        src, sbuf_ = self.d(name)
        dst, dbuf_ = self.d(name + "_bf")
        n0 = src.shape[0]
        step = n0 // nsplit
        for i in range(nsplit):
            k.dma("g", dst[i * step:(i + 1) * step], src[i * step:(i + 1) * step],
                  ins=(sbuf_,), **gw(dbuf_, i, nsplit))

    def load_w(self, name, cols, koff=0, kdim=None, moff=0, mdim=None, expert=None,
               q="s"):
        k = self.k
        ap, dbuf_ = self.d(name + "_bf")
        if expert is not None:
            ap = ap[expert]
        Kfull, Mfull = ap.shape
        kdim = kdim if kdim is not None else Kfull
        mdim = mdim if mdim is not None else Mfull
        nkc = kdim // 128
        view = ap[koff:koff + kdim, moff:moff + mdim].rearrange(
            "(kc p) m -> p kc m", p=128)
        dst = self.wslab[:, cols:cols + nkc * mdim].rearrange(
            "p (kc m) -> p kc m", m=mdim)
        k.dma(q, dst, view, ins=(dbuf_,), outs=(self.wslabb,))

        def acc(kc, mc=0, width=128, mo=None):
            c0 = cols + kc * mdim + (mo if mo is not None else mc * 128)
            return self.wslab[:, c0:c0 + width]
        return acc

    def dbg_dump(self, tag, src_name):
        if not DBG:
            return
        ap, b = self.d(src_name)
        name = "dbg_" + tag
        self.dram[name] = self.nc.dram_tensor(name, list(ap.shape), ap.dtype,
                                              kind="ExternalOutput")
        self.dbuf[name] = Buf(name)
        self.k.dma("c", self.dram[name].ap(), ap[:], ins=(b,),
                   outs=(self.dbuf[name],))

    def load_vec(self, name):
        if not hasattr(self, "_vecs"):
            self._vecs = {}
        if name in self._vecs:
            return self._vecs[name]
        k = self.k
        ap, dbuf_ = self.d(name)
        t, b = k.sb("sv_" + name, list(ap.shape), F32)
        k.dma("s", t[:], ap[:], ins=(dbuf_,), outs=(b,))
        self._vecs[name] = (t, b)
        return t, b

    def bcast(self, hi, lo, psum_t, psum_b, n, pair2=None):
        k = self.k
        if pair2 is None:
            cs, csb_ = self.csb["ones1"]
            k.mm(psum_b, psum_t[:, 0:n],
                 [(cs[0:1, :], hi, [csb_, self.rbfb]),
                  (cs[0:1, :], lo, [csb_, self.rbfb])])
        else:
            ch, chb = self.csb["ones1h"]
            k.mm(psum_b, psum_t[0:64, 0:n],
                 [(ch[0:1, :], hi, [chb, self.rbfb]),
                  (ch[0:1, :], lo, [chb, self.rbfb])])
            k.mm(psum_b, psum_t[64:128, 0:n],
                 [(ch[0:1, :], pair2[0], [chb, self.rbfb]),
                  (ch[0:1, :], pair2[1], [chb, self.rbfb])])

    def use_rows(self, slots, tt, width=TT):
        k = self.k
        rd, rdb = self.d("rowsd")
        out = []
        for i, sl in enumerate(slots):
            fsl = self.rsc[0:1, i * TT:i * TT + width]
            k.dma("s", fsl, rd[sl:sl + 1, tt * TT:tt * TT + width],
                  ins=(rdb,), outs=(self.rscb,))
            hi = self.rbf[0:1, (2 * i) * TT:(2 * i) * TT + width]
            lo = self.rbf[0:1, (2 * i + 1) * TT:(2 * i + 1) * TT + width]
            k.op("vector", "tensor_copy", hi, fsl, ins=(self.rscb,), outs=(self.rbfb,))
            k.op("vector", "tensor_tensor", lo, fsl, hi, ALU.subtract,
                 ins=(self.rscb, self.rbfb), outs=(self.rbfb,))
            out.append((hi, lo))
        return out

    def stats_pass(self, src_name, kind, out_slots):
        k = self.k
        src, srcb = self.d(src_name)
        rd, rdb = self.d("rowsd")
        cs1, cs1b = self.csb["onesc1"]
        xbt, xbtb = self.s1b, self.s1bb
        xsq, xsqb = self.s2b, self.s2bb
        R, Rb = self.rsc, self.rscb

        def rs(i):
            return R[0:1, i * TT:(i + 1) * TT]

        for tt in range(NTT):
            sl = slice(tt * TT, (tt + 1) * TT)
            for c in range(NCH):
                k.dma("g", xbt[:, c * TT:(c + 1) * TT],
                      src[c * 128:(c + 1) * 128, sl], ins=(srcb,), **gw(xbtb, c, NCH))
            for c in range(NCH):
                k.op("scalar", "activation", xsq[:, c * TT:(c + 1) * TT],
                     xbt[:, c * TT:(c + 1) * TT], AF.Square, ins=(xbtb,),
                     **gw(xsqb, c, NCH))
            p2, p2b = self.bank()
            k.mm(p2b, p2[0:1, 0:TT],
                 [(cs1[:, 0:1], xsq[:, c * TT:(c + 1) * TT], [cs1b, xsqb])
                  for c in range(NCH)])
            if kind in ("ln", "lnrms"):
                p1, p1b = self.bank()
                k.mm(p1b, p1[0:1, 0:TT],
                     [(cs1[:, 0:1], xbt[:, c * TT:(c + 1) * TT], [cs1b, xbtb])
                      for c in range(NCH)])
                k.op("vector", "tensor_scalar", rs(0), p1[0:1, 0:TT], 1.0 / D, None,
                     ALU.mult, ins=(p1b,), outs=(Rb,))
                k.op("vector", "tensor_tensor", rs(1), rs(0), rs(0), ALU.mult,
                     ins=(Rb,), outs=(Rb,))
                k.op("vector", "scalar_tensor_tensor", rs(1), p2[0:1, 0:TT], 1.0 / D,
                     rs(1), ALU.mult, ALU.subtract, ins=(p2b, Rb), outs=(Rb,))
                k.op("scalar", "activation", rs(2), rs(1), AF.Ln, bias=np.float32(1e-5),
                     ins=(Rb,), outs=(Rb,))
                k.op("scalar", "activation", rs(2), rs(2), AF.Exp, scale=-0.5,
                     ins=(Rb,), outs=(Rb,))
                k.op("vector", "tensor_tensor", rs(3), rs(0), rs(2), ALU.mult,
                     ins=(Rb,), outs=(Rb,))
                k.op("vector", "tensor_scalar", rs(3), rs(3), -1.0, None, ALU.mult,
                     ins=(Rb,), outs=(Rb,))
                k.dma("s", rd[out_slots[0]:out_slots[0] + 1, sl], rs(2), ins=(Rb,), outs=(rdb,))
                k.dma("s", rd[out_slots[1]:out_slots[1] + 1, sl], rs(3), ins=(Rb,), outs=(rdb,))
                if kind == "lnrms":
                    k.op("vector", "tensor_tensor", rs(0), rs(2), rs(2), ALU.mult,
                         ins=(Rb,), outs=(Rb,))
                    k.op("vector", "tensor_tensor", rs(0), rs(0), rs(1), ALU.mult,
                         ins=(Rb,), outs=(Rb,))
                    k.op("scalar", "activation", rs(0), rs(0), AF.Ln, bias=np.float32(1e-6),
                         ins=(Rb,), outs=(Rb,))
                    k.op("scalar", "activation", rs(0), rs(0), AF.Exp, scale=-0.5,
                         ins=(Rb,), outs=(Rb,))
                    k.op("vector", "tensor_tensor", rs(1), rs(2), rs(0), ALU.mult,
                         ins=(Rb,), outs=(Rb,))
                    k.dma("s", rd[out_slots[2]:out_slots[2] + 1, sl], rs(1), ins=(Rb,), outs=(rdb,))
                    k.op("vector", "tensor_tensor", rs(1), rs(3), rs(0), ALU.mult,
                         ins=(Rb,), outs=(Rb,))
                    k.dma("s", rd[out_slots[3]:out_slots[3] + 1, sl], rs(1), ins=(Rb,), outs=(rdb,))
                    k.op("vector", "tensor_scalar", rs(1), rs(2), 1.0, None, ALU.add,
                         ins=(Rb,), outs=(Rb,))
                    k.dma("s", rd[out_slots[4]:out_slots[4] + 1, sl], rs(1), ins=(Rb,), outs=(rdb,))
            else:
                k.op("scalar", "activation", rs(2), p2[0:1, 0:TT], AF.Ln, bias=np.float32(1e-6),
                     scale=1.0 / D, ins=(p2b,), outs=(Rb,))
                k.op("scalar", "activation", rs(2), rs(2), AF.Exp, scale=-0.5,
                     ins=(Rb,), outs=(Rb,))
                k.dma("s", rd[out_slots[0]:out_slots[0] + 1, sl], rs(2), ins=(Rb,), outs=(rdb,))



def build_l0a(P):
    k = P.k
    src_name = "x_in" if P.blocks[0] == "l0a" else "xdram"
    dst_name = "x_out" if P.blocks[-1] == "l0a" else "xdram"
    src, srcb = P.d(src_name)
    dst, dstb = P.d(dst_name)
    gate_d, gate_db = P.d("gate_sp")

    for w in ("w_l0a_in", "w_l0a_pw", "w_l0a_win", "w_l0a_wout", "w_l0a_out"):
        P.precast(w)
    v_inb = P.load_vec("v_l0a_inb")
    v_dwb = P.load_vec("v_l0a_dwb")
    v_pwb = P.load_vec("v_l0a_pwb")
    v_outb = P.load_vec("v_l0a_outb")
    v_ns = P.load_vec("v_l0a_ns")
    v_dww = P.load_vec("v_l0a_dww")
    v_ssm = P.load_vec("v_l0a_ssm")

    P.stats_pass(src_name, "lnrms", [0, 1, 2, 3, 4])

    sc = P.sm[:, 0:8]
    scb = P.smb
    ssm_t, ssm_b = v_ssm

    def scol(i):
        return sc[:, i:i + 1]

    k.op("scalar", "activation", scol(0), ssm_t[:, 3:4], AF.Exp, ins=(ssm_b,), outs=(scb,))
    k.op("scalar", "activation", scol(1), ssm_t[:, 0:1], AF.Exp, ins=(ssm_b,), outs=(scb,))
    k.op("vector", "tensor_scalar", scol(1), scol(1), -1.0, None, ALU.mult, ins=(scb,), outs=(scb,))
    k.op("vector", "tensor_tensor", scol(2), scol(1), scol(0), ALU.mult, ins=(scb,), outs=(scb,))
    k.op("scalar", "activation", scol(2), scol(2), AF.Exp, ins=(scb,), outs=(scb,))
    k.op("scalar", "activation", scol(3), scol(1), AF.Abs, ins=(scb,), outs=(scb,))
    k.op("vector", "tensor_scalar", scol(4), scol(3), 1e-6, None, ALU.is_lt, ins=(scb,), outs=(scb,))
    k.op("vector", "tensor_tensor", scol(5), scol(1), scol(4), ALU.mult, ins=(scb,), outs=(scb,))
    k.op("vector", "tensor_tensor", scol(5), scol(1), scol(5), ALU.subtract, ins=(scb,), outs=(scb,))
    k.op("vector", "tensor_tensor", scol(5), scol(5), scol(4), ALU.add, ins=(scb,), outs=(scb,))
    k.op("vector", "reciprocal", scol(6), scol(5), ins=(scb,), outs=(scb,))
    k.op("vector", "tensor_scalar", scol(7), scol(2), -1.0, None, ALU.add, ins=(scb,), outs=(scb,))
    k.op("vector", "tensor_tensor", scol(6), scol(6), scol(7), ALU.mult, ins=(scb,), outs=(scb,))
    k.op("vector", "tensor_tensor", scol(7), scol(6), scol(4), ALU.mult, ins=(scb,), outs=(scb,))
    k.op("vector", "tensor_tensor", scol(6), scol(6), scol(7), ALU.subtract, ins=(scb,), outs=(scb,))
    k.op("vector", "tensor_tensor", scol(7), scol(0), scol(4), ALU.mult, ins=(scb,), outs=(scb,))
    k.op("vector", "tensor_tensor", scol(6), scol(6), scol(7), ALU.add, ins=(scb,), outs=(scb,))
    k.op("vector", "tensor_tensor", scol(7), scol(6), ssm_t[:, 1:2], ALU.mult,
         ins=(scb, ssm_b), outs=(scb,))

    w_in = P.load_w("w_l0a_in", 0)
    tn, tnb = P.s2b, P.s2bb
    uss, ussb = P.f1, P.f1b

    for s in range(2):
        xh, xhb = (P.abig, P.abigb) if s == 0 else (P.bbig, P.bbigb)
        for lt in range(TPS):
            tt = s * TPS + lt
            sl = slice(tt * TT, (tt + 1) * TT)
            rws = P.use_rows([2, 3], tt)
            xbt, xbtb = P.s1b, P.s1bb
            for c in range(NCH):
                k.dma("g", xbt[:, c * TT:(c + 1) * TT],
                      src[c * 128:(c + 1) * 128, sl], ins=(srcb,), **gw(xbtb, c, NCH))
            P.bcast(rws[0][0], rws[0][1], P.pb1, P.pb1b, TT)
            P.bcast(rws[1][0], rws[1][1], P.pb2, P.pb2b, TT)
            for c in range(NCH):
                csl = slice(c * TT, (c + 1) * TT)
                k.op("vector", "tensor_tensor", tn[:, csl], xbt[:, csl],
                     P.pb1[:, 0:TT], ALU.mult, ins=(xbtb, P.pb1b), outs_war=(tnb,))
                k.op("vector", "tensor_tensor", tn[:, csl], tn[:, csl],
                     P.pb2[:, 0:TT], ALU.add, ins=(P.pb2b,))
                k.op("vector", "tensor_scalar", tn[:, csl], tn[:, csl],
                     v_ns[0][:, c:c + 1], None, ALU.mult, ins=(v_ns[1],),
                     finals=(tnb,) if c == NCH - 1 else ())
            for mc in range(16):
                pt_, pb_ = P.bank()
                k.mm(pb_, pt_[:, 0:TT],
                     [(w_in(kc, mc), tn[:, kc * TT:(kc + 1) * TT], [P.wslabb, tnb])
                      for kc in range(NCH)])
                if mc < 8:
                    k.op("vector", "tensor_scalar",
                         xh[:, mc * SEQ + lt * TT: mc * SEQ + (lt + 1) * TT],
                         pt_[:, 0:TT], v_inb[0][:, mc:mc + 1], None, ALU.add,
                         ins=(pb_, v_inb[1]), **gw(xhb, lt * 8 + mc, TPS * 8))
                else:
                    gt = P.smb_t[:, 0:TT]
                    k.op("scalar", "activation", gt, pt_[:, 0:TT], AF.Sigmoid,
                         bias=v_inb[0][:, mc:mc + 1], ins=(pb_, v_inb[1]),
                         outs=(P.smb_b,))
                    k.dma("s", gate_d[(mc - 8) * 128:(mc - 7) * 128, sl], gt,
                          ins=(P.smb_b,), outs=(gate_db,))

    w_pw = P.load_w("w_l0a_pw", 0)
    w_win = P.load_w("w_l0a_win", 8192)

    for s in range(2):
        xh, xhb = (P.abig, P.abigb) if s == 0 else (P.bbig, P.bbigb)
        for lt in range(TPS):
            tt = s * TPS + lt
            cvt, cvtb = P.s1b, P.s1bb
            acc = P.sm[:, 512:1024]
            accb = P.smb
            t0_, t1_ = lt * TT, (lt + 1) * TT
            for c in range(NCH):
                xrow = xh[:, c * SEQ: (c + 1) * SEQ]
                w0 = v_dww[0][:, c * 3 + 0: c * 3 + 1]
                w1 = v_dww[0][:, c * 3 + 1: c * 3 + 2]
                w2 = v_dww[0][:, c * 3 + 2: c * 3 + 3]
                k.op("vector", "tensor_scalar", acc, xrow[:, t0_:t1_],
                     w1, v_dwb[0][:, c:c + 1], ALU.mult, ALU.add,
                     ins=(xhb, v_dww[1], v_dwb[1]), outs=(accb,))
                l0 = 1 if lt == 0 else 0
                k.op("vector", "scalar_tensor_tensor", acc[:, l0:TT],
                     xrow[:, t0_ + l0 - 1:t1_ - 1], w0, acc[:, l0:TT],
                     ALU.mult, ALU.add, ins=(xhb,), outs=(accb,))
                r = TT - 1 if lt == TPS - 1 else TT
                k.op("vector", "scalar_tensor_tensor",
                     cvt[:, c * TT:c * TT + r],
                     xrow[:, t0_ + 1:t0_ + 1 + r], w2, acc[:, 0:r],
                     ALU.mult, ALU.add, ins=(xhb, accb), outs_war=(cvtb,))
                if r < TT:
                    k.op("vector", "tensor_copy", cvt[:, c * TT + r:(c + 1) * TT],
                         acc[:, r:TT], ins=(accb,))
                if c == NCH - 1:
                    k.op("vector", "engine_nop", finals=(cvtb,))
            xcv, xcvb = P.s2b, P.s2bb
            for mc in range(NCH):
                pt_, pb_ = P.bank()
                k.mm(pb_, pt_[:, 0:TT],
                     [(w_pw(kc, mc), cvt[:, kc * TT:(kc + 1) * TT], [P.wslabb, cvtb])
                      for kc in range(NCH)])
                k.op("vector", "tensor_scalar", xcv[:, mc * TT:(mc + 1) * TT],
                     pt_[:, 0:TT], v_pwb[0][:, mc:mc + 1], None, ALU.add,
                     ins=(pb_, v_pwb[1]), **gw(xcvb, mc, NCH))
            pt_, pb_ = P.bank()
            k.mm(pb_, pt_[:, 0:TT],
                 [(w_win(kc, 0), xcv[:, kc * TT:(kc + 1) * TT], [P.wslabb, xcvb])
                  for kc in range(NCH)])
            k.op("scalar", "copy", uss[:, tt * TT:(tt + 1) * TT],
                 pt_[:, 0:TT], ins=(pb_,), **gw(ussb, tt, NTT))

    bst, bstb = P.s2b, P.s2bb
    k.op("vector", "tensor_scalar", bst[:], uss[:], scol(7), None, ALU.mult,
         ins=(ussb, scb), outs=(bstb,))
    abt, abtb = k.sb("l0a_abt", [128, SEQ], F32)
    k.op("vector", "memset", abt[:], 1.0, outs=(abtb,))
    k.op("vector", "tensor_scalar", abt[:], abt[:], scol(2), None, ALU.mult,
         ins=(scb,), outs=(abtb,))
    for s in range(2):
        sl = slice(s * SEQ, (s + 1) * SEQ)
        k.op("vector", "tensor_tensor_scan", uss[0:64, sl], abt[0:64, :],
             bst[0:64, sl], 0.0, ALU.mult, ALU.add, ins=(bstb, abtb), outs=(ussb,))
        k.op("vector", "tensor_tensor_scan", uss[64:128, sl][:, ::-1],
             abt[64:128, :][:, ::-1], bst[64:128, sl][:, ::-1], 0.0,
             ALU.mult, ALU.add, ins=(bstb, abtb), outs=(ussb,))
    ysb, ysbb = P.s1b, P.s1bb
    k.op("vector", "tensor_scalar", ysb[:], uss[:], ssm_t[:, 2:3], None, ALU.mult,
         ins=(ussb, ssm_b), outs=(ysbb,))

    w_wout = P.load_w("w_l0a_wout", 0)
    w_out = P.load_w("w_l0a_out", 1024)
    ygb, ygbb = P.s2b, P.s2bb
    xf, xfb = P.f1, P.f1b
    res = P.sm[:, 1024:1536]
    resb = P.smb
    for tt in range(NTT):
        sl = slice(tt * TT, (tt + 1) * TT)
        rws = P.use_rows([4, 1], tt)
        for c in range(NCH):
            k.dma("g", xf[:, c * TT:(c + 1) * TT], src[c * 128:(c + 1) * 128, sl],
                  ins=(srcb,), **gw(xfb, c, NCH))
        for mc in range(NCH):
            gt = P.smb_t[:, 1024 + (mc % 2) * TT: 1024 + (mc % 2 + 1) * TT]
            k.dma("s", gt, gate_d[mc * 128:(mc + 1) * 128, sl],
                  ins=(gate_db,), outs=(P.smb_b,))
            pt_, pb_ = P.bank()
            k.mm(pb_, pt_[:, 0:TT], [(w_wout(0, mc), ysb[:, sl], [P.wslabb, ysbb])])
            k.op("vector", "tensor_tensor", ygb[:, mc * TT:(mc + 1) * TT],
                 pt_[:, 0:TT], gt, ALU.mult, ins=(pb_, P.smb_b), **gw(ygbb, mc, NCH))
        P.bcast(rws[0][0], rws[0][1], P.pb1, P.pb1b, TT)
        P.bcast(rws[1][0], rws[1][1], P.pb2, P.pb2b, TT)
        for mc in range(NCH):
            pt_, pb_ = P.bank()
            k.mm(pb_, pt_[:, 0:TT],
                 [(w_out(kc, mc), ygb[:, kc * TT:(kc + 1) * TT], [P.wslabb, ygbb])
                  for kc in range(NCH)])
            csl = slice(mc * TT, (mc + 1) * TT)
            k.op("vector", "tensor_tensor", res, xf[:, csl], P.pb1[:, 0:TT],
                 ALU.mult, ins=(xfb, P.pb1b), outs=(resb,))
            k.op("vector", "tensor_tensor", res, res, P.pb2[:, 0:TT],
                 ALU.add, ins=(P.pb2b,))
            k.op("vector", "tensor_tensor", res, res, pt_[:, 0:TT],
                 ALU.add, ins=(pb_,))
            k.op("vector", "tensor_scalar", res, res,
                 v_outb[0][:, mc:mc + 1], None, ALU.add, ins=(v_outb[1],),
                 outs=(resb,))
            k.dma("s", dst[mc * 128:(mc + 1) * 128, sl], res, ins=(resb,),
                  **gw(dstb, tt * NCH + mc, NTT * NCH))




def build_l0f(P):
    k = P.k
    i = P.blocks.index("l0f")
    src_name = "x_in" if i == 0 else "xdram"
    dst_name = "x_out" if i == len(P.blocks) - 1 else "xdram"
    src, srcb = P.d(src_name)
    dst, dstb = P.d(dst_name)
    sdram, sdramb = P.d("sdram")
    ebuf, ebufb = P.d("ebuf")
    ybuf, ybufb = P.d("ybuf")

    for w in ("w_l0f_qw", "w_l0f_kw", "w_l0f_vw", "w_l0f_ow", "w_l0f_gate"):
        P.precast(w)
    P.precast("w_l0f_w1", 8)
    P.precast("w_l0f_w2", 8)
    v_qb = P.load_vec("v_l0f_qb")
    v_kb = P.load_vec("v_l0f_kb")
    v_vb = P.load_vec("v_l0f_vb")
    v_ob = P.load_vec("v_l0f_ob")
    v_n1 = P.load_vec("v_l0f_n1")
    v_n2 = P.load_vec("v_l0f_n2")

    P.stats_pass(src_name, "ln", [0, 1])

    vt, vtb = P.vt, P.vtb
    cs1, cs1b = P.csb["onesc1"]

    for s in range(2):
        w_q = P.load_w("w_l0f_qw", 0)
        w_kk = P.load_w("w_l0f_kw", 8192)
        w_v = P.load_w("w_l0f_vw", 10240)
        ssl = slice(s * SEQ, (s + 1) * SEQ)
        x_in, x_inb = P.abig, P.abigb
        for lt in range(TPS):
            tt = s * TPS + lt
            sl = slice(tt * TT, (tt + 1) * TT)
            rws = P.use_rows([0, 1], tt)
            xbt, xbtb = P.s1b, P.s1bb
            for c in range(NCH):
                k.dma("g", xbt[:, c * TT:(c + 1) * TT],
                      src[c * 128:(c + 1) * 128, sl], ins=(srcb,), **gw(xbtb, c, NCH))
            P.bcast(rws[0][0], rws[0][1], P.pb1, P.pb1b, TT)
            P.bcast(rws[1][0], rws[1][1], P.pb2, P.pb2b, TT)
            for c in range(NCH):
                csl = slice(c * TT, (c + 1) * TT)
                osl = slice(c * SEQ + lt * TT, c * SEQ + (lt + 1) * TT)
                k.op("vector", "tensor_tensor", x_in[:, osl], xbt[:, csl],
                     P.pb1[:, 0:TT], ALU.mult, ins=(xbtb, P.pb1b), outs_war=(x_inb,))
                k.op("vector", "tensor_tensor", x_in[:, osl], x_in[:, osl],
                     P.pb2[:, 0:TT], ALU.add, ins=(P.pb2b,),
                     finals=(x_inb,) if (lt == TPS - 1 and c == NCH - 1) else ())
        ksb, ksbb = P.s1b, P.s1bb
        vsb, vsbb = P.s2b, P.s2bb
        for mc in range(2):
            for lt in range(TPS):
                pt_, pb_ = P.bank()
                k.mm(pb_, pt_[:, 0:TT],
                     [(w_kk(kc, mc), x_in[:, kc * SEQ + lt * TT: kc * SEQ + (lt + 1) * TT],
                       [P.wslabb, x_inb]) for kc in range(NCH)])
                k.op("vector", "tensor_scalar",
                     ksb[:, mc * SEQ + lt * TT: mc * SEQ + (lt + 1) * TT],
                     pt_[:, 0:TT], v_kb[0][:, mc:mc + 1], None, ALU.add,
                     ins=(pb_, v_kb[1]), **gw(ksbb, mc * TPS + lt, 2 * TPS))
                pt_, pb_ = P.bank()
                k.mm(pb_, pt_[:, 0:TT],
                     [(w_v(kc, mc), x_in[:, kc * SEQ + lt * TT: kc * SEQ + (lt + 1) * TT],
                       [P.wslabb, x_inb]) for kc in range(NCH)])
                k.op("vector", "tensor_scalar",
                     vsb[:, mc * SEQ + lt * TT: mc * SEQ + (lt + 1) * TT],
                     pt_[:, 0:TT], v_vb[0][:, mc:mc + 1], None, ALU.add,
                     ins=(pb_, v_vb[1]), **gw(vsbb, mc * TPS + lt, 2 * TPS))
        k.op("vector", "memset", vt[:], 1.0, outs=(vtb,))
        idt, idtb = P.csb["ident"]
        for kvh in range(4):
            vrow = vsb[(kvh % 2) * 64:(kvh % 2) * 64 + 64, :]
            for g2 in range(2):
                for j in range(8):
                    kt = g2 * 8 + j
                    k.op("tensor", "transpose", P.pt[:, j * 128:j * 128 + 64],
                         vsb[(kvh % 2) * 64:(kvh % 2) * 64 + 64,
                             (kvh // 2) * SEQ + kt * 128:(kvh // 2) * SEQ + (kt + 1) * 128],
                         idt[(kvh % 2) * 64:(kvh % 2) * 64 + 64,
                             (kvh % 2) * 64:(kvh % 2) * 64 + 64], ins=(vsbb, idtb),
                         outs_war=(P.ptb,) if j == 0 else (),
                         finals=(P.ptb,) if j == 7 else ())
                k.op("vector", "tensor_copy",
                     vt[:, (kvh * 16 + g2 * 8) * 65:(kvh * 16 + g2 * 8) * 65 + 8 * 65].rearrange(
                         "p (t c) -> p t c", c=65)[:, :, 0:64],
                     P.pt[:, 0:1024].rearrange("p (t c) -> p t c", c=128)[:, :, 0:64],
                     ins=(P.ptb,), outs_war=(vtb,),
                     finals=(vtb,) if (kvh == 3 and g2 == 1) else ())
        asb, asbb = P.bbig, P.bbigb
        qh = P.smb_t
        qhb = P.smb_b
        at0 = P.s2b
        at0b = P.s2bb
        PAIRS = [(0, 4), (1, 5), (2, 6), (3, 7), (8, 12), (9, 13), (10, 14), (11, 15)]
        for j, (ha, hb) in enumerate(PAIRS):
            for lt in range(TPS):
                pt_, pb_ = P.bank()
                k.mm(pb_, pt_[:, 0:TT],
                     [(w_q(kc, mc=j),
                       x_in[:, kc * SEQ + lt * TT: kc * SEQ + (lt + 1) * TT],
                       [P.wslabb, x_inb]) for kc in range(NCH)])
                k.op("vector", "tensor_scalar", qh[:, lt * TT:(lt + 1) * TT],
                     pt_[:, 0:TT], v_qb[0][:, j:j + 1], None, ALU.add,
                     ins=(pb_, v_qb[1]), **gw(qhb, lt, TPS))
            for hh, h in enumerate((ha, hb)):
                qb0 = hh * 64
                kvh = h // 4
                assert (kvh % 2) == hh
                for qc in range(TPS):
                    qsl = slice(qc * TT, (qc + 1) * TT)
                    avb = P.pb2b
                    sbanks = {}

                    def score(kt):
                        pt_, pb_ = P.bank()
                        k.mm(pb_, pt_[:, 0:TT],
                             [(ksb[qb0:qb0 + 64,
                                   (kvh // 2) * SEQ + kt * 128:(kvh // 2) * SEQ + (kt + 1) * 128],
                               qh[qb0:qb0 + 64, qsl], [ksbb, qhb])])
                        sbanks[kt] = (pt_, pb_)
                    score(0)
                    for kt in range(16):
                        if kt + 1 < 16:
                            score(kt + 1)
                        pt_, pb_ = sbanks.pop(kt)
                        atile = at0[:, (kt % 2) * TT:(kt % 2 + 1) * TT]
                        k.op("scalar", "activation", atile, pt_[0:128, 0:TT], AF.Exp,
                             scale=0.125, ins=(pb_,), outs=(at0b,))
                        k.op("tensor", "matmul", P.pb2[0:65, 0:TT],
                             vt[:, (kvh * 16 + kt) * 65:(kvh * 16 + kt + 1) * 65],
                             atile, start=(kt == 0), stop=(kt == 15),
                             ins=(vtb, at0b),
                             outs_war=(avb,) if kt == 0 else (),
                             finals=(avb,) if kt == 15 else ())
                    avs = P.sm[:, 0:TT]
                    k.op("scalar", "copy", avs[0:65, :], P.pb2[0:65, 0:TT],
                         ins=(avb,), outs=(P.smb,))
                    rrow = P.sm[0:1, TT:2 * TT]
                    k.op("vector", "reciprocal", rrow, avs[64:65, :], ins=(P.smb,), outs=(P.smb,))
                    hi = P.rbf[0:1, 0:TT]
                    lo = P.rbf[0:1, TT:2 * TT]
                    k.op("vector", "tensor_copy", hi, rrow, ins=(P.smb,), outs=(P.rbfb,))
                    k.op("vector", "tensor_tensor", lo, rrow, hi, ALU.subtract,
                         ins=(P.smb, P.rbfb), outs=(P.rbfb,))
                    P.bcast(hi, lo, P.pb1, P.pb1b, TT)
                    dst_sl = asb[(h % 2) * 64:(h % 2) * 64 + 64,
                                 (h // 2) * SEQ + qc * TT:(h // 2) * SEQ + (qc + 1) * TT]
                    fin = dict(outs_war=(asbb,),
                               finals=(asbb,) if (j == 7 and hh == 1 and qc == TPS - 1) else ())
                    if h % 2 == 0:
                        k.op("vector", "tensor_tensor", dst_sl,
                             avs[0:64, :], P.pb1[0:64, 0:TT], ALU.mult,
                             ins=(P.smb, P.pb1b), **fin)
                    else:
                        tmp = at0[:, 1024:1024 + TT]
                        k.op("vector", "tensor_tensor", tmp[0:64, :],
                             avs[0:64, :], P.pb1[0:64, 0:TT], ALU.mult,
                             ins=(P.smb, P.pb1b), outs=(at0b,))
                        idtc, idtcb = P.csb["ident"]
                        prl, prlb = P.bank()
                        k.op("tensor", "matmul", prl[64:128, 0:TT],
                             idtc[0:64, 0:64], tmp[0:64, :], start=True, stop=True,
                             ins=(idtcb, at0b), outs=(prlb,))
                        k.op("vector", "tensor_copy", dst_sl, prl[64:128, 0:TT],
                             ins=(prlb,), **fin)
        if DBG and s == 1:
            for nm, t_, b_ in [("vt", vt, vtb), ("asb", P.bbig, P.bbigb),
                               ("ksb", P.s1b, P.s1bb)]:
                dd = P.nc.dram_tensor("dbg_" + nm, list(t_[:].shape), t_[:].dtype,
                                      kind="ExternalOutput")
                P.k.dma("c", dd.ap(), t_[:], ins=(b_,), outs=(Buf("d" + nm),))
        w_o = P.load_w("w_l0f_ow", 0)
        if DBG and s == 1:
            dws = P.nc.dram_tensor("dbg_wslab", [128, 8192], BF16, kind="ExternalOutput")
            P.k.dma("c", dws.ap(), P.wslab[:, 0:8192], ins=(P.wslabb,), outs=(Buf("dw"),))
        s1t, s1tb = P.f1, P.f1b
        sq = P.smb_t
        sqb = P.smb_b
        for lt in range(TPS):
            tt = s * TPS + lt
            sl = slice(tt * TT, (tt + 1) * TT)
            p2, p2b_ = P.pb2, P.pb2b
            for mc in range(NCH):
                pt_, pb_ = P.bank()
                k.mm(pb_, pt_[:, 0:TT],
                     [(w_o(kc, mc), asb[:, kc * SEQ + lt * TT: kc * SEQ + (lt + 1) * TT],
                       [P.wslabb, asbb]) for kc in range(NCH)])
                csl = slice(mc * TT, (mc + 1) * TT)
                k.op("vector", "tensor_scalar", s1t[:, csl], pt_[:, 0:TT],
                     v_ob[0][:, mc:mc + 1], None, ALU.add, ins=(pb_, v_ob[1]),
                     outs_war=(s1tb,))
                k.op("vector", "tensor_tensor", s1t[:, csl], s1t[:, csl],
                     x_in[:, mc * SEQ + lt * TT: mc * SEQ + (lt + 1) * TT],
                     ALU.add, ins=(x_inb,), finals=(s1tb,) if mc == NCH - 1 else ())
            for mc in range(NCH):
                csl = slice(mc * TT, (mc + 1) * TT)
                k.dma("s", sdram[mc * 128:(mc + 1) * 128, sl], s1t[:, csl],
                      ins=(s1tb,), **gw(sdramb, (s * TPS + lt) * NCH + mc, NTT * NCH))
                k.op("scalar", "activation", sq[:, 0:TT], s1t[:, csl], AF.Square,
                     ins=(s1tb,), outs=(sqb,))
                k.op("tensor", "matmul", p2[0:1, 0:TT], cs1[:, 0:1], sq[:, 0:TT],
                     start=(mc == 0), stop=(mc == NCH - 1), ins=(cs1b, sqb),
                     outs_war=(p2b_,) if mc == 0 else (),
                     finals=(p2b_,) if mc == NCH - 1 else ())
            rd, rdb = P.d("rowsd")
            r2row = P.sm[0:1, 0:TT]
            k.op("scalar", "activation", r2row, p2[0:1, 0:TT], AF.Ln,
                 bias=np.float32(1e-6), scale=1.0 / D, ins=(p2b_,), outs=(P.smb,))
            k.op("scalar", "activation", r2row, r2row, AF.Exp, scale=-0.5,
                 ins=(P.smb,), outs=(P.smb,))
            k.dma("s", rd[5:6, sl], r2row, ins=(P.smb,), outs=(rdb,))

    P.dbg_dump("s1", "sdram")
    build_l0f_moe(P, dst, dstb)


def build_l0f_moe(P, dst, dstb):
    k = P.k
    sdram, sdramb = P.d("sdram")
    ebuf, ebufb = P.d("ebuf")
    ybuf, ybufb = P.d("ybuf")
    cc_in, cc_inb = P.d("cc_in")
    cc_out, cc_outb = P.d("cc_out")
    cmask, cmaskb = P.d("core_mask")
    rd, rdb = P.d("rowsd")
    v_n1 = P.load_vec("v_l0f_n1")
    v_n2 = P.load_vec("v_l0f_n2")
    w_g = P.load_w("w_l0f_gate", 0)
    cs1, cs1b = P.csb["onesc1"]
    us, usb = P.csb["ustrict"]
    io8, io8b = P.csb["iota8"]
    io8B, io8Bb = P.csb["iota8B"]
    idt, idtb = P.csb["ident"]
    ones1, ones1b = P.csb["ones1"]

    meta, metab = k.sb("l0f_meta", [128, 6 * 32], F32)
    sidx, sidxb = k.sb("l0f_sidx", [128, 64], I32)
    run_, runb = k.sb("l0f_run", [4, 16], F32)
    runbf, runbfb = k.sb("l0f_runbf", [1, 32], BF16)

    k.op("vector", "memset", run_[0:1, 0:8], 0.0, outs=(runb,))
    k.op("vector", "memset", runbf[:], 0.0, outs=(runbfb,))

    for tt in range(NTT):
        sl = slice(tt * TT, (tt + 1) * TT)
        s1t, s1tb = P.f1, P.f1b
        for c in range(NCH):
            k.dma("g", s1t[:, c * TT:(c + 1) * TT], sdram[c * 128:(c + 1) * 128, sl],
                  ins=(sdramb,), **gw(s1tb, c, NCH))
        rws = P.use_rows([5], tt)
        P.bcast(rws[0][0], rws[0][1], P.pb1, P.pb1b, TT)
        s2t, s2tb = P.s1b, P.s1bb
        for c in range(NCH):
            csl = slice(c * TT, (c + 1) * TT)
            k.op("vector", "tensor_tensor", s2t[:, csl], s1t[:, csl],
                 P.pb1[:, 0:TT], ALU.mult, ins=(s1tb, P.pb1b), outs_war=(s2tb,))
            k.op("vector", "tensor_scalar", s2t[:, csl], s2t[:, csl],
                 v_n1[0][:, c:c + 1], None, ALU.mult, ins=(v_n1[1],),
                 finals=(s2tb,) if c == NCH - 1 else ())
        pg, pgb = P.bank()
        k.mm(pgb, pg[0:8, 0:TT],
             [(w_g(kc, mo=0, width=8), s2t[:, kc * TT:(kc + 1) * TT], [P.wslabb, s2tb])
              for kc in range(NCH)])
        probs = P.sm[:, 0:TT]
        k.op("scalar", "activation", probs[0:8, :], pg[0:8, 0:TT], AF.Sigmoid,
             ins=(pgb,), outs=(P.smb,))
        for st in range(4):
            stt = tt * 4 + st
            pts, ptsb = P.bank()
            idf, idfb = P.csb["identf"]
            k.op("tensor", "transpose", pts[:, 0:8],
                 probs[0:8, st * 128:(st + 1) * 128], idf[0:8, 0:8], ins=(P.smb, idfb),
                 outs=(ptsb,))
            pm = P.sm[:, TT:TT + 8]
            k.op("vector", "tensor_copy", pm, pts[:, 0:8], ins=(ptsb,), outs=(P.smb,))
            w1c = P.sm[:, TT + 8:TT + 9]; w2c = P.sm[:, TT + 9:TT + 10]
            i1c = P.sm[:, TT + 10:TT + 11]; i2c = P.sm[:, TT + 11:TT + 12]
            em = P.sm[:, TT + 16:TT + 24]
            em2 = P.sm[:, TT + 24:TT + 32]
            emsb = P.smb_t[:, 0:8]
            k.op("vector", "tensor_reduce", w1c, pm, mybir.AxisListType.X, ALU.max,
                 ins=(P.smb,), outs=(P.smb,))
            k.op("vector", "tensor_scalar", em, pm, w1c, None, ALU.is_equal,
                 ins=(P.smb,), outs=(P.smb,))
            if DBG and tt == 0 and st == 0:
                dbe2 = P.nc.dram_tensor("dbg_e2", [128, 24], F32, kind="ExternalOutput")
                dt2 = P.k.sb("dbg_e2t", [128, 24], F32)
                k.op("vector", "tensor_copy", dt2[0][:, 0:8], pm, ins=(P.smb,), outs=(dt2[1],))
                k.op("vector", "tensor_copy", dt2[0][:, 8:16], em, ins=(P.smb,), outs=(dt2[1],))
                k.op("vector", "tensor_copy", dt2[0][:, 16:17], w1c, ins=(P.smb,), outs=(dt2[1],))
                k.dma("c", dbe2.ap(), dt2[0][:], ins=(dt2[1],), outs=(Buf("dbe2"),))
            k.op("vector", "scalar_tensor_tensor", em2, em, -65536.0, io8B[:],
                 ALU.mult, ALU.add, ins=(P.smb, io8Bb), outs=(P.smb,))
            k.op("vector", "tensor_reduce", i1c, em2, mybir.AxisListType.X, ALU.min,
                 ins=(P.smb,), outs=(P.smb,))
            k.op("vector", "tensor_scalar", em, io8[:], i1c, None, ALU.is_equal,
                 ins=(P.smb, io8b), outs=(P.smb,))
            k.op("vector", "scalar_tensor_tensor", em2, em, -BIG, pm,
                 ALU.mult, ALU.add, ins=(P.smb,), outs=(P.smb,))
            k.op("vector", "tensor_reduce", w2c, em2, mybir.AxisListType.X, ALU.max,
                 ins=(P.smb,), outs=(P.smb,))
            k.op("vector", "tensor_scalar", em2, em2, w2c, None, ALU.is_equal,
                 ins=(P.smb,), outs=(P.smb,))
            k.op("vector", "scalar_tensor_tensor", em2, em2, -65536.0, io8B[:],
                 ALU.mult, ALU.add, ins=(P.smb, io8Bb), outs=(P.smb,))
            k.op("vector", "tensor_reduce", i2c, em2, mybir.AxisListType.X, ALU.min,
                 ins=(P.smb,), outs=(P.smb,))
            k.op("vector", "tensor_scalar", em2, io8[:], i2c, None, ALU.is_equal,
                 ins=(P.smb, io8b), outs=(P.smb,))
            wsum = P.sm[:, TT + 12:TT + 13]
            k.op("vector", "tensor_scalar", wsum, w1c, 1e-6, None, ALU.add,
                 ins=(P.smb,), outs=(P.smb,))
            k.op("vector", "tensor_tensor", wsum, wsum, w2c, ALU.add,
                 ins=(P.smb,), outs=(P.smb,))
            k.op("vector", "reciprocal", wsum, wsum, ins=(P.smb,), outs=(P.smb,))
            k.op("vector", "tensor_tensor", w1c, w1c, wsum, ALU.mult,
                 ins=(P.smb,), outs=(P.smb,))
            k.op("vector", "tensor_tensor", w2c, w2c, wsum, ALU.mult,
                 ins=(P.smb,), outs=(P.smb,))
            k.op("vector", "tensor_tensor", em, em, em2, ALU.add,
                 ins=(P.smb,), outs=(P.smb,))
            k.op("vector", "tensor_copy", emsb, em, ins=(P.smb,), outs=(P.smb_b,))
            ppre, ppreb = P.bank()
            k.mm(ppreb, ppre[:, 0:8],
                 [(us[:], emsb, [usb, P.smb_b]),
                  (ones1[0:1, :], runbf[0:1, 0:8], [ones1b, runbfb]),
                  (ones1[0:1, :], runbf[0:1, 8:16], [ones1b, runbfb])])
            pct, pctb = P.bank()
            k.mm(pctb, pct[0:1, 0:8], [(cs1[:, 0:1], emsb, [cs1b, P.smb_b])])
            k.op("vector", "tensor_scalar", em, io8[:], i1c, None, ALU.is_equal,
                 ins=(P.smb, io8b), outs=(P.smb,))
            tdot = P.sm[:, TT + 32:TT + 40]
            k.op("vector", "tensor_tensor", tdot, ppre[:, 0:8], em, ALU.mult,
                 ins=(ppreb, P.smb), outs=(P.smb,))
            lr1 = meta[:, stt:stt + 1]
            k.op("vector", "tensor_reduce", lr1, tdot, mybir.AxisListType.X, ALU.add,
                 ins=(P.smb,), outs=(metab,))
            k.op("vector", "tensor_scalar", em2, io8[:], i2c, None, ALU.is_equal,
                 ins=(P.smb, io8b), outs=(P.smb,))
            k.op("vector", "tensor_tensor", tdot, ppre[:, 0:8], em2, ALU.mult,
                 ins=(ppreb, P.smb), outs=(P.smb,))
            lr2 = meta[:, 32 + stt:32 + stt + 1]
            k.op("vector", "tensor_reduce", lr2, tdot, mybir.AxisListType.X, ALU.add,
                 ins=(P.smb,), outs=(metab,))
            k.op("vector", "tensor_copy", meta[:, 64 + stt:64 + stt + 1], i1c,
                 ins=(P.smb,), outs=(metab,))
            k.op("vector", "tensor_copy", meta[:, 96 + stt:96 + stt + 1], i2c,
                 ins=(P.smb,), outs=(metab,))
            k.op("vector", "tensor_copy", meta[:, 128 + stt:128 + stt + 1], w1c,
                 ins=(P.smb,), outs=(metab,))
            k.op("vector", "tensor_copy", meta[:, 160 + stt:160 + stt + 1], w2c,
                 ins=(P.smb,), outs=(metab,))
            if DBG and tt == 0 and st == 0:
                P._dbe = P.nc.dram_tensor("dbg_e1", [128, 64], F32, kind="ExternalOutput")
                P._dt = P.k.sb("dbg_e1t", [128, 64], F32)
                dt_ = P._dt
                k.op("vector", "tensor_copy", dt_[0][:, 0:8], pm, ins=(P.smb,), outs=(dt_[1],))
                k.op("vector", "tensor_copy", dt_[0][:, 8:16], em, ins=(P.smb,), outs=(dt_[1],))
                k.op("vector", "tensor_copy", dt_[0][:, 16:24], em2, ins=(P.smb,), outs=(dt_[1],))
                k.op("vector", "tensor_copy", dt_[0][:, 24:32], ppre[:, 0:8], ins=(ppreb,), outs=(dt_[1],))
                k.op("vector", "tensor_copy", dt_[0][0:1, 32:40], pct[0:1, 0:8], ins=(pctb,), outs=(dt_[1],))
            if DBG and tt == 0 and st == 1:
                dt_ = P._dt
                k.op("vector", "tensor_copy", dt_[0][0:1, 40:48], run_[0:1, 0:8],
                     ins=(runb,), outs=(dt_[1],))
                k.op("vector", "tensor_copy", dt_[0][0:1, 48:56], pct[0:1, 0:8],
                     ins=(pctb,), outs=(dt_[1],))
            k.op("vector", "tensor_tensor", run_[0:1, 0:8], run_[0:1, 0:8],
                 pct[0:1, 0:8], ALU.add, ins=(pctb, runb), outs=(runb,))
            k.op("vector", "tensor_copy", runbf[0:1, 0:8], run_[0:1, 0:8],
                 ins=(runb,), outs=(runbfb,))
            k.op("vector", "tensor_tensor", runbf[0:1, 8:16], run_[0:1, 0:8],
                 runbf[0:1, 0:8], ALU.subtract, ins=(runb, runbfb), outs=(runbfb,))
            for kk2, (ic, lrc, col) in enumerate(
                    [(i1c, lr1, 0), (i2c, lr2, 1)]):
                slc = P.sm[:, TT + 40 + kk2:TT + 41 + kk2]
                inb = P.sm[:, TT + 42 + kk2:TT + 43 + kk2]
                k.op("vector", "scalar_tensor_tensor", slc, ic, float(ECAP), lrc,
                     ALU.mult, ALU.add, ins=(P.smb, metab), outs=(P.smb,))
                k.op("vector", "tensor_scalar", inb, lrc, float(ECAP), None,
                     ALU.is_lt, ins=(metab,), outs=(P.smb,))
                k.op("vector", "tensor_scalar", slc, slc, OOBIDX, None,
                     ALU.subtract, ins=(P.smb,), outs=(P.smb,))
                k.op("vector", "tensor_tensor", slc, slc, inb, ALU.mult,
                     ins=(P.smb,), outs=(P.smb,))
                k.op("vector", "tensor_scalar", slc, slc, OOBIDX, None,
                     ALU.add, ins=(P.smb,), outs=(P.smb,))
                k.op("vector", "tensor_copy", sidx[:, stt * 2 + kk2:stt * 2 + kk2 + 1],
                     slc, ins=(P.smb,), outs=(sidxb,))
            for c in range(NCH):
                k.op("tensor", "transpose", P.pt[:, c * 128:(c + 1) * 128],
                     s2t[:, c * TT + st * 128: c * TT + (st + 1) * 128], idt[:],
                     ins=(s2tb, idtb), **gw(P.ptb, c, NCH))
            stm = P.smb_t[:, 0:1024]
            k.op("vector", "tensor_copy", stm, P.pt[:, 0:1024], ins=(P.ptb,),
                 outs=(P.smb_b,))
            for kk2 in range(2):
                k.idma(ebuf[:],
                       IndirectOffsetOnAxis(ap=sidx[:, stt * 2 + kk2:stt * 2 + kk2 + 1], axis=0),
                       stm, None, 8 * ECAP - 1,
                       ins=(P.smb_b, sidxb), **gw(ebufb, stt * 2 + kk2, 64))
    if DBG:
        k.op("vector", "tensor_copy", P._dt[0][0:1, 56:64], run_[0:1, 0:8],
             ins=(runb,), outs=(P._dt[1],))
        k.dma("c", P._dbe.ap(), P._dt[0][:], ins=(P._dt[1],), outs=(Buf("dbe"),))
    k.dma("s", cc_in[:], run_[0:1, 0:8], ins=(runb,), outs=(cc_inb,))
    k.op("gpsimd", "collective_compute", "AllGather", ALU.bypass,
         [[0, 1, 2, 3, 4, 5, 6, 7]], [cc_in.opt()], [cc_out.opt()],
         sem="cc", inc=1, ins=(cc_inb,), outs=(cc_outb,))
    call, callb = k.sb("l0f_call", [8, 8], F32)
    k.dma("s", call[:], cc_out[:], ins=(cc_outb,), outs=(callb,))
    cbf, cbfb = k.sb("l0f_cbf", [8, 17], BF16)
    k.op("vector", "tensor_copy", cbf[0:8, 0:8], call[:], ins=(callb,), outs=(cbfb,))
    k.op("vector", "tensor_tensor", call[:], call[:], cbf[0:8, 0:8], ALU.subtract,
         ins=(callb, cbfb), outs=(callb,))
    k.op("vector", "tensor_copy", cbf[0:8, 8:16], call[:], ins=(callb,), outs=(cbfb,))
    cmb, cmbb = k.sb("l0f_cmb", [8, 1], BF16)
    cmf, cmfb = k.sb("l0f_cmf", [8, 1], F32)
    k.dma("s", cmf[:], cmask[:], ins=(cmaskb,), outs=(cmfb,))
    k.op("vector", "tensor_copy", cmb[:], cmf[:], ins=(cmfb,), outs=(cmbb,))
    pco, pcob = P.bank()
    k.mm(pcob, pco[0:1, 0:8],
         [(cmb[:], cbf[0:8, 0:8], [cmbb, cbfb]),
          (cmb[:], cbf[0:8, 8:16], [cmbb, cbfb])])
    coffr = P.sm[0:1, 0:8]
    k.op("vector", "tensor_copy", coffr, pco[0:1, 0:8], ins=(pcob,), outs=(P.smb,))
    chi = P.rbf[0:1, 0:8]; clo = P.rbf[0:1, 8:16]
    k.op("vector", "tensor_copy", chi, coffr, ins=(P.smb,), outs=(P.rbfb,))
    k.op("vector", "tensor_tensor", clo, coffr, chi, ALU.subtract,
         ins=(P.smb, P.rbfb), outs=(P.rbfb,))
    P.bcast(chi, clo, P.pb1, P.pb1b, 8)
    cofft, cofftb = k.sb("l0f_cofft", [128, 8], F32)
    k.op("vector", "tensor_copy", cofft[:], P.pb1[:, 0:8], ins=(P.pb1b,), outs=(cofftb,))
    io8_t, io8_tb = P.csb["iota8"]
    for stt in range(32):
        em = P.sm[:, 8:16]
        td = P.sm[:, 16:24]
        for kk2 in range(2):
            ic = meta[:, 64 + 32 * kk2 + stt:64 + 32 * kk2 + stt + 1]
            lrc = meta[:, 32 * kk2 + stt:32 * kk2 + stt + 1]
            wnc = meta[:, 128 + 32 * kk2 + stt:128 + 32 * kk2 + stt + 1]
            k.op("vector", "tensor_scalar", em, io8_t[:], ic, None, ALU.is_equal,
                 ins=(io8_tb, metab), outs=(P.smb,))
            k.op("vector", "tensor_tensor", td, em, cofft[:], ALU.mult,
                 ins=(P.smb, cofftb), outs=(P.smb,))
            gr = P.sm[:, 24:25]
            k.op("vector", "tensor_reduce", gr, td, mybir.AxisListType.X, ALU.add,
                 ins=(P.smb,), outs=(P.smb,))
            k.op("vector", "tensor_tensor", gr, gr, lrc, ALU.add,
                 ins=(P.smb, metab), outs=(P.smb,))
            k.op("vector", "tensor_scalar", gr, gr, float(GCAP), None, ALU.is_lt,
                 ins=(P.smb,), outs=(P.smb,))
            k.op("vector", "tensor_tensor", wnc, wnc, gr, ALU.mult,
                 ins=(P.smb, metab), outs=(metab,))
            k.op("vector", "tensor_scalar", gr, lrc, float(ECAP), None, ALU.is_lt,
                 ins=(metab,), outs=(P.smb,))
            k.op("vector", "tensor_tensor", wnc, wnc, gr, ALU.mult,
                 ins=(P.smb, metab), outs=(metab,))
    h_sb, h_sbb = P.s1b, P.s1bb
    for e in range(8):
        for half in range(2):
            w2h = P.load_w("w_l0f_w2", 0, expert=e, koff=half * 1024, kdim=1024)
            w1h = P.load_w("w_l0f_w1", 8192, expert=e, moff=half * 1024, mdim=1024)
            for sc_ in range(4):
                xe, xeb = P.abig, P.abigb
                for c in range(NCH):
                    k.dma("s", xe[:, c * TT:(c + 1) * TT],
                          ebuf[e * ECAP + sc_ * TT:e * ECAP + (sc_ + 1) * TT,
                               c * 128:(c + 1) * 128],
                          transpose=True, ins=(ebufb,), **gw(xeb, c, NCH))
                for mc in range(NCH):
                    pt_, pb_ = P.bank()
                    k.mm(pb_, pt_[:, 0:TT],
                         [(w1h(kc, mc), xe[:, kc * TT:(kc + 1) * TT],
                           [P.wslabb, xeb]) for kc in range(NCH)])
                    k.op("scalar", "activation", h_sb[:, mc * TT:(mc + 1) * TT],
                         pt_[:, 0:TT], AF.Gelu, ins=(pb_,), **gw(h_sbb, mc, NCH))
                for st2 in range(4):
                    ytmp = P.sm[:, 0:1024]
                    for n2 in range(2):
                        pt_, pb_ = P.bank()
                        k.mm(pb_, pt_[:, 0:TT],
                             [(h_sb[:, hc * TT + st2 * 128: hc * TT + (st2 + 1) * 128],
                               w2h(hc, mo=n2 * 512, width=512),
                               [h_sbb, P.wslabb]) for hc in range(NCH)])
                        k.op("scalar", "copy", ytmp[:, n2 * 512:(n2 + 1) * 512],
                             pt_[:, 0:TT], ins=(pb_,),
                             outs_war=(P.smb,), finals=(P.smb,) if n2 == 1 else ())
                    rows = ybuf[e * ECAP + sc_ * TT + st2 * 128:
                                e * ECAP + sc_ * TT + (st2 + 1) * 128, :]
                    if half == 0:
                        k.dma("s", rows, ytmp, ins=(P.smb,),
                              **gw(ybufb, ((e * 2 + half) * 4 + sc_) * 4 + st2, 256))
                    else:
                        k.dma("g", rows, ytmp, accum_op=ALU.add, ins=(P.smb,),
                              **gw(ybufb, ((e * 2 + half) * 4 + sc_) * 4 + st2, 256))
    P.dbg_dump("ebuf", "ebuf")
    P.dbg_dump("ybuf", "ybuf")
    if DBG:
        mdump = P.nc.dram_tensor("dbg_meta", [128, 192], F32, kind="ExternalOutput")
        P.k.dma("c", mdump.ap(), meta[:], ins=(metab,), outs=(Buf("dm"),))
        sdump = P.nc.dram_tensor("dbg_sidx", [128, 64], I32, kind="ExternalOutput")
        P.k.dma("c", sdump.ap(), sidx[:], ins=(sidxb,), outs=(Buf("ds"),))
        cdump = P.nc.dram_tensor("dbg_call", [8, 8], F32, kind="ExternalOutput")
        P.k.dma("c", cdump.ap(), call[:], ins=(callb,), outs=(Buf("dc"),))
    for tt in range(NTT):
        sl = slice(tt * TT, (tt + 1) * TT)
        s1t, s1tb = P.f1, P.f1b
        for c in range(NCH):
            k.dma("g", s1t[:, c * TT:(c + 1) * TT], sdram[c * 128:(c + 1) * 128, sl],
                  ins=(sdramb,), **gw(s1tb, c, NCH))
        rws = P.use_rows([5], tt)
        P.bcast(rws[0][0], rws[0][1], P.pb1, P.pb1b, TT)
        for c in range(NCH):
            csl = slice(c * TT, (c + 1) * TT)
            k.op("vector", "tensor_tensor", s1t[:, csl], s1t[:, csl],
                 P.pb1[:, 0:TT], ALU.mult, ins=(P.pb1b,), outs_war=(s1tb,))
            k.op("vector", "tensor_scalar", s1t[:, csl], s1t[:, csl],
                 v_n1[0][:, c:c + 1], None, ALU.mult, ins=(v_n1[1],),
                 finals=(s1tb,) if c == NCH - 1 else ())
        for st in range(4):
            stt = tt * 4 + st
            g1 = P.sm[:, 0:1024]
            g2 = P.sm[:, 1024:2048]
            k.idma(g1, None, ybuf[:],
                   IndirectOffsetOnAxis(ap=sidx[:, stt * 2:stt * 2 + 1], axis=0),
                   8 * ECAP - 1, ins=(ybufb, sidxb), outs=(P.smb,))
            k.idma(g2, None, ybuf[:],
                   IndirectOffsetOnAxis(ap=sidx[:, stt * 2 + 1:stt * 2 + 2], axis=0),
                   8 * ECAP - 1, ins=(ybufb, sidxb), outs=(P.smb,))
            k.op("vector", "tensor_scalar", g1, g1,
                 meta[:, 128 + stt:128 + stt + 1], None, ALU.mult,
                 ins=(metab,), outs=(P.smb,))
            k.op("vector", "scalar_tensor_tensor", g1, g2,
                 meta[:, 160 + stt:160 + stt + 1], g1, ALU.mult, ALU.add,
                 ins=(metab,), outs=(P.smb,))
            mb = P.smb_t[:, 0:1024]
            k.op("vector", "tensor_copy", mb, g1, ins=(P.smb,), outs=(P.smb_b,))
            idt2, idt2b = P.csb["ident"]
            for c in range(NCH):
                k.op("tensor", "transpose", P.pt[:, c * 128:(c + 1) * 128],
                     mb[:, c * 128:(c + 1) * 128], idt2[:], ins=(P.smb_b, idt2b),
                     **gw(P.ptb, c, NCH))
            for c in range(NCH):
                k.op("vector", "tensor_tensor",
                     s1t[:, c * TT + st * 128: c * TT + (st + 1) * 128],
                     s1t[:, c * TT + st * 128: c * TT + (st + 1) * 128],
                     P.pt[:, c * 128:(c + 1) * 128], ALU.add,
                     ins=(P.ptb,), outs=(s1tb,))
        sq = P.smb_t
        cs1_, cs1_b = P.csb["onesc1"]
        p2, p2b_ = P.pb2, P.pb2b
        for c in range(NCH):
            csl = slice(c * TT, (c + 1) * TT)
            k.dma("s", sdram[c * 128:(c + 1) * 128, sl], s1t[:, csl],
                  ins=(s1tb,), **gw(sdramb, tt * NCH + c, NTT * NCH))
            k.op("scalar", "activation", sq[:, 0:TT], s1t[:, csl], AF.Square,
                 ins=(s1tb,), outs=(P.smb_b,))
            k.op("tensor", "matmul", p2[0:1, 0:TT], cs1_[:, 0:1], sq[:, 0:TT],
                 start=(c == 0), stop=(c == NCH - 1), ins=(cs1_b, P.smb_b),
                 outs_war=(p2b_,) if c == 0 else (),
                 finals=(p2b_,) if c == NCH - 1 else ())
        r2row = P.sm[0:1, 0:TT]
        k.op("scalar", "activation", r2row, p2[0:1, 0:TT], AF.Ln,
             bias=np.float32(1e-6), scale=1.0 / D, ins=(p2b_,), outs=(P.smb,))
        k.op("scalar", "activation", r2row, r2row, AF.Exp, scale=-0.5,
             ins=(P.smb,), outs=(P.smb,))
        k.dma("s", rd[6:7, sl], r2row, ins=(P.smb,), outs=(rdb,))
    P.dbg_dump("s3", "sdram")
    src, srcb = P.d("x_in" if P.blocks.index("l0f") == 0 else "xdram")
    HT = 256
    for ht in range(RPC // HT):
        sl = slice(ht * HT, (ht + 1) * HT)
        s3t, s3tb = P.f1, P.f1b
        for c in range(NCH):
            k.dma("g", s3t[:, c * HT:(c + 1) * HT], sdram[c * 128:(c + 1) * 128, sl],
                  ins=(sdramb,), **gw(s3tb, c, 2 * NCH))
            k.dma("g", s3t[:, 2048 + c * HT:2048 + (c + 1) * HT],
                  src[c * 128:(c + 1) * 128, sl], ins=(srcb,),
                  **gw(s3tb, NCH + c, 2 * NCH))
        rd2, rd2b = P.d("rowsd")
        fsl = P.rsc[0:1, 0:HT]
        k.dma("s", fsl, rd2[6:7, sl], ins=(rd2b,), outs=(P.rscb,))
        rhi = P.rbf[0:1, 0:HT]
        rlo = P.rbf[0:1, HT:2 * HT]
        k.op("vector", "tensor_copy", rhi, fsl, ins=(P.rscb,), outs=(P.rbfb,))
        k.op("vector", "tensor_tensor", rlo, fsl, rhi, ALU.subtract,
             ins=(P.rscb, P.rbfb), outs=(P.rbfb,))
        P.bcast(rhi, rlo, P.pb1, P.pb1b, HT)
        res = P.sm[:, 0:HT]
        for c in range(NCH):
            k.op("vector", "tensor_tensor", res, s3t[:, c * HT:(c + 1) * HT],
                 P.pb1[:, 0:HT], ALU.mult, ins=(s3tb, P.pb1b), outs=(P.smb,))
            k.op("vector", "tensor_scalar", res, res, v_n2[0][:, c:c + 1], None,
                 ALU.mult, ins=(v_n2[1],), outs=(P.smb,))
            k.op("vector", "tensor_tensor", res, res,
                 s3t[:, 2048 + c * HT:2048 + (c + 1) * HT], ALU.add,
                 ins=(s3tb,), outs=(P.smb,))
            k.dma("s", dst[c * 128:(c + 1) * 128, sl], res, ins=(P.smb,),
                  **gw(dstb, ht * NCH + c, (RPC // HT) * NCH))




def build_l1a(P):
    k = P.k
    i = P.blocks.index("l1a")
    src, srcb = P.d("x_in" if i == 0 else "xdram")
    dst, dstb = P.d("x_out" if i == len(P.blocks) - 1 else "xdram")

    for w in ("w_l1a_qk", "w_l1a_v", "w_l1a_out"):
        P.precast(w)
    v_qkb = P.load_vec("v_l1a_qkb")
    v_vb = P.load_vec("v_l1a_vb2")
    v_gb = P.load_vec("v_l1a_gb")
    v_ob = P.load_vec("v_l1a_outb")

    P.stats_pass("x_in" if i == 0 else "xdram", "ln", [0, 1])
    vt, vtb = P.vt, P.vtb
    idt, idtb = P.csb["ident"]
    oh0, oh0b = P.csb["onesc_h0"]
    oh1, oh1b = P.csb["onesc_h1"]
    masks = {0: P.csb["m_diag"], 4: P.csb["m_win"], -1: P.csb["m_full"]}
    w_out = P.load_w("w_l1a_out", 2048)

    def hilo2(rA, rB, psum_t, psum_b, ins):
        hA = P.rbf[0:1, 0:TT]; lA = P.rbf[0:1, TT:2 * TT]
        hB = P.rbf[0:1, 2 * TT:3 * TT]; lB = P.rbf[0:1, 3 * TT:4 * TT]
        k.op("vector", "tensor_copy", hA, rA, ins=ins, outs=(P.rbfb,))
        k.op("vector", "tensor_tensor", lA, rA, hA, ALU.subtract,
             ins=tuple(ins) + (P.rbfb,), outs=(P.rbfb,))
        k.op("vector", "tensor_copy", hB, rB, ins=ins, outs=(P.rbfb,))
        k.op("vector", "tensor_tensor", lB, rB, hB, ALU.subtract,
             ins=tuple(ins) + (P.rbfb,), outs=(P.rbfb,))
        P.bcast(hA, lA, psum_t, psum_b, TT, pair2=(hB, lB))

    for s in range(2):
        x_in, x_inb = P.abig, P.abigb
        for lt in range(TPS):
            tt = s * TPS + lt
            sl = slice(tt * TT, (tt + 1) * TT)
            rws = P.use_rows([0, 1], tt)
            xbt, xbtb = P.s1b, P.s1bb
            for c in range(NCH):
                k.dma("g", xbt[:, c * TT:(c + 1) * TT],
                      src[c * 128:(c + 1) * 128, sl], ins=(srcb,), **gw(xbtb, c, NCH))
            P.bcast(rws[0][0], rws[0][1], P.pb1, P.pb1b, TT)
            P.bcast(rws[1][0], rws[1][1], P.pb2, P.pb2b, TT)
            for c in range(NCH):
                csl = slice(c * TT, (c + 1) * TT)
                osl = slice(c * SEQ + lt * TT, c * SEQ + (lt + 1) * TT)
                k.op("vector", "tensor_tensor", x_in[:, osl], xbt[:, csl],
                     P.pb1[:, 0:TT], ALU.mult, ins=(xbtb, P.pb1b), outs_war=(x_inb,))
                k.op("vector", "tensor_tensor", x_in[:, osl], x_in[:, osl],
                     P.pb2[:, 0:TT], ALU.add, ins=(P.pb2b,),
                     finals=(x_inb,) if (lt == TPS - 1 and c == NCH - 1) else ())
        asb, asbb = P.bbig, P.bbigb
        qkt, qktb = P.f1, P.f1b
        for h in range(16):
            w_qk = P.load_w("w_l1a_qk", 0, moff=h * 128, mdim=128)
            w_v = P.load_w("w_l1a_v", 1024, moff=h * 64, mdim=64)
            for lt in range(TPS):
                pt_, pb_ = P.bank()
                k.mm(pb_, pt_[:, 0:TT],
                     [(w_qk(kc), x_in[:, kc * SEQ + lt * TT: kc * SEQ + (lt + 1) * TT],
                       [P.wslabb, x_inb]) for kc in range(NCH)])
                k.op("vector", "tensor_scalar", qkt[:, lt * TT:(lt + 1) * TT],
                     pt_[:, 0:TT], v_qkb[0][:, h:h + 1], None, ALU.add,
                     ins=(pb_, v_qkb[1]), **gw(qktb, lt, TPS))
                pt_, pb_ = P.bank()
                k.mm(pb_, pt_[0:64, 0:TT],
                     [(w_v(kc, width=64), x_in[:, kc * SEQ + lt * TT: kc * SEQ + (lt + 1) * TT],
                       [P.wslabb, x_inb]) for kc in range(NCH)])
                k.op("vector", "tensor_scalar", P.s2b[0:64, lt * TT:(lt + 1) * TT],
                     pt_[0:64, 0:TT], v_vb[0][0:64, h:h + 1], None, ALU.add,
                     ins=(pb_, v_vb[1]), **gw(P.s2bb, lt, TPS))
            k.op("vector", "memset", vt[:, 0:16 * 65], 1.0, outs=(vtb,))
            for g2 in range(2):
                for j in range(8):
                    kt = g2 * 8 + j
                    k.op("tensor", "transpose", P.pt[:, j * 128:j * 128 + 64],
                         P.s2b[0:64, kt * 128:(kt + 1) * 128], idt[0:64, 0:64],
                         ins=(P.s2bb, idtb), **gw(P.ptb, j, 8))
                k.op("vector", "tensor_copy",
                     vt[:, (g2 * 8) * 65:(g2 * 8) * 65 + 8 * 65].rearrange(
                         "p (t c) -> p t c", c=65)[:, :, 0:64],
                     P.pt[:, 0:1024].rearrange("p (t c) -> p t c", c=128)[:, :, 0:64],
                     ins=(P.ptb,), outs_war=(vtb,), finals=(vtb,) if g2 == 1 else ())
            for ch in range(TPS):
                csl = slice(ch * TT, (ch + 1) * TT)
                qkb_t = P.smb_t[:, 0:TT]
                sqb_t = P.smb_t[:, TT:2 * TT]
                k.op("scalar", "activation", qkb_t, qkt[:, csl], AF.Copy,
                     ins=(qktb,), outs=(P.smb_b,))
                k.op("scalar", "activation", sqb_t, qkt[:, csl], AF.Square,
                     ins=(qktb,), outs=(P.smb_b,))
                pa, pab = P.bank(); pb, pbb = P.bank()
                pc, pcb = P.bank(); pd, pdb = P.bank()
                k.mm(pab, pa[0:1, 0:TT], [(oh0[:], qkb_t, [oh0b, P.smb_b])])
                k.mm(pbb, pb[0:1, 0:TT], [(oh1[:], qkb_t, [oh1b, P.smb_b])])
                k.mm(pcb, pc[0:1, 0:TT], [(oh0[:], sqb_t, [oh0b, P.smb_b])])
                k.mm(pdb, pd[0:1, 0:TT], [(oh1[:], sqb_t, [oh1b, P.smb_b])])
                R, Rb = P.rsc, P.rscb
                muq = R[0:1, 0:TT]; muk = R[0:1, TT:2 * TT]
                Aq = R[0:1, 2 * TT:3 * TT]; Ak = R[0:1, 3 * TT:4 * TT]
                k.op("vector", "tensor_scalar", muq, pa[0:1, 0:TT], 1.0 / 64, None,
                     ALU.mult, ins=(pab,), outs=(Rb,))
                k.op("vector", "tensor_tensor", Aq, muq, muq, ALU.mult, ins=(Rb,), outs=(Rb,))
                k.op("vector", "scalar_tensor_tensor", Aq, pc[0:1, 0:TT], 1.0 / 64,
                     Aq, ALU.mult, ALU.subtract, ins=(pcb, Rb), outs=(Rb,))
                k.op("scalar", "activation", Aq, Aq, AF.Ln, bias=np.float32(1e-5),
                     ins=(Rb,), outs=(Rb,))
                k.op("scalar", "activation", Aq, Aq, AF.Exp, scale=-0.5,
                     ins=(Rb,), outs=(Rb,))
                k.op("vector", "tensor_scalar", muk, pb[0:1, 0:TT], 1.0 / 64, None,
                     ALU.mult, ins=(pbb,), outs=(Rb,))
                k.op("vector", "tensor_tensor", Ak, muk, muk, ALU.mult, ins=(Rb,), outs=(Rb,))
                k.op("vector", "scalar_tensor_tensor", Ak, pd[0:1, 0:TT], 1.0 / 64,
                     Ak, ALU.mult, ALU.subtract, ins=(pdb, Rb), outs=(Rb,))
                k.op("scalar", "activation", Ak, Ak, AF.Ln, bias=np.float32(1e-5),
                     ins=(Rb,), outs=(Rb,))
                k.op("scalar", "activation", Ak, Ak, AF.Exp, scale=-0.5,
                     ins=(Rb,), outs=(Rb,))
                hilo2(Aq, Ak, P.pb1, P.pb1b, (Rb,))
                k.op("vector", "tensor_tensor", muq, muq, Aq, ALU.mult, ins=(Rb,), outs=(Rb,))
                k.op("vector", "tensor_scalar", muq, muq, -1.0, None, ALU.mult,
                     ins=(Rb,), outs=(Rb,))
                k.op("vector", "tensor_tensor", muk, muk, Ak, ALU.mult, ins=(Rb,), outs=(Rb,))
                k.op("vector", "tensor_scalar", muk, muk, -1.0, None, ALU.mult,
                     ins=(Rb,), outs=(Rb,))
                hilo2(muq, muk, P.pb2, P.pb2b, (Rb,))
                t1 = P.sm[:, 0:TT]
                k.op("vector", "tensor_tensor", t1, qkt[:, csl], P.pb1[:, 0:TT],
                     ALU.mult, ins=(qktb, P.pb1b), outs=(P.smb,))
                k.op("vector", "tensor_tensor", t1, t1, P.pb2[:, 0:TT],
                     ALU.add, ins=(P.pb2b,), outs=(P.smb,))
                qkh = P.smb_t[:, 0:TT]
                k.op("vector", "tensor_scalar", qkh, t1, v_gb[0][:, 0:1],
                     v_gb[0][:, 1:2], ALU.mult, ALU.add, ins=(P.smb, v_gb[1]),
                     outs=(P.smb_b,))
                k.op("vector", "tensor_copy", P.s2b[0:64, csl], qkh[0:64, :],
                     ins=(P.smb_b,), outs=(P.s2bb,))
                prl, prlb = P.bank()
                k.op("tensor", "matmul", prl[0:64, 0:TT], idt[64:128, 64:128],
                     qkh[64:128, :], start=True, stop=True,
                     ins=(idtb, P.smb_b), outs=(prlb,))
                k.op("scalar", "copy", P.s2b[0:64, 2048 + ch * TT:2048 + (ch + 1) * TT],
                     prl[0:64, 0:TT], ins=(prlb,), outs=(P.s2bb,))
            for qc in range(TPS):
                qsl = slice(qc * TT, (qc + 1) * TT)
                avb = P.pb2b
                ktlo = max(0, 4 * qc - 4)
                kts = list(range(ktlo, 4 * qc + 4))
                sbanks = {}

                def score(ki, kt):
                    pt_, pb_ = P.bank()
                    k.mm(pb_, pt_[:, 0:TT],
                         [(P.s2b[0:64, 2048 + kt * 128:2048 + (kt + 1) * 128],
                           P.s2b[0:64, qsl], [P.s2bb])])
                    for jt in range(4):
                        diff = (4 * qc + jt) - kt
                        if diff < 0 or diff == 0 or diff == 4:
                            mk_t, mk_b = masks[0 if diff == 0 else (4 if diff == 4 else -1)]
                            k.op("vector", "tensor_tensor",
                                 pt_[:, jt * 128:(jt + 1) * 128],
                                 pt_[:, jt * 128:(jt + 1) * 128], mk_t[:],
                                 ALU.add, ins=(pb_, mk_b), outs=(pb_,))
                    sbanks[ki] = (pt_, pb_)
                score(0, kts[0])
                for ki, kt in enumerate(kts):
                    if ki + 1 < len(kts):
                        score(ki + 1, kts[ki + 1])
                    pt_, pb_ = sbanks.pop(ki)
                    atile = P.smb_t[:, (ki % 2) * TT:(ki % 2 + 1) * TT]
                    k.op("scalar", "activation", atile, pt_[:, 0:TT], AF.Exp,
                         scale=0.125, ins=(pb_,), outs=(P.smb_b,))
                    k.op("tensor", "matmul", P.pb2[0:65, 0:TT],
                         vt[:, kt * 65:(kt + 1) * 65], atile,
                         start=(ki == 0), stop=(ki == len(kts) - 1),
                         ins=(vtb, P.smb_b),
                         outs_war=(avb,) if ki == 0 else (),
                         finals=(avb,) if ki == len(kts) - 1 else ())
                avs = P.sm[:, 0:TT]
                k.op("scalar", "copy", avs[0:65, :], P.pb2[0:65, 0:TT],
                     ins=(avb,), outs=(P.smb,))
                rrow = P.sm[0:1, TT:2 * TT]
                k.op("vector", "reciprocal", rrow, avs[64:65, :], ins=(P.smb,), outs=(P.smb,))
                hi = P.rbf[0:1, 0:TT]; lo = P.rbf[0:1, TT:2 * TT]
                k.op("vector", "tensor_copy", hi, rrow, ins=(P.smb,), outs=(P.rbfb,))
                k.op("vector", "tensor_tensor", lo, rrow, hi, ALU.subtract,
                     ins=(P.smb, P.rbfb), outs=(P.rbfb,))
                P.bcast(hi, lo, P.pb1, P.pb1b, TT)
                dst_sl = asb[(h % 2) * 64:(h % 2) * 64 + 64,
                             (h // 2) * SEQ + qc * TT:(h // 2) * SEQ + (qc + 1) * TT]
                fin = dict(outs_war=(asbb,),
                           finals=(asbb,) if (h == 15 and qc == TPS - 1) else ())
                if h % 2 == 0:
                    k.op("vector", "tensor_tensor", dst_sl, avs[0:64, :],
                         P.pb1[0:64, 0:TT], ALU.mult, ins=(P.smb, P.pb1b), **fin)
                else:
                    tmp = P.smb_t[:, 1024:1024 + TT]
                    k.op("vector", "tensor_tensor", tmp[0:64, :], avs[0:64, :],
                         P.pb1[0:64, 0:TT], ALU.mult, ins=(P.smb, P.pb1b),
                         outs=(P.smb_b,))
                    prl, prlb = P.bank()
                    k.op("tensor", "matmul", prl[64:128, 0:TT],
                         idt[0:64, 0:64], tmp[0:64, :], start=True, stop=True,
                         ins=(idtb, P.smb_b), outs=(prlb,))
                    k.op("vector", "tensor_copy", dst_sl, prl[64:128, 0:TT],
                         ins=(prlb,), **fin)
        xf, xfb = P.f1, P.f1b
        for lt in range(TPS):
            tt = s * TPS + lt
            sl = slice(tt * TT, (tt + 1) * TT)
            for c in range(NCH):
                k.dma("g", xf[:, c * TT:(c + 1) * TT], src[c * 128:(c + 1) * 128, sl],
                      ins=(srcb,), **gw(xfb, c, NCH))
            for mc in range(NCH):
                pt_, pb_ = P.bank()
                k.mm(pb_, pt_[:, 0:TT],
                     [(w_out(kc, mc), asb[:, kc * SEQ + lt * TT: kc * SEQ + (lt + 1) * TT],
                       [P.wslabb, asbb]) for kc in range(NCH)])
                res = P.sm[:, 0:TT]
                k.op("vector", "tensor_scalar", res, pt_[:, 0:TT],
                     v_ob[0][:, mc:mc + 1], None, ALU.add, ins=(pb_, v_ob[1]),
                     outs=(P.smb,))
                k.op("vector", "tensor_tensor", res, res, xf[:, mc * TT:(mc + 1) * TT],
                     ALU.add, ins=(xfb,), outs=(P.smb,))
                k.dma("s", dst[mc * 128:(mc + 1) * 128, sl], res, ins=(P.smb,),
                      **gw(dstb, (s * TPS + lt) * NCH + mc, NTT * NCH))




def build_l1f(P):
    k = P.k
    i = P.blocks.index("l1f")
    src, srcb = P.d("x_in" if i == 0 else "xdram")
    dst, dstb = P.d("x_out" if i == len(P.blocks) - 1 else "xdram")
    fout, foutb = P.d("fout")
    hg, hgb = P.d("hg")
    cmbd, cmbdb = P.d("cmbd")
    rd, rdb = P.d("rowsd")

    for w in ("w_l1f_router", "w_l1f_sg", "w_l1f_su", "w_l1f_sd"):
        P.precast(w)
    P.precast("w_l1f_g", 4)
    P.precast("w_l1f_u", 4)
    P.precast("w_l1f_d", 4)
    corr_ap, corrb_ = P.d("v_l1f_corr")

    P.stats_pass("x_in" if i == 0 else "xdram", "ln", [0, 1])
    io16, io16b = P.csb["iota16"]
    io16B, io16Bb = P.csb["iota16B"]
    g16, g16b = P.csb["grp16"]
    idf, idfb = P.csb["identf"]

    corrt, corrtb = k.sb("l1f_corrt", [128, 16], F32)
    k.dma("s", P.rsc[0:1, 0:16], corr_ap[:], ins=(corrb_,), outs=(P.rscb,))
    hi = P.rbf[0:1, 0:16]; lo = P.rbf[0:1, 16:32]
    k.op("vector", "tensor_copy", hi, P.rsc[0:1, 0:16], ins=(P.rscb,), outs=(P.rbfb,))
    k.op("vector", "tensor_tensor", lo, P.rsc[0:1, 0:16], hi, ALU.subtract,
         ins=(P.rscb, P.rbfb), outs=(P.rbfb,))
    P.bcast(hi, lo, P.pb1, P.pb1b, 16)
    k.op("vector", "tensor_copy", corrt[:], P.pb1[:, 0:16], ins=(P.pb1b,), outs=(corrtb,))

    def xin_tile(tt, dstt, dsttb):
        sl = slice(tt * TT, (tt + 1) * TT)
        rws = P.use_rows([0, 1], tt)
        xbt, xbtb = P.smb_t, P.smb_b
        P.bcast(rws[0][0], rws[0][1], P.pb1, P.pb1b, TT)
        P.bcast(rws[1][0], rws[1][1], P.pb2, P.pb2b, TT)
        for c in range(NCH):
            k.dma("g", xbt[:, 0:TT] if False else xbt[:, (c % 2) * TT:(c % 2) * TT + TT],
                  src[c * 128:(c + 1) * 128, sl], ins=(srcb,), outs=(xbtb,))
            csl = slice(c * TT, (c + 1) * TT)
            k.op("vector", "tensor_tensor", dstt[:, csl],
                 xbt[:, (c % 2) * TT:(c % 2) * TT + TT],
                 P.pb1[:, 0:TT], ALU.mult, ins=(xbtb, P.pb1b), outs_war=(dsttb,))
            k.op("vector", "tensor_tensor", dstt[:, csl], dstt[:, csl],
                 P.pb2[:, 0:TT], ALU.add, ins=(P.pb2b,),
                 finals=(dsttb,) if c == NCH - 1 else ())

    w_r = P.load_w("w_l1f_router", 16256)
    xint, xintb = P.s1b, P.s1bb

    for tt in range(NTT):
        sl = slice(tt * TT, (tt + 1) * TT)
        xin_tile(tt, xint, xintb)
        pg, pgb = P.bank()
        k.mm(pgb, pg[0:16, 0:TT],
             [(w_r(kc, mo=0, width=16), xint[:, kc * TT:(kc + 1) * TT],
               [P.wslabb, xintb]) for kc in range(NCH)])
        scf = P.sm[0:16, 1024:1536]
        k.op("scalar", "activation", scf, pg[0:16, 0:TT], AF.Sigmoid,
             ins=(pgb,), outs=(P.smb,))
        for st in range(4):
            pts, ptsb = P.bank()
            k.op("tensor", "transpose", pts[:, 0:16],
                 scf[:, st * 128:(st + 1) * 128], idf[0:16, 0:16],
                 ins=(P.smb, idfb), outs=(ptsb,))
            sc_tm = P.sm[:, 0:16]
            sb_ = P.sm[:, 16:32]
            msk = P.sm[:, 32:48]
            gst = P.sm[:, 48:52]
            t1c = P.sm[:, 52:53]; t2c = P.sm[:, 53:54]
            em = P.sm[:, 64:80]
            em2 = P.sm[:, 80:96]
            cmb = P.sm[:, 96:112]
            wk = P.sm[:, 112:116]
            k.op("vector", "tensor_copy", sc_tm, pts[:, 0:16], ins=(ptsb,), outs=(P.smb,))
            k.op("vector", "tensor_tensor", sb_, sc_tm, corrt[:], ALU.add,
                 ins=(corrtb,), outs=(P.smb,))
            for gi in range(4):
                gsl = slice(16 + gi * 4, 16 + (gi + 1) * 4)
                gs_sl = P.sm[:, gsl] if False else sb_[:, gi * 4:(gi + 1) * 4]
                k.op("vector", "tensor_reduce", t1c, gs_sl, mybir.AxisListType.X,
                     ALU.max, ins=(P.smb,), outs=(P.smb,))
                k.op("vector", "tensor_scalar", em[:, 0:4], gs_sl, t1c, None,
                     ALU.is_equal, ins=(P.smb,), outs=(P.smb,))
                k.op("vector", "scalar_tensor_tensor", em2[:, 0:4], em[:, 0:4],
                     -65536.0, io16B[:, gi * 4:(gi + 1) * 4], ALU.mult, ALU.add,
                     ins=(P.smb, io16Bb), outs=(P.smb,))
                k.op("vector", "tensor_reduce", t2c, em2[:, 0:4], mybir.AxisListType.X,
                     ALU.min, ins=(P.smb,), outs=(P.smb,))
                k.op("vector", "tensor_scalar", em[:, 0:4], io16[:, gi * 4:(gi + 1) * 4],
                     t2c, None, ALU.is_equal, ins=(P.smb, io16b), outs=(P.smb,))
                k.op("vector", "scalar_tensor_tensor", em2[:, 0:4], em[:, 0:4],
                     -BIG, gs_sl, ALU.mult, ALU.add, ins=(P.smb,), outs=(P.smb,))
                k.op("vector", "tensor_reduce", t2c, em2[:, 0:4], mybir.AxisListType.X,
                     ALU.max, ins=(P.smb,), outs=(P.smb,))
                k.op("vector", "tensor_tensor", gst[:, gi:gi + 1], t1c, t2c, ALU.add,
                     ins=(P.smb,), outs=(P.smb,))
            k.op("vector", "tensor_reduce", t1c, gst, mybir.AxisListType.X, ALU.max,
                 ins=(P.smb,), outs=(P.smb,))
            k.op("vector", "tensor_scalar", em[:, 0:4], gst, t1c, None, ALU.is_equal,
                 ins=(P.smb,), outs=(P.smb,))
            k.op("vector", "scalar_tensor_tensor", em2[:, 0:4], em[:, 0:4], -65536.0,
                 io16B[:, 0:4], ALU.mult, ALU.add, ins=(P.smb, io16Bb), outs=(P.smb,))
            k.op("vector", "tensor_reduce", t1c, em2[:, 0:4], mybir.AxisListType.X,
                 ALU.min, ins=(P.smb,), outs=(P.smb,))
            k.op("vector", "tensor_scalar", em[:, 0:4], io16[:, 0:4], t1c, None,
                 ALU.is_equal, ins=(P.smb, io16b), outs=(P.smb,))
            k.op("vector", "scalar_tensor_tensor", em2[:, 0:4], em[:, 0:4], -BIG,
                 gst, ALU.mult, ALU.add, ins=(P.smb,), outs=(P.smb,))
            k.op("vector", "tensor_reduce", t2c, em2[:, 0:4], mybir.AxisListType.X,
                 ALU.max, ins=(P.smb,), outs=(P.smb,))
            k.op("vector", "tensor_scalar", em2[:, 0:4], em2[:, 0:4], t2c, None,
                 ALU.is_equal, ins=(P.smb,), outs=(P.smb,))
            k.op("vector", "scalar_tensor_tensor", em2[:, 0:4], em2[:, 0:4], -65536.0,
                 io16B[:, 0:4], ALU.mult, ALU.add, ins=(P.smb, io16Bb), outs=(P.smb,))
            k.op("vector", "tensor_reduce", t2c, em2[:, 0:4], mybir.AxisListType.X,
                 ALU.min, ins=(P.smb,), outs=(P.smb,))
            k.op("vector", "tensor_scalar", msk, g16[:], t1c, None, ALU.is_equal,
                 ins=(P.smb, g16b), outs=(P.smb,))
            k.op("vector", "tensor_scalar", em, g16[:], t2c, None, ALU.is_equal,
                 ins=(P.smb, g16b), outs=(P.smb,))
            k.op("vector", "tensor_tensor", msk, msk, em, ALU.add, ins=(P.smb,), outs=(P.smb,))
            if DBG and tt == 1 and st == 3:
                dbr_t = P.nc.dram_tensor("dbg_rt", [128, 48], F32, kind="ExternalOutput")
                drt = P.k.sb("dbg_rtt", [128, 48], F32)
                k.op("vector", "tensor_copy", drt[0][:, 0:4], gst, ins=(P.smb,), outs=(drt[1],))
                k.op("vector", "tensor_copy", drt[0][:, 4:5], t1c, ins=(P.smb,), outs=(drt[1],))
                k.op("vector", "tensor_copy", drt[0][:, 5:6], t2c, ins=(P.smb,), outs=(drt[1],))
                k.op("vector", "tensor_copy", drt[0][:, 6:22], msk, ins=(P.smb,), outs=(drt[1],))
                k.op("vector", "tensor_copy", drt[0][:, 22:38], sb_, ins=(P.smb,), outs=(drt[1],))
                k.dma("c", dbr_t.ap(), drt[0][:], ins=(drt[1],), outs=(Buf("drt"),))
            k.op("vector", "tensor_scalar", em2, sb_, 1000.0, None, ALU.subtract,
                 ins=(P.smb,), outs=(P.smb,))
            k.op("vector", "scalar_tensor_tensor", msk, msk, 1000.0, em2,
                 ALU.mult, ALU.add, ins=(P.smb,), outs=(P.smb,))
            k.op("vector", "memset", cmb, 0.0, outs=(P.smb,))
            k.op("vector", "memset", wk, 0.0, outs=(P.smb,))
            for kk2 in range(4):
                k.op("vector", "tensor_reduce", t1c, msk, mybir.AxisListType.X,
                     ALU.max, ins=(P.smb,), outs=(P.smb,))
                k.op("vector", "tensor_scalar", em, msk, t1c, None, ALU.is_equal,
                     ins=(P.smb,), outs=(P.smb,))
                k.op("vector", "scalar_tensor_tensor", em2, em, -65536.0, io16B[:],
                     ALU.mult, ALU.add, ins=(P.smb, io16Bb), outs=(P.smb,))
                k.op("vector", "tensor_reduce", t1c, em2, mybir.AxisListType.X,
                     ALU.min, ins=(P.smb,), outs=(P.smb,))
                k.op("vector", "tensor_scalar", em, io16[:], t1c, None, ALU.is_equal,
                     ins=(P.smb, io16b), outs=(P.smb,))
                k.op("vector", "scalar_tensor_tensor", msk, em, -BIG, msk,
                     ALU.mult, ALU.add, ins=(P.smb,), outs=(P.smb,))
                k.op("vector", "tensor_tensor", em2, em, sc_tm, ALU.mult,
                     ins=(P.smb,), outs=(P.smb,))
                k.op("vector", "tensor_reduce", wk[:, kk2:kk2 + 1], em2,
                     mybir.AxisListType.X, ALU.add, ins=(P.smb,), outs=(P.smb,))
                k.op("vector", "scalar_tensor_tensor", cmb, em, 1.0, cmb,
                     ALU.mult, ALU.add, ins=(P.smb,), outs=(P.smb,))
            k.op("vector", "tensor_reduce", t1c, wk, mybir.AxisListType.X, ALU.add,
                 ins=(P.smb,), outs=(P.smb,))
            k.op("vector", "tensor_scalar", t1c, t1c, 1e-20, None, ALU.add,
                 ins=(P.smb,), outs=(P.smb,))
            k.op("vector", "reciprocal", t1c, t1c, ins=(P.smb,), outs=(P.smb,))
            k.op("vector", "tensor_tensor", cmb, cmb, sc_tm, ALU.mult,
                 ins=(P.smb,), outs=(P.smb,))
            k.op("vector", "tensor_scalar", cmb, cmb, t1c, None, ALU.mult,
                 ins=(P.smb,), outs=(P.smb,))
            pcm, pcmb = P.bank()
            k.op("tensor", "transpose", pcm[0:16, 0:128], cmb, idf[:],
                 ins=(P.smb, idfb), outs=(pcmb,))
            cm_s = P.sm[0:16, 1536:1664]
            k.op("vector", "tensor_copy", cm_s, pcm[0:16, 0:128], ins=(pcmb,), outs=(P.smb,))
            k.dma("s", cmbd[0:16, tt * TT + st * 128: tt * TT + (st + 1) * 128],
                  cm_s, ins=(P.smb,), outs=(cmbdb,))

    P.dbg_dump("cmbd", "cmbd")
    for phase, wname in (("sg", "w_l1f_sg"), ("su", "w_l1f_su")):
        w_s = P.load_w(wname, 0)
        for tt in range(NTT):
            sl = slice(tt * TT, (tt + 1) * TT)
            xin_tile(tt, xint, xintb)
            for mc in range(16):
                pt_, pb_ = P.bank()
                k.mm(pb_, pt_[:, 0:TT],
                     [(w_s(kc, mc), xint[:, kc * TT:(kc + 1) * TT],
                       [P.wslabb, xintb]) for kc in range(NCH)])
                ht = P.smb_t[:, 1024:1024 + TT]
                if phase == "sg":
                    k.op("scalar", "activation", ht, pt_[:, 0:TT], AF.Silu,
                         ins=(pb_,), outs=(P.smb_b,))
                else:
                    gt2 = P.smb_t[:, 1536:1536 + TT]
                    k.dma("g", gt2, hg[mc * 128:(mc + 1) * 128, sl],
                          ins=(hgb,), outs=(P.smb_b,))
                    k.op("vector", "tensor_tensor", ht, gt2, pt_[:, 0:TT],
                         ALU.mult, ins=(P.smb_b, pb_), outs=(P.smb_b,))
                k.dma("s", hg[mc * 128:(mc + 1) * 128, sl], ht, ins=(P.smb_b,),
                      outs=(hgb,))
    w_sd = P.load_w("w_l1f_sd", 0)
    for tt in range(NTT):
        sl = slice(tt * TT, (tt + 1) * TT)
        hb1, hb1b = P.s1b, P.s1bb
        hb2, hb2b = P.s2b, P.s2bb
        for c in range(16):
            t_, b_ = (hb1, hb1b) if c < 8 else (hb2, hb2b)
            k.dma("g", t_[:, (c % 8) * TT:(c % 8 + 1) * TT],
                  hg[c * 128:(c + 1) * 128, sl], ins=(hgb,), **gw(b_, c % 8, 8))
        for mc in range(NCH):
            pt_, pb_ = P.bank()
            k.mm(pb_, pt_[:, 0:TT],
                 [((w_sd(kc, mc)), (hb1 if kc < 8 else hb2)[:, (kc % 8) * TT:(kc % 8 + 1) * TT],
                   [P.wslabb, hb1b if kc < 8 else hb2b]) for kc in range(16)])
            fo = P.sm[:, 0:TT]
            k.op("scalar", "copy", fo, pt_[:, 0:TT], ins=(pb_,), outs=(P.smb,))
            k.dma("s", fout[mc * 128:(mc + 1) * 128, sl], fo, ins=(P.smb,),
                  **gw(foutb, tt * NCH + mc, NTT * NCH))

    racc, raccb = P.f1, P.f1b
    h_all, h_allb = P.bbig, P.bbigb
    for tt in range(NTT):
        sl = slice(tt * TT, (tt + 1) * TT)
        xin_tile(tt, xint, xintb)
        for e in range(16):
            w_ge = P.load_w("w_l1f_g", (e % 2) * 4096, expert=e)
            w_ue = P.load_w("w_l1f_u", (e % 2) * 4096 + 2048, expert=e)
            fsl = P.rsc[0:1, 0:TT]
            k.dma("s", fsl, cmbd[e:e + 1, sl], ins=(cmbdb,), outs=(P.rscb,))
            chi = P.rbf[0:1, 0:TT]; clo = P.rbf[0:1, TT:2 * TT]
            k.op("vector", "tensor_copy", chi, fsl, ins=(P.rscb,), outs=(P.rbfb,))
            k.op("vector", "tensor_tensor", clo, fsl, chi, ALU.subtract,
                 ins=(P.rscb, P.rbfb), outs=(P.rbfb,))
            P.bcast(chi, clo, P.pb1, P.pb1b, TT)
            for m2 in range(2):
                pt_, pb_ = P.bank()
                k.mm(pb_, pt_[:, 0:TT],
                     [(w_ge(kc, m2), xint[:, kc * TT:(kc + 1) * TT],
                       [P.wslabb, xintb]) for kc in range(NCH)])
                gt2 = P.smb_t[:, 0:TT]
                k.op("scalar", "activation", gt2, pt_[:, 0:TT], AF.Silu,
                     ins=(pb_,), outs=(P.smb_b,))
                pt2_, pb2_ = P.bank()
                k.mm(pb2_, pt2_[:, 0:TT],
                     [(w_ue(kc, m2), xint[:, kc * TT:(kc + 1) * TT],
                       [P.wslabb, xintb]) for kc in range(NCH)])
                hcol = (e * 2 + m2) * TT
                k.op("vector", "tensor_tensor", h_all[:, hcol:hcol + TT], gt2,
                     pt2_[:, 0:TT], ALU.mult, ins=(P.smb_b, pb2_),
                     outs_war=(h_allb,))
                k.op("vector", "tensor_tensor", h_all[:, hcol:hcol + TT],
                     h_all[:, hcol:hcol + TT], P.pb1[:, 0:TT], ALU.mult,
                     ins=(P.pb1b,),
                     finals=(h_allb,) if (e == 15 and m2 == 1) else ())
        for eg in range(2):
            wd, wdb = P.abig, P.abigb
            ap_, dbuf_ = P.d("w_l1f_d_bf")
            for ei in range(8):
                k.dma("s", wd[:, ei * 2048:(ei + 1) * 2048].rearrange(
                          "p (kc m) -> p kc m", kc=2),
                      ap_[eg * 8 + ei].rearrange("(kc p) m -> p kc m", p=128),
                      ins=(dbuf_,), **gw(wdb, ei, 8))
            for mc in range(NCH):
                pt_, pb_ = P.bank()
                parts = []
                for ei in range(8):
                    for kc in range(2):
                        parts.append((wd[:, (ei * 2 + kc) * 1024 + mc * 128:
                                         (ei * 2 + kc) * 1024 + mc * 128 + 128],
                                      h_all[:, ((eg * 8 + ei) * 2 + kc) * TT:
                                            ((eg * 8 + ei) * 2 + kc + 1) * TT],
                                      [wdb, h_allb]))
                k.mm(pb_, pt_[:, 0:TT], parts)
                if eg == 0:
                    k.op("scalar", "copy", racc[:, mc * TT:(mc + 1) * TT],
                         pt_[:, 0:TT], ins=(pb_,), outs_war=(raccb,))
                else:
                    k.op("vector", "tensor_tensor", racc[:, mc * TT:(mc + 1) * TT],
                         racc[:, mc * TT:(mc + 1) * TT], pt_[:, 0:TT], ALU.add,
                         ins=(pb_,), finals=(raccb,) if mc == NCH - 1 else ())
        for mc in range(NCH):
            fo = P.sm[:, 0:TT]
            x3t = P.sm[:, TT:2 * TT]
            k.dma("s", fo, fout[mc * 128:(mc + 1) * 128, sl], ins=(foutb,), outs=(P.smb,))
            k.dma("g", x3t, src[mc * 128:(mc + 1) * 128, sl], ins=(srcb,), outs=(P.smb,))
            k.op("vector", "tensor_tensor", fo, fo, racc[:, mc * TT:(mc + 1) * TT],
                 ALU.add, ins=(raccb, P.smb), outs=(P.smb,))
            k.op("vector", "tensor_tensor", fo, fo, x3t, ALU.add, ins=(P.smb,), outs=(P.smb,))
            k.dma("s", dst[mc * 128:(mc + 1) * 128, sl], fo, ins=(P.smb,),
                  **gw(dstb, tt * NCH + mc, NTT * NCH))


BLOCK_BUILDERS = {}


BLOCK_BUILDERS["l0a"] = build_l0a
BLOCK_BUILDERS["l0f"] = build_l0f
BLOCK_BUILDERS["l1a"] = build_l1a
BLOCK_BUILDERS["l1f"] = build_l1f


def run_blocks(blocks, g, xs, cms, trace=False):
    P = Prog(blocks, g)
    nc = P.build()
    in_maps = []
    for c in range(NCORES):
        m = {n: g[n] for n in P.used_inputs if n in g}
        if "x_in" in P.used_inputs:
            m["x_in"] = xs[c]
        if "core_mask" in P.used_inputs:
            m["core_mask"] = cms[c]
        in_maps.append(m)
    return run_bass_kernel_spmd(nc, in_maps, core_ids=list(range(NCORES)), trace=trace)


def kernel(**inputs):
    g, xs, cms = stage_inputs(inputs)
    res = run_blocks(["l0a", "l0f", "l1a", "l1f"], g, xs, cms)
    outs = []
    for c in range(NCORES):
        o = res.results[c]["x_out"]
        outs.append(np.ascontiguousarray(o.T).reshape(2, SEQ, D))
    return np.concatenate(outs, axis=0)


# version 06 (final)
# speedup vs baseline: 1.1689x; 1.0323x over previous
import numpy as np
import ml_dtypes
from contextlib import ExitStack

import concourse.bass as bass
import concourse.mybir as mybir
from concourse.bass import IndirectOffsetOnAxis
from concourse.bass_utils import run_bass_kernel_spmd

F32 = mybir.dt.float32
BF16 = mybir.dt.bfloat16
I32 = mybir.dt.int32
AF = mybir.ActivationFunctionType
ALU = mybir.AluOpType
BF = ml_dtypes.bfloat16

NCORES = 8
D = 1024
SEQ = 2048
RPC = 4096
NCH = 8
TT = 512
NTT = RPC // TT
TPS = SEQ // TT
BIG = 1.0e9
OOBIDX = float(8 * 2048)
DBG = False

ECAP = 2048
GCAP = 10240

ENG_SEM = {"tensor": "pe", "vector": "vv", "scalar": "aa"}
DMA_Q = {"s": ("sync", "dms"), "g": ("gpsimd", "dmg"), "c": ("scalar", "dmc")}


def _merge(d, sem, cnt):
    if d.get(sem, 0) < cnt:
        d[sem] = cnt


class Buf:
    __slots__ = ("w", "r", "name", "wbig", "rsmall")

    def __init__(self, name=""):
        self.name = name
        self.w = {}
        self.r = {}
        self.wbig = True
        self.rsmall = False


def gw(buf, i, n):
    return dict(outs_war=(buf,), finals=(buf,) if i == n - 1 else ())


class Kb:
    def __init__(self, nc, es):
        self.nc = nc
        self.es = es
        self.streams = {e: [] for e in ("tensor", "vector", "scalar", "sync", "gpsimd")}
        self.sems = {}
        self.cnt = {}
        self.wm = {e: {} for e in self.streams}

    def sem(self, name):
        if name not in self.sems:
            self.sems[name] = self.es.enter_context(self.nc.semaphore(name))
            self.cnt[name] = 0
        return self.sems[name]

    def op(self, eng, meth, *args, ins=(), outs=(), outs_war=(), finals=(),
           sem=None, inc=1, **kw):
        if sem is None:
            sem = ENG_SEM[eng]
        try:
            big = args and hasattr(args[0], "free_size") and args[0].free_size() >= 384
        except Exception:
            big = False
        deps = {}
        dsmall = {}
        for b in ins:
            for s, c in b.w.items():
                _merge(deps, s, c)
                if not b.wbig:
                    dsmall[s] = True
        for b in tuple(outs) + tuple(outs_war):
            for s, c in b.w.items():
                _merge(deps, s, c)
                if not b.wbig:
                    dsmall[s] = True
            for s, c in b.r.items():
                _merge(deps, s, c)
                if b.rsmall:
                    dsmall[s] = True
        self.sem(sem)
        self.cnt[sem] += inc
        mark = (sem, self.cnt[sem])
        own = ENG_SEM.get(eng)
        waits = []
        for s, c in deps.items():
            if c <= 0:
                continue
            if s == own:
                if eng == "tensor":
                    continue
                if eng == "vector" and big and not dsmall.get(s):
                    continue
            if self.wm[eng].get(s, 0) >= c:
                continue
            waits.append((self.sems[s], c))
            self.wm[eng][s] = c
        semh = self.sems[sem]

        def fn(e, waits=waits, meth=meth, args=args, kw=kw, semh=semh, inc=inc):
            for h, c in waits:
                e.wait_ge(h, c)
            getattr(e, meth)(*args, **kw).then_inc(semh, inc)

        self.streams[eng].append(fn)
        for b in ins:
            _merge(b.r, *mark)
            if sem == ENG_SEM.get(eng) and not big:
                b.rsmall = True
        for b in tuple(outs) + tuple(finals):
            b.w = {mark[0]: mark[1]}
            b.r = {}
            b.wbig = big or eng not in ("vector", "scalar")
            b.rsmall = False
        return mark

    def dma(self, q, out, in_, **kw):
        eng, sem = DMA_Q[q]
        return self.op(eng, "dma_start", sem=sem, inc=16, out=out, in_=in_, **kw)

    def idma(self, out, out_offset, in_, in_offset, nmax=None, compute=ALU.bypass, **kw):
        return self.op("gpsimd", "indirect_dma_start", out, out_offset, in_,
                       in_offset, sem="dmg", inc=16, compute_op=compute, **kw)

    def mm(self, bank, out_ap, parts):
        n = len(parts)
        for j, (lh, rh, ib) in enumerate(parts):
            self.op("tensor", "matmul", out_ap, lh, rh,
                    start=(j == 0), stop=(j == n - 1), ins=ib,
                    outs_war=(bank,) if j == 0 else (),
                    finals=(bank,) if j == n - 1 else ())

    def sb(self, name, shape, dtype):
        t = self.es.enter_context(self.nc.sbuf_tensor(name, list(shape), dtype))
        return t, Buf(name)

    def ps(self, name, shape, dtype=F32):
        t = self.es.enter_context(self.nc.psum_tensor(name, list(shape), dtype))
        return t, Buf(name)



def _vec128(v):
    v = np.asarray(v, np.float32)
    m = v.shape[0]
    if m < 128:
        v = np.concatenate([v, np.zeros(128 - m, np.float32)])
        return np.ascontiguousarray(v.reshape(1, 128).transpose())
    return np.ascontiguousarray(v.reshape(m // 128, 128).transpose())


def stage_inputs(inp):
    g = {}
    g["w_l0a_in"] = np.ascontiguousarray(np.asarray(inp["L0_attn_in_w"], np.float32).T)
    g["w_l0a_pw"] = np.ascontiguousarray(np.asarray(inp["L0_attn_pw_w"], np.float32).T)
    g["w_l0a_win"] = np.ascontiguousarray(np.concatenate(
        [np.asarray(inp["L0_attn_f_win"], np.float32).T,
         np.asarray(inp["L0_attn_b_win"], np.float32).T], axis=1))
    g["w_l0a_wout"] = np.ascontiguousarray(np.concatenate(
        [np.asarray(inp["L0_attn_f_wout"], np.float32).T,
         np.asarray(inp["L0_attn_b_wout"], np.float32).T], axis=0))
    g["w_l0a_out"] = np.ascontiguousarray(np.asarray(inp["L0_attn_out_w"], np.float32).T)
    g["v_l0a_inb"] = _vec128(inp["L0_attn_in_b"])
    g["v_l0a_dwb"] = _vec128(inp["L0_attn_dw_b"])
    g["v_l0a_pwb"] = _vec128(inp["L0_attn_pw_b"])
    g["v_l0a_outb"] = _vec128(inp["L0_attn_out_b"])
    g["v_l0a_ns"] = _vec128(inp["L0_attn_norm_scale"])
    dw = np.asarray(inp["L0_attn_dw_w"], np.float32)[:, 0, :]
    g["v_l0a_dww"] = np.ascontiguousarray(
        dw.reshape(8, 128, 3).transpose(1, 0, 2).reshape(128, 24))
    ssm = np.zeros((128, 4), np.float32)
    ssm[:64, 0] = inp["L0_attn_f_logA"]; ssm[64:, 0] = inp["L0_attn_b_logA"]
    ssm[:64, 1] = inp["L0_attn_f_B"];    ssm[64:, 1] = inp["L0_attn_b_B"]
    ssm[:64, 2] = inp["L0_attn_f_C"];    ssm[64:, 2] = inp["L0_attn_b_C"]
    ssm[:64, 3] = float(inp["L0_attn_f_logdt"]); ssm[64:, 3] = float(inp["L0_attn_b_logdt"])
    g["v_l0a_ssm"] = ssm

    qw_ = np.asarray(inp["L0_ffn_qw"], np.float32)
    qb_ = np.asarray(inp["L0_ffn_qb"], np.float32)
    pairs = [(0, 4), (1, 5), (2, 6), (3, 7), (8, 12), (9, 13), (10, 14), (11, 15)]
    qw_cols = []
    qb_cols = np.zeros((128, 8), np.float32)
    for j, (ha, hb) in enumerate(pairs):
        qw_cols.append(qw_[:, ha * 64:(ha + 1) * 64])
        qw_cols.append(qw_[:, hb * 64:(hb + 1) * 64])
        qb_cols[:64, j] = qb_[ha * 64:(ha + 1) * 64]
        qb_cols[64:, j] = qb_[hb * 64:(hb + 1) * 64]
    g["w_l0f_qw"] = np.ascontiguousarray(np.concatenate(qw_cols, axis=1))
    g["w_l0f_kw"] = np.ascontiguousarray(np.asarray(inp["L0_ffn_kw"], np.float32))
    g["w_l0f_vw"] = np.ascontiguousarray(np.asarray(inp["L0_ffn_vw"], np.float32))
    g["w_l0f_ow"] = np.ascontiguousarray(np.asarray(inp["L0_ffn_ow"], np.float32))
    g["w_l0f_gate"] = np.ascontiguousarray(np.asarray(inp["L0_ffn_gate_w"], np.float32))
    g["w_l0f_w1"] = np.ascontiguousarray(np.asarray(inp["L0_ffn_w1"], np.float32))
    g["w_l0f_w2"] = np.ascontiguousarray(np.asarray(inp["L0_ffn_w2"], np.float32))
    g["v_l0f_qb"] = qb_cols
    g["v_l0f_kb"] = _vec128(inp["L0_ffn_kb"])
    g["v_l0f_vb"] = _vec128(inp["L0_ffn_vb"])
    g["v_l0f_ob"] = _vec128(inp["L0_ffn_ob"])
    g["v_l0f_n1"] = _vec128(inp["L0_ffn_norm1_w"])
    g["v_l0f_n2"] = _vec128(inp["L0_ffn_norm2_w"])

    Wqkv = np.asarray(inp["L1_attn_W_qkv"], np.float32)
    bqkv = np.asarray(inp["L1_attn_b_qkv"], np.float32)
    qk_cols = []
    qkb_cols = np.zeros((128, 16), np.float32)
    vb_cols = np.zeros((64, 16), np.float32)
    for h in range(16):
        qk_cols.append(Wqkv[:, h * 64:(h + 1) * 64])
        qk_cols.append(Wqkv[:, 1024 + h * 64:1024 + (h + 1) * 64])
        qkb_cols[:64, h] = bqkv[h * 64:(h + 1) * 64]
        qkb_cols[64:, h] = bqkv[1024 + h * 64:1024 + (h + 1) * 64]
        vb_cols[:, h] = bqkv[2048 + h * 64:2048 + (h + 1) * 64]
    g["w_l1a_qk"] = np.ascontiguousarray(np.concatenate(qk_cols, axis=1))
    g["w_l1a_v"] = np.ascontiguousarray(Wqkv[:, 2048:])
    g["v_l1a_qkb"] = qkb_cols
    g["v_l1a_vb2"] = np.ascontiguousarray(np.vstack([vb_cols, vb_cols]))
    g["w_l1a_out"] = np.ascontiguousarray(np.asarray(inp["L1_attn_W_out"], np.float32))
    g["v_l1a_outb"] = _vec128(inp["L1_attn_b_out"])
    gb = np.zeros((128, 2), np.float32)
    gb[:64, 0] = inp["L1_attn_q_gamma"]; gb[64:, 0] = inp["L1_attn_k_gamma"]
    gb[:64, 1] = inp["L1_attn_q_beta"];  gb[64:, 1] = inp["L1_attn_k_beta"]
    g["v_l1a_gb"] = gb

    g["w_l1f_router"] = np.ascontiguousarray(np.asarray(inp["L1_ffn_router_w"], np.float32).T)
    g["v_l1f_corr"] = np.ascontiguousarray(
        np.asarray(inp["L1_ffn_correction_bias"], np.float32).reshape(1, 16))
    g["w_l1f_g"] = np.ascontiguousarray(np.asarray(inp["L1_ffn_gate_w"], np.float32))
    g["w_l1f_u"] = np.ascontiguousarray(np.asarray(inp["L1_ffn_up_w"], np.float32))
    g["w_l1f_d"] = np.ascontiguousarray(np.asarray(inp["L1_ffn_down_w"], np.float32))
    g["w_l1f_sg"] = np.ascontiguousarray(np.asarray(inp["L1_ffn_shared_gate_w"], np.float32).T)
    g["w_l1f_su"] = np.ascontiguousarray(np.asarray(inp["L1_ffn_shared_up_w"], np.float32).T)
    g["w_l1f_sd"] = np.ascontiguousarray(np.asarray(inp["L1_ffn_shared_down_w"], np.float32).T)

    x = np.asarray(inp["L0_attn_T"], np.float32)
    xs = [np.ascontiguousarray(x[2 * c:2 * c + 2].reshape(RPC, D).T) for c in range(NCORES)]
    cms = [np.asarray([[1.0 if r < c else 0.0] for r in range(8)], np.float32)
           for c in range(NCORES)]
    return g, xs, cms



class Prog:
    def __init__(self, blocks, g):
        self.blocks = blocks
        self.gshapes = {n: v.shape for n, v in g.items()}
        nc = bass.Bass("TRN2", target_bir_lowering=False,
                       dynamic_dma_scratch_size=49152)
        self.nc = nc
        self.es = ExitStack()
        self.k = Kb(nc, self.es)
        self.dram = {}
        self.dbuf = {}
        self.used_inputs = set()
        self._internal("xdram", [D, RPC], F32)
        self._internal("gate_sp", [D, RPC], BF16)
        self._internal("sdram", [D, RPC], F32)
        self._internal("fout", [D, RPC], F32)
        self._internal("hg", [2048, RPC], BF16)
        self._internal("ebuf", [8 * ECAP + 1, D], BF16)
        self._internal("ybuf", [8 * ECAP + 1, D], F32)
        self._internal("cc_in", [1, 8], F32)
        self._internal("cc_out", [8, 8], F32)
        self._internal("rowsd", [16, RPC], F32)
        self._internal("cmbd", [16, RPC], F32)
        self.dram["x_out"] = nc.dram_tensor("x_out", [D, RPC], F32, kind="ExternalOutput")
        self.dbuf["x_out"] = Buf("x_out")
        self.const = {}
        self.cadd("ident", np.eye(128, dtype=BF))
        self.cadd("identf", np.eye(128, dtype=np.float32))
        self.cadd("iota8B", (np.broadcast_to(np.arange(8, dtype=np.float32), (128, 8)) + 65536.0).copy())
        self.cadd("ustrict", np.ascontiguousarray(np.triu(np.ones((128, 128), BF), 1)))
        self.cadd("ones1", np.ones((1, 128), BF))
        self.cadd("ones1h", np.ones((1, 64), BF))
        self.cadd("onesc1", np.ones((128, 1), BF))
        oh0 = np.zeros((128, 1), BF); oh0[:64] = 1
        oh1 = np.zeros((128, 1), BF); oh1[64:] = 1
        self.cadd("onesc_h0", oh0)
        self.cadd("onesc_h1", oh1)
        self.cadd("iota8", np.broadcast_to(np.arange(8, dtype=np.float32), (128, 8)).copy())
        self.cadd("iota16", np.broadcast_to(np.arange(16, dtype=np.float32), (128, 16)).copy())
        self.cadd("iota16B", (np.broadcast_to(np.arange(16, dtype=np.float32), (128, 16)) + 65536.0).copy())
        self.cadd("grp16", np.broadcast_to((np.arange(16) // 4).astype(np.float32), (128, 16)).copy())
        kk = np.arange(128)[:, None]; qq = np.arange(128)[None, :]
        self.cadd("m_diag", np.where(qq >= kk, 0.0, -BIG).astype(np.float32))
        self.cadd("m_win", np.where(qq < kk, 0.0, -BIG).astype(np.float32))
        self.cadd("m_full", np.full((128, 128), -BIG, np.float32))

    def cadd(self, name, arr):
        self.const[name] = (self.nc.inline_tensor(np.ascontiguousarray(arr), name="c_" + name),
                            arr.dtype, arr.shape)

    def _internal(self, name, shape, dtype):
        self.dram[name] = self.nc.dram_tensor(name, list(shape), dtype, kind="Internal")
        self.dbuf[name] = Buf(name)

    def d(self, name):
        if name not in self.dram:
            if name.endswith("_bf"):
                base = name[:-3]
                self._internal(name, list(self.gshapes[base]), BF16)
            else:
                self.dram[name] = self.nc.dram_tensor(
                    name, list(self.gshapes[name]) if name in self.gshapes else
                    ([D, RPC] if name == "x_in" else [8, 1]), F32, kind="ExternalInput")
                self.dbuf[name] = Buf(name)
                self.used_inputs.add(name)
        return self.dram[name].ap(), self.dbuf[name]

    def build(self):
        k = self.k
        with self.es:
            self.pool = []
            for i in range(5):
                t, b = k.ps(f"p{i}", [128, 512], F32)
                self.pool.append((t, b))
            self.pt, self.ptb = k.ps("pt", [128, 1024], BF16)
            self.pb1, self.pb1b = k.ps("pb1", [128, 512], F32)
            self.pb2, self.pb2b = k.ps("pb2", [128, 512], F32)
            self._pool_i = 0
            self.csb = {}
            for n, (t, dt, shp) in self.const.items():
                sdt = BF16 if dt == BF else F32
                st, sbf = k.sb("cs_" + n, list(shp), sdt)
                k.dma("s", st[:], t.ap(), outs=(sbf,))
                self.csb[n] = (st, sbf)
            self.cflt, self.cfltb = k.sb("cflt", [128, 8], F32)
            self._fconsts = {}
            self.wslab, self.wslabb = k.sb("wslab", [128, 16384], BF16)
            self.abig, self.abigb = k.sb("abig", [128, 16384], BF16)
            self.bbig, self.bbigb = k.sb("bbig", [128, 16384], BF16)
            self.f1, self.f1b = k.sb("f1", [128, 4096], F32)
            self.s1b, self.s1bb = k.sb("s1b", [128, 4096], BF16)
            self.s2b, self.s2bb = k.sb("s2b", [128, 4096], BF16)
            self.sm, self.smb = k.sb("sm", [128, 2048], F32)
            self.smb_t, self.smb_b = k.sb("smb", [128, 2048], BF16)
            self.rsc, self.rscb = k.sb("rsc", [1, 2048], F32)
            self.vt, self.vtb = k.sb("vt", [128, 4 * 16 * 65], BF16)
            self.rbf, self.rbfb = k.sb("rbf", [1, 2048], BF16)
            for blk in self.blocks:
                BLOCK_BUILDERS[blk](self)
            with self.nc.Block() as block:
                for eng in ("sync", "gpsimd", "tensor", "vector", "scalar"):
                    fns = self.k.streams[eng]
                    getattr(block, eng)(lambda e, fns=fns: [f(e) for f in fns])
        return self.nc

    def fconst(self, val, rows=1):
        if val not in self._fconsts:
            i = len(self._fconsts)
            self.k.op("vector", "memset", self.cflt[:, i:i + 1], float(val),
                      outs=(self.cfltb,))
            self._fconsts[val] = i
        i = self._fconsts[val]
        return self.cflt[0:rows, i:i + 1]

    def bank(self):
        t, b = self.pool[self._pool_i % len(self.pool)]
        self._pool_i += 1
        return t, b

    def precast(self, name, nsplit=1):
        k = self.k
        src, sbuf_ = self.d(name)
        dst, dbuf_ = self.d(name + "_bf")
        n0 = src.shape[0]
        step = n0 // nsplit
        for i in range(nsplit):
            k.dma("g", dst[i * step:(i + 1) * step], src[i * step:(i + 1) * step],
                  ins=(sbuf_,), **gw(dbuf_, i, nsplit))

    def load_w(self, name, cols, koff=0, kdim=None, moff=0, mdim=None, expert=None,
               q="s", wbuf=None):
        k = self.k
        ap, dbuf_ = self.d(name + "_bf")
        if expert is not None:
            ap = ap[expert]
        Kfull, Mfull = ap.shape
        kdim = kdim if kdim is not None else Kfull
        mdim = mdim if mdim is not None else Mfull
        nkc = kdim // 128
        view = ap[koff:koff + kdim, moff:moff + mdim].rearrange(
            "(kc p) m -> p kc m", p=128)
        dst = self.wslab[:, cols:cols + nkc * mdim].rearrange(
            "p (kc m) -> p kc m", m=mdim)
        k.dma(q, dst, view, ins=(dbuf_,),
              outs=(wbuf if wbuf is not None else self.wslabb,))

        def acc(kc, mc=0, width=128, mo=None):
            c0 = cols + kc * mdim + (mo if mo is not None else mc * 128)
            return self.wslab[:, c0:c0 + width]
        return acc

    def dbg_dump(self, tag, src_name):
        if not DBG:
            return
        ap, b = self.d(src_name)
        name = "dbg_" + tag
        self.dram[name] = self.nc.dram_tensor(name, list(ap.shape), ap.dtype,
                                              kind="ExternalOutput")
        self.dbuf[name] = Buf(name)
        self.k.dma("c", self.dram[name].ap(), ap[:], ins=(b,),
                   outs=(self.dbuf[name],))

    def load_vec(self, name):
        if not hasattr(self, "_vecs"):
            self._vecs = {}
        if name in self._vecs:
            return self._vecs[name]
        k = self.k
        ap, dbuf_ = self.d(name)
        t, b = k.sb("sv_" + name, list(ap.shape), F32)
        k.dma("s", t[:], ap[:], ins=(dbuf_,), outs=(b,))
        self._vecs[name] = (t, b)
        return t, b

    def bcast(self, hi, lo, psum_t, psum_b, n, pair2=None):
        k = self.k
        if pair2 is None:
            cs, csb_ = self.csb["ones1"]
            k.mm(psum_b, psum_t[:, 0:n],
                 [(cs[0:1, :], hi, [csb_, self.rbfb]),
                  (cs[0:1, :], lo, [csb_, self.rbfb])])
        else:
            ch, chb = self.csb["ones1h"]
            k.mm(psum_b, psum_t[0:64, 0:n],
                 [(ch[0:1, :], hi, [chb, self.rbfb]),
                  (ch[0:1, :], lo, [chb, self.rbfb])])
            k.mm(psum_b, psum_t[64:128, 0:n],
                 [(ch[0:1, :], pair2[0], [chb, self.rbfb]),
                  (ch[0:1, :], pair2[1], [chb, self.rbfb])])

    def use_rows(self, slots, tt, width=TT):
        k = self.k
        rd, rdb = self.d("rowsd")
        out = []
        for i, sl in enumerate(slots):
            fsl = self.rsc[0:1, i * TT:i * TT + width]
            k.dma("s", fsl, rd[sl:sl + 1, tt * TT:tt * TT + width],
                  ins=(rdb,), outs=(self.rscb,))
            hi = self.rbf[0:1, (2 * i) * TT:(2 * i) * TT + width]
            lo = self.rbf[0:1, (2 * i + 1) * TT:(2 * i + 1) * TT + width]
            k.op("vector", "tensor_copy", hi, fsl, ins=(self.rscb,), outs=(self.rbfb,))
            k.op("vector", "tensor_tensor", lo, fsl, hi, ALU.subtract,
                 ins=(self.rscb, self.rbfb), outs=(self.rbfb,))
            out.append((hi, lo))
        return out

    def stats_pass(self, src_name, kind, out_slots):
        k = self.k
        src, srcb = self.d(src_name)
        rd, rdb = self.d("rowsd")
        cs1, cs1b = self.csb["onesc1"]
        xbt, xbtb = self.s1b, self.s1bb
        xsq, xsqb = self.s2b, self.s2bb
        R, Rb = self.rsc, self.rscb

        def rs(i):
            return R[0:1, i * TT:(i + 1) * TT]

        for tt in range(NTT):
            sl = slice(tt * TT, (tt + 1) * TT)
            for c in range(NCH):
                k.dma("g", xbt[:, c * TT:(c + 1) * TT],
                      src[c * 128:(c + 1) * 128, sl], ins=(srcb,), **gw(xbtb, c, NCH))
            for c in range(NCH):
                k.op("scalar", "activation", xsq[:, c * TT:(c + 1) * TT],
                     xbt[:, c * TT:(c + 1) * TT], AF.Square, ins=(xbtb,),
                     **gw(xsqb, c, NCH))
            p2, p2b = self.bank()
            k.mm(p2b, p2[0:1, 0:TT],
                 [(cs1[:, 0:1], xsq[:, c * TT:(c + 1) * TT], [cs1b, xsqb])
                  for c in range(NCH)])
            if kind in ("ln", "lnrms"):
                p1, p1b = self.bank()
                k.mm(p1b, p1[0:1, 0:TT],
                     [(cs1[:, 0:1], xbt[:, c * TT:(c + 1) * TT], [cs1b, xbtb])
                      for c in range(NCH)])
                k.op("vector", "tensor_scalar", rs(0), p1[0:1, 0:TT], 1.0 / D, None,
                     ALU.mult, ins=(p1b,), outs=(Rb,))
                k.op("vector", "tensor_tensor", rs(1), rs(0), rs(0), ALU.mult,
                     ins=(Rb,), outs=(Rb,))
                k.op("vector", "scalar_tensor_tensor", rs(1), p2[0:1, 0:TT], 1.0 / D,
                     rs(1), ALU.mult, ALU.subtract, ins=(p2b, Rb), outs=(Rb,))
                k.op("scalar", "activation", rs(2), rs(1), AF.Ln, bias=np.float32(1e-5),
                     ins=(Rb,), outs=(Rb,))
                k.op("scalar", "activation", rs(2), rs(2), AF.Exp, scale=-0.5,
                     ins=(Rb,), outs=(Rb,))
                k.op("vector", "tensor_tensor", rs(3), rs(0), rs(2), ALU.mult,
                     ins=(Rb,), outs=(Rb,))
                k.op("vector", "tensor_scalar", rs(3), rs(3), -1.0, None, ALU.mult,
                     ins=(Rb,), outs=(Rb,))
                k.dma("s", rd[out_slots[0]:out_slots[0] + 1, sl], rs(2), ins=(Rb,), outs=(rdb,))
                k.dma("s", rd[out_slots[1]:out_slots[1] + 1, sl], rs(3), ins=(Rb,), outs=(rdb,))
                if kind == "lnrms":
                    k.op("vector", "tensor_tensor", rs(0), rs(2), rs(2), ALU.mult,
                         ins=(Rb,), outs=(Rb,))
                    k.op("vector", "tensor_tensor", rs(0), rs(0), rs(1), ALU.mult,
                         ins=(Rb,), outs=(Rb,))
                    k.op("scalar", "activation", rs(0), rs(0), AF.Ln, bias=np.float32(1e-6),
                         ins=(Rb,), outs=(Rb,))
                    k.op("scalar", "activation", rs(0), rs(0), AF.Exp, scale=-0.5,
                         ins=(Rb,), outs=(Rb,))
                    k.op("vector", "tensor_tensor", rs(1), rs(2), rs(0), ALU.mult,
                         ins=(Rb,), outs=(Rb,))
                    k.dma("s", rd[out_slots[2]:out_slots[2] + 1, sl], rs(1), ins=(Rb,), outs=(rdb,))
                    k.op("vector", "tensor_tensor", rs(1), rs(3), rs(0), ALU.mult,
                         ins=(Rb,), outs=(Rb,))
                    k.dma("s", rd[out_slots[3]:out_slots[3] + 1, sl], rs(1), ins=(Rb,), outs=(rdb,))
                    k.op("vector", "tensor_scalar", rs(1), rs(2), 1.0, None, ALU.add,
                         ins=(Rb,), outs=(Rb,))
                    k.dma("s", rd[out_slots[4]:out_slots[4] + 1, sl], rs(1), ins=(Rb,), outs=(rdb,))
            else:
                k.op("scalar", "activation", rs(2), p2[0:1, 0:TT], AF.Ln, bias=np.float32(1e-6),
                     scale=1.0 / D, ins=(p2b,), outs=(Rb,))
                k.op("scalar", "activation", rs(2), rs(2), AF.Exp, scale=-0.5,
                     ins=(Rb,), outs=(Rb,))
                k.dma("s", rd[out_slots[0]:out_slots[0] + 1, sl], rs(2), ins=(Rb,), outs=(rdb,))



def build_l0a(P):
    k = P.k
    src_name = "x_in" if P.blocks[0] == "l0a" else "xdram"
    dst_name = "x_out" if P.blocks[-1] == "l0a" else "xdram"
    src, srcb = P.d(src_name)
    dst, dstb = P.d(dst_name)
    gate_d, gate_db = P.d("gate_sp")

    for w in ("w_l0a_in", "w_l0a_pw", "w_l0a_win", "w_l0a_wout", "w_l0a_out"):
        P.precast(w)
    v_inb = P.load_vec("v_l0a_inb")
    v_dwb = P.load_vec("v_l0a_dwb")
    v_pwb = P.load_vec("v_l0a_pwb")
    v_outb = P.load_vec("v_l0a_outb")
    v_ns = P.load_vec("v_l0a_ns")
    v_dww = P.load_vec("v_l0a_dww")
    v_ssm = P.load_vec("v_l0a_ssm")

    P.stats_pass(src_name, "lnrms", [0, 1, 2, 3, 4])

    sc = P.sm[:, 0:8]
    scb = P.smb
    ssm_t, ssm_b = v_ssm

    def scol(i):
        return sc[:, i:i + 1]

    k.op("scalar", "activation", scol(0), ssm_t[:, 3:4], AF.Exp, ins=(ssm_b,), outs=(scb,))
    k.op("scalar", "activation", scol(1), ssm_t[:, 0:1], AF.Exp, ins=(ssm_b,), outs=(scb,))
    k.op("vector", "tensor_scalar", scol(1), scol(1), -1.0, None, ALU.mult, ins=(scb,), outs=(scb,))
    k.op("vector", "tensor_tensor", scol(2), scol(1), scol(0), ALU.mult, ins=(scb,), outs=(scb,))
    k.op("scalar", "activation", scol(2), scol(2), AF.Exp, ins=(scb,), outs=(scb,))
    k.op("scalar", "activation", scol(3), scol(1), AF.Abs, ins=(scb,), outs=(scb,))
    k.op("vector", "tensor_scalar", scol(4), scol(3), 1e-6, None, ALU.is_lt, ins=(scb,), outs=(scb,))
    k.op("vector", "tensor_tensor", scol(5), scol(1), scol(4), ALU.mult, ins=(scb,), outs=(scb,))
    k.op("vector", "tensor_tensor", scol(5), scol(1), scol(5), ALU.subtract, ins=(scb,), outs=(scb,))
    k.op("vector", "tensor_tensor", scol(5), scol(5), scol(4), ALU.add, ins=(scb,), outs=(scb,))
    k.op("vector", "reciprocal", scol(6), scol(5), ins=(scb,), outs=(scb,))
    k.op("vector", "tensor_scalar", scol(7), scol(2), -1.0, None, ALU.add, ins=(scb,), outs=(scb,))
    k.op("vector", "tensor_tensor", scol(6), scol(6), scol(7), ALU.mult, ins=(scb,), outs=(scb,))
    k.op("vector", "tensor_tensor", scol(7), scol(6), scol(4), ALU.mult, ins=(scb,), outs=(scb,))
    k.op("vector", "tensor_tensor", scol(6), scol(6), scol(7), ALU.subtract, ins=(scb,), outs=(scb,))
    k.op("vector", "tensor_tensor", scol(7), scol(0), scol(4), ALU.mult, ins=(scb,), outs=(scb,))
    k.op("vector", "tensor_tensor", scol(6), scol(6), scol(7), ALU.add, ins=(scb,), outs=(scb,))
    k.op("vector", "tensor_tensor", scol(7), scol(6), ssm_t[:, 1:2], ALU.mult,
         ins=(scb, ssm_b), outs=(scb,))

    w_in = P.load_w("w_l0a_in", 0)
    tn, tnb = P.s2b, P.s2bb
    uss, ussb = P.f1, P.f1b

    for s in range(2):
        xh, xhb = (P.abig, P.abigb) if s == 0 else (P.bbig, P.bbigb)
        for lt in range(TPS):
            tt = s * TPS + lt
            sl = slice(tt * TT, (tt + 1) * TT)
            rws = P.use_rows([2, 3], tt)
            xbt, xbtb = P.s1b, P.s1bb
            for c in range(NCH):
                k.dma("g", xbt[:, c * TT:(c + 1) * TT],
                      src[c * 128:(c + 1) * 128, sl], ins=(srcb,), **gw(xbtb, c, NCH))
            P.bcast(rws[0][0], rws[0][1], P.pb1, P.pb1b, TT)
            P.bcast(rws[1][0], rws[1][1], P.pb2, P.pb2b, TT)
            for c in range(NCH):
                csl = slice(c * TT, (c + 1) * TT)
                k.op("vector", "tensor_tensor", tn[:, csl], xbt[:, csl],
                     P.pb1[:, 0:TT], ALU.mult, ins=(xbtb, P.pb1b), outs_war=(tnb,))
                k.op("vector", "tensor_tensor", tn[:, csl], tn[:, csl],
                     P.pb2[:, 0:TT], ALU.add, ins=(P.pb2b,))
                k.op("vector", "tensor_scalar", tn[:, csl], tn[:, csl],
                     v_ns[0][:, c:c + 1], None, ALU.mult, ins=(v_ns[1],),
                     finals=(tnb,) if c == NCH - 1 else ())
            for mc in range(16):
                pt_, pb_ = P.bank()
                k.mm(pb_, pt_[:, 0:TT],
                     [(w_in(kc, mc), tn[:, kc * TT:(kc + 1) * TT], [P.wslabb, tnb])
                      for kc in range(NCH)])
                if mc < 8:
                    k.op("vector", "tensor_scalar",
                         xh[:, mc * SEQ + lt * TT: mc * SEQ + (lt + 1) * TT],
                         pt_[:, 0:TT], v_inb[0][:, mc:mc + 1], None, ALU.add,
                         ins=(pb_, v_inb[1]), **gw(xhb, lt * 8 + mc, TPS * 8))
                else:
                    gt = P.smb_t[:, 0:TT]
                    k.op("scalar", "activation", gt, pt_[:, 0:TT], AF.Sigmoid,
                         bias=v_inb[0][:, mc:mc + 1], ins=(pb_, v_inb[1]),
                         outs=(P.smb_b,))
                    k.dma("s", gate_d[(mc - 8) * 128:(mc - 7) * 128, sl], gt,
                          ins=(P.smb_b,), outs=(gate_db,))

    w_pw = P.load_w("w_l0a_pw", 0)
    w_win = P.load_w("w_l0a_win", 8192)

    for s in range(2):
        xh, xhb = (P.abig, P.abigb) if s == 0 else (P.bbig, P.bbigb)
        for lt in range(TPS):
            tt = s * TPS + lt
            cvt, cvtb = P.s1b, P.s1bb
            acc = P.sm[:, 512:1024]
            accb = P.smb
            t0_, t1_ = lt * TT, (lt + 1) * TT
            for c in range(NCH):
                xrow = xh[:, c * SEQ: (c + 1) * SEQ]
                w0 = v_dww[0][:, c * 3 + 0: c * 3 + 1]
                w1 = v_dww[0][:, c * 3 + 1: c * 3 + 2]
                w2 = v_dww[0][:, c * 3 + 2: c * 3 + 3]
                k.op("vector", "tensor_scalar", acc, xrow[:, t0_:t1_],
                     w1, v_dwb[0][:, c:c + 1], ALU.mult, ALU.add,
                     ins=(xhb, v_dww[1], v_dwb[1]), outs=(accb,))
                l0 = 1 if lt == 0 else 0
                k.op("vector", "scalar_tensor_tensor", acc[:, l0:TT],
                     xrow[:, t0_ + l0 - 1:t1_ - 1], w0, acc[:, l0:TT],
                     ALU.mult, ALU.add, ins=(xhb,), outs=(accb,))
                r = TT - 1 if lt == TPS - 1 else TT
                k.op("vector", "scalar_tensor_tensor",
                     cvt[:, c * TT:c * TT + r],
                     xrow[:, t0_ + 1:t0_ + 1 + r], w2, acc[:, 0:r],
                     ALU.mult, ALU.add, ins=(xhb, accb), outs_war=(cvtb,))
                if r < TT:
                    k.op("vector", "tensor_copy", cvt[:, c * TT + r:(c + 1) * TT],
                         acc[:, r:TT], ins=(accb,))
                if c == NCH - 1:
                    k.op("vector", "engine_nop", finals=(cvtb,))
            xcv, xcvb = P.s2b, P.s2bb
            for mc in range(NCH):
                pt_, pb_ = P.bank()
                k.mm(pb_, pt_[:, 0:TT],
                     [(w_pw(kc, mc), cvt[:, kc * TT:(kc + 1) * TT], [P.wslabb, cvtb])
                      for kc in range(NCH)])
                k.op("vector", "tensor_scalar", xcv[:, mc * TT:(mc + 1) * TT],
                     pt_[:, 0:TT], v_pwb[0][:, mc:mc + 1], None, ALU.add,
                     ins=(pb_, v_pwb[1]), **gw(xcvb, mc, NCH))
            pt_, pb_ = P.bank()
            k.mm(pb_, pt_[:, 0:TT],
                 [(w_win(kc, 0), xcv[:, kc * TT:(kc + 1) * TT], [P.wslabb, xcvb])
                  for kc in range(NCH)])
            k.op("scalar", "copy", uss[:, tt * TT:(tt + 1) * TT],
                 pt_[:, 0:TT], ins=(pb_,), **gw(ussb, tt, NTT))

    bst, bstb = P.s2b, P.s2bb
    k.op("vector", "tensor_scalar", bst[:], uss[:], scol(7), None, ALU.mult,
         ins=(ussb, scb), outs=(bstb,))
    abt, abtb = k.sb("l0a_abt", [128, SEQ], F32)
    k.op("vector", "memset", abt[:], 1.0, outs=(abtb,))
    k.op("vector", "tensor_scalar", abt[:], abt[:], scol(2), None, ALU.mult,
         ins=(scb,), outs=(abtb,))
    for s in range(2):
        sl = slice(s * SEQ, (s + 1) * SEQ)
        k.op("vector", "tensor_tensor_scan", uss[0:64, sl], abt[0:64, :],
             bst[0:64, sl], 0.0, ALU.mult, ALU.add, ins=(bstb, abtb), outs=(ussb,))
        k.op("vector", "tensor_tensor_scan", uss[64:128, sl][:, ::-1],
             abt[64:128, :][:, ::-1], bst[64:128, sl][:, ::-1], 0.0,
             ALU.mult, ALU.add, ins=(bstb, abtb), outs=(ussb,))
    ysb, ysbb = P.s1b, P.s1bb
    k.op("vector", "tensor_scalar", ysb[:], uss[:], ssm_t[:, 2:3], None, ALU.mult,
         ins=(ussb, ssm_b), outs=(ysbb,))

    w_wout = P.load_w("w_l0a_wout", 0)
    w_out = P.load_w("w_l0a_out", 1024)
    ygb, ygbb = P.s2b, P.s2bb
    xf, xfb = P.f1, P.f1b
    res = P.sm[:, 1024:1536]
    resb = P.smb
    for tt in range(NTT):
        sl = slice(tt * TT, (tt + 1) * TT)
        rws = P.use_rows([4, 1], tt)
        for c in range(NCH):
            k.dma("g", xf[:, c * TT:(c + 1) * TT], src[c * 128:(c + 1) * 128, sl],
                  ins=(srcb,), **gw(xfb, c, NCH))
        for mc in range(NCH):
            gt = P.smb_t[:, 1024 + (mc % 2) * TT: 1024 + (mc % 2 + 1) * TT]
            k.dma("s", gt, gate_d[mc * 128:(mc + 1) * 128, sl],
                  ins=(gate_db,), outs=(P.smb_b,))
            pt_, pb_ = P.bank()
            k.mm(pb_, pt_[:, 0:TT], [(w_wout(0, mc), ysb[:, sl], [P.wslabb, ysbb])])
            k.op("vector", "tensor_tensor", ygb[:, mc * TT:(mc + 1) * TT],
                 pt_[:, 0:TT], gt, ALU.mult, ins=(pb_, P.smb_b), **gw(ygbb, mc, NCH))
        P.bcast(rws[0][0], rws[0][1], P.pb1, P.pb1b, TT)
        P.bcast(rws[1][0], rws[1][1], P.pb2, P.pb2b, TT)
        for mc in range(NCH):
            pt_, pb_ = P.bank()
            k.mm(pb_, pt_[:, 0:TT],
                 [(w_out(kc, mc), ygb[:, kc * TT:(kc + 1) * TT], [P.wslabb, ygbb])
                  for kc in range(NCH)])
            csl = slice(mc * TT, (mc + 1) * TT)
            k.op("vector", "tensor_tensor", res, xf[:, csl], P.pb1[:, 0:TT],
                 ALU.mult, ins=(xfb, P.pb1b), outs=(resb,))
            k.op("vector", "tensor_tensor", res, res, P.pb2[:, 0:TT],
                 ALU.add, ins=(P.pb2b,))
            k.op("vector", "tensor_tensor", res, res, pt_[:, 0:TT],
                 ALU.add, ins=(pb_,))
            k.op("vector", "tensor_scalar", res, res,
                 v_outb[0][:, mc:mc + 1], None, ALU.add, ins=(v_outb[1],),
                 outs=(resb,))
            k.dma("s", dst[mc * 128:(mc + 1) * 128, sl], res, ins=(resb,),
                  **gw(dstb, tt * NCH + mc, NTT * NCH))




def build_l0f(P):
    k = P.k
    i = P.blocks.index("l0f")
    src_name = "x_in" if i == 0 else "xdram"
    dst_name = "x_out" if i == len(P.blocks) - 1 else "xdram"
    src, srcb = P.d(src_name)
    dst, dstb = P.d(dst_name)
    sdram, sdramb = P.d("sdram")
    ebuf, ebufb = P.d("ebuf")
    ybuf, ybufb = P.d("ybuf")

    for w in ("w_l0f_qw", "w_l0f_kw", "w_l0f_vw", "w_l0f_ow", "w_l0f_gate"):
        P.precast(w)
    P.precast("w_l0f_w1", 8)
    P.precast("w_l0f_w2", 8)
    v_qb = P.load_vec("v_l0f_qb")
    v_kb = P.load_vec("v_l0f_kb")
    v_vb = P.load_vec("v_l0f_vb")
    v_ob = P.load_vec("v_l0f_ob")
    v_n1 = P.load_vec("v_l0f_n1")
    v_n2 = P.load_vec("v_l0f_n2")

    P.stats_pass(src_name, "ln", [0, 1])

    vt, vtb = P.vt, P.vtb
    cs1, cs1b = P.csb["onesc1"]

    for s in range(2):
        w_q = P.load_w("w_l0f_qw", 0)
        w_kk = P.load_w("w_l0f_kw", 8192)
        w_v = P.load_w("w_l0f_vw", 10240)
        ssl = slice(s * SEQ, (s + 1) * SEQ)
        x_in, x_inb = P.abig, P.abigb
        for lt in range(TPS):
            tt = s * TPS + lt
            sl = slice(tt * TT, (tt + 1) * TT)
            rws = P.use_rows([0, 1], tt)
            xbt, xbtb = P.s1b, P.s1bb
            for c in range(NCH):
                k.dma("g", xbt[:, c * TT:(c + 1) * TT],
                      src[c * 128:(c + 1) * 128, sl], ins=(srcb,), **gw(xbtb, c, NCH))
            P.bcast(rws[0][0], rws[0][1], P.pb1, P.pb1b, TT)
            P.bcast(rws[1][0], rws[1][1], P.pb2, P.pb2b, TT)
            for c in range(NCH):
                csl = slice(c * TT, (c + 1) * TT)
                osl = slice(c * SEQ + lt * TT, c * SEQ + (lt + 1) * TT)
                k.op("vector", "tensor_tensor", x_in[:, osl], xbt[:, csl],
                     P.pb1[:, 0:TT], ALU.mult, ins=(xbtb, P.pb1b), outs_war=(x_inb,))
                k.op("vector", "tensor_tensor", x_in[:, osl], x_in[:, osl],
                     P.pb2[:, 0:TT], ALU.add, ins=(P.pb2b,),
                     finals=(x_inb,) if (lt == TPS - 1 and c == NCH - 1) else ())
        ksb, ksbb = P.s1b, P.s1bb
        vsb, vsbb = P.s2b, P.s2bb
        for mc in range(2):
            for lt in range(TPS):
                pt_, pb_ = P.bank()
                k.mm(pb_, pt_[:, 0:TT],
                     [(w_kk(kc, mc), x_in[:, kc * SEQ + lt * TT: kc * SEQ + (lt + 1) * TT],
                       [P.wslabb, x_inb]) for kc in range(NCH)])
                k.op("vector", "tensor_scalar",
                     ksb[:, mc * SEQ + lt * TT: mc * SEQ + (lt + 1) * TT],
                     pt_[:, 0:TT], v_kb[0][:, mc:mc + 1], None, ALU.add,
                     ins=(pb_, v_kb[1]), **gw(ksbb, mc * TPS + lt, 2 * TPS))
                pt_, pb_ = P.bank()
                k.mm(pb_, pt_[:, 0:TT],
                     [(w_v(kc, mc), x_in[:, kc * SEQ + lt * TT: kc * SEQ + (lt + 1) * TT],
                       [P.wslabb, x_inb]) for kc in range(NCH)])
                k.op("vector", "tensor_scalar",
                     vsb[:, mc * SEQ + lt * TT: mc * SEQ + (lt + 1) * TT],
                     pt_[:, 0:TT], v_vb[0][:, mc:mc + 1], None, ALU.add,
                     ins=(pb_, v_vb[1]), **gw(vsbb, mc * TPS + lt, 2 * TPS))
        k.op("vector", "memset", vt[:], 1.0, outs=(vtb,))
        idt, idtb = P.csb["ident"]
        for kvh in range(4):
            vrow = vsb[(kvh % 2) * 64:(kvh % 2) * 64 + 64, :]
            for g2 in range(2):
                for j in range(8):
                    kt = g2 * 8 + j
                    k.op("tensor", "transpose", P.pt[:, j * 128:j * 128 + 64],
                         vsb[(kvh % 2) * 64:(kvh % 2) * 64 + 64,
                             (kvh // 2) * SEQ + kt * 128:(kvh // 2) * SEQ + (kt + 1) * 128],
                         idt[(kvh % 2) * 64:(kvh % 2) * 64 + 64,
                             (kvh % 2) * 64:(kvh % 2) * 64 + 64], ins=(vsbb, idtb),
                         outs_war=(P.ptb,) if j == 0 else (),
                         finals=(P.ptb,) if j == 7 else ())
                k.op("vector", "tensor_copy",
                     vt[:, (kvh * 16 + g2 * 8) * 65:(kvh * 16 + g2 * 8) * 65 + 8 * 65].rearrange(
                         "p (t c) -> p t c", c=65)[:, :, 0:64],
                     P.pt[:, 0:1024].rearrange("p (t c) -> p t c", c=128)[:, :, 0:64],
                     ins=(P.ptb,), outs_war=(vtb,),
                     finals=(vtb,) if (kvh == 3 and g2 == 1) else ())
        asb, asbb = P.bbig, P.bbigb
        qh = P.smb_t
        qhb = P.smb_b
        at0 = P.s2b
        at0b = P.s2bb
        PAIRS = [(0, 4), (1, 5), (2, 6), (3, 7), (8, 12), (9, 13), (10, 14), (11, 15)]
        for j, (ha, hb) in enumerate(PAIRS):
            for lt in range(TPS):
                pt_, pb_ = P.bank()
                k.mm(pb_, pt_[:, 0:TT],
                     [(w_q(kc, mc=j),
                       x_in[:, kc * SEQ + lt * TT: kc * SEQ + (lt + 1) * TT],
                       [P.wslabb, x_inb]) for kc in range(NCH)])
                k.op("vector", "tensor_scalar", qh[:, lt * TT:(lt + 1) * TT],
                     pt_[:, 0:TT], v_qb[0][:, j:j + 1], None, ALU.add,
                     ins=(pb_, v_qb[1]), **gw(qhb, lt, TPS))
            for hh, h in enumerate((ha, hb)):
                qb0 = hh * 64
                kvh = h // 4
                assert (kvh % 2) == hh
                for qc in range(TPS):
                    qsl = slice(qc * TT, (qc + 1) * TT)
                    avb = P.pb2b
                    sbanks = {}

                    def score(kt):
                        pt_, pb_ = P.bank()
                        k.mm(pb_, pt_[:, 0:TT],
                             [(ksb[qb0:qb0 + 64,
                                   (kvh // 2) * SEQ + kt * 128:(kvh // 2) * SEQ + (kt + 1) * 128],
                               qh[qb0:qb0 + 64, qsl], [ksbb, qhb])])
                        sbanks[kt] = (pt_, pb_)
                    score(0)
                    for kt in range(16):
                        if kt + 1 < 16:
                            score(kt + 1)
                        pt_, pb_ = sbanks.pop(kt)
                        atile = at0[:, (kt % 2) * TT:(kt % 2 + 1) * TT]
                        k.op("scalar", "activation", atile, pt_[0:128, 0:TT], AF.Exp,
                             scale=0.125, ins=(pb_,), outs=(at0b,))
                        k.op("tensor", "matmul", P.pb2[0:65, 0:TT],
                             vt[:, (kvh * 16 + kt) * 65:(kvh * 16 + kt + 1) * 65],
                             atile, start=(kt == 0), stop=(kt == 15),
                             ins=(vtb, at0b),
                             outs_war=(avb,) if kt == 0 else (),
                             finals=(avb,) if kt == 15 else ())
                    avs = P.sm[:, 0:TT]
                    k.op("scalar", "copy", avs[0:65, :], P.pb2[0:65, 0:TT],
                         ins=(avb,), outs=(P.smb,))
                    rrow = P.sm[0:1, TT:2 * TT]
                    k.op("vector", "reciprocal", rrow, avs[64:65, :], ins=(P.smb,), outs=(P.smb,))
                    hi = P.rbf[0:1, 0:TT]
                    lo = P.rbf[0:1, TT:2 * TT]
                    k.op("vector", "tensor_copy", hi, rrow, ins=(P.smb,), outs=(P.rbfb,))
                    k.op("vector", "tensor_tensor", lo, rrow, hi, ALU.subtract,
                         ins=(P.smb, P.rbfb), outs=(P.rbfb,))
                    P.bcast(hi, lo, P.pb1, P.pb1b, TT)
                    dst_sl = asb[(h % 2) * 64:(h % 2) * 64 + 64,
                                 (h // 2) * SEQ + qc * TT:(h // 2) * SEQ + (qc + 1) * TT]
                    fin = dict(outs_war=(asbb,),
                               finals=(asbb,) if (j == 7 and hh == 1 and qc == TPS - 1) else ())
                    if h % 2 == 0:
                        k.op("vector", "tensor_tensor", dst_sl,
                             avs[0:64, :], P.pb1[0:64, 0:TT], ALU.mult,
                             ins=(P.smb, P.pb1b), **fin)
                    else:
                        tmp = at0[:, 1024:1024 + TT]
                        k.op("vector", "tensor_tensor", tmp[0:64, :],
                             avs[0:64, :], P.pb1[0:64, 0:TT], ALU.mult,
                             ins=(P.smb, P.pb1b), outs=(at0b,))
                        idtc, idtcb = P.csb["ident"]
                        prl, prlb = P.bank()
                        k.op("tensor", "matmul", prl[64:128, 0:TT],
                             idtc[0:64, 0:64], tmp[0:64, :], start=True, stop=True,
                             ins=(idtcb, at0b), outs=(prlb,))
                        k.op("vector", "tensor_copy", dst_sl, prl[64:128, 0:TT],
                             ins=(prlb,), **fin)
        if DBG and s == 1:
            for nm, t_, b_ in [("vt", vt, vtb), ("asb", P.bbig, P.bbigb),
                               ("ksb", P.s1b, P.s1bb)]:
                dd = P.nc.dram_tensor("dbg_" + nm, list(t_[:].shape), t_[:].dtype,
                                      kind="ExternalOutput")
                P.k.dma("c", dd.ap(), t_[:], ins=(b_,), outs=(Buf("d" + nm),))
        w_o = P.load_w("w_l0f_ow", 0)
        if DBG and s == 1:
            dws = P.nc.dram_tensor("dbg_wslab", [128, 8192], BF16, kind="ExternalOutput")
            P.k.dma("c", dws.ap(), P.wslab[:, 0:8192], ins=(P.wslabb,), outs=(Buf("dw"),))
        s1t, s1tb = P.f1, P.f1b
        sq = P.smb_t
        sqb = P.smb_b
        for lt in range(TPS):
            tt = s * TPS + lt
            sl = slice(tt * TT, (tt + 1) * TT)
            p2, p2b_ = P.pb2, P.pb2b
            for mc in range(NCH):
                pt_, pb_ = P.bank()
                k.mm(pb_, pt_[:, 0:TT],
                     [(w_o(kc, mc), asb[:, kc * SEQ + lt * TT: kc * SEQ + (lt + 1) * TT],
                       [P.wslabb, asbb]) for kc in range(NCH)])
                csl = slice(mc * TT, (mc + 1) * TT)
                k.op("vector", "tensor_scalar", s1t[:, csl], pt_[:, 0:TT],
                     v_ob[0][:, mc:mc + 1], None, ALU.add, ins=(pb_, v_ob[1]),
                     outs_war=(s1tb,))
                k.op("vector", "tensor_tensor", s1t[:, csl], s1t[:, csl],
                     x_in[:, mc * SEQ + lt * TT: mc * SEQ + (lt + 1) * TT],
                     ALU.add, ins=(x_inb,), finals=(s1tb,) if mc == NCH - 1 else ())
            for mc in range(NCH):
                csl = slice(mc * TT, (mc + 1) * TT)
                k.dma("s", sdram[mc * 128:(mc + 1) * 128, sl], s1t[:, csl],
                      ins=(s1tb,), **gw(sdramb, (s * TPS + lt) * NCH + mc, NTT * NCH))
                k.op("scalar", "activation", sq[:, 0:TT], s1t[:, csl], AF.Square,
                     ins=(s1tb,), outs=(sqb,))
                k.op("tensor", "matmul", p2[0:1, 0:TT], cs1[:, 0:1], sq[:, 0:TT],
                     start=(mc == 0), stop=(mc == NCH - 1), ins=(cs1b, sqb),
                     outs_war=(p2b_,) if mc == 0 else (),
                     finals=(p2b_,) if mc == NCH - 1 else ())
            rd, rdb = P.d("rowsd")
            r2row = P.sm[0:1, 0:TT]
            k.op("scalar", "activation", r2row, p2[0:1, 0:TT], AF.Ln,
                 bias=np.float32(1e-6), scale=1.0 / D, ins=(p2b_,), outs=(P.smb,))
            k.op("scalar", "activation", r2row, r2row, AF.Exp, scale=-0.5,
                 ins=(P.smb,), outs=(P.smb,))
            k.dma("s", rd[5:6, sl], r2row, ins=(P.smb,), outs=(rdb,))

    P.dbg_dump("s1", "sdram")
    build_l0f_moe(P, dst, dstb)


def build_l0f_moe(P, dst, dstb):
    k = P.k
    sdram, sdramb = P.d("sdram")
    ebuf, ebufb = P.d("ebuf")
    ybuf, ybufb = P.d("ybuf")
    cc_in, cc_inb = P.d("cc_in")
    cc_out, cc_outb = P.d("cc_out")
    cmask, cmaskb = P.d("core_mask")
    rd, rdb = P.d("rowsd")
    v_n1 = P.load_vec("v_l0f_n1")
    v_n2 = P.load_vec("v_l0f_n2")
    w_g = P.load_w("w_l0f_gate", 0)
    cs1, cs1b = P.csb["onesc1"]
    us, usb = P.csb["ustrict"]
    io8, io8b = P.csb["iota8"]
    io8B, io8Bb = P.csb["iota8B"]
    idt, idtb = P.csb["ident"]
    ones1, ones1b = P.csb["ones1"]

    meta, metab = k.sb("l0f_meta", [128, 6 * 32], F32)
    sidx, sidxb = k.sb("l0f_sidx", [128, 64], I32)
    run_, runb = k.sb("l0f_run", [4, 16], F32)
    runbf, runbfb = k.sb("l0f_runbf", [1, 32], BF16)

    k.op("vector", "memset", run_[0:1, 0:8], 0.0, outs=(runb,))
    k.op("vector", "memset", runbf[:], 0.0, outs=(runbfb,))

    for tt in range(NTT):
        sl = slice(tt * TT, (tt + 1) * TT)
        s1t, s1tb = P.f1, P.f1b
        for c in range(NCH):
            k.dma("g", s1t[:, c * TT:(c + 1) * TT], sdram[c * 128:(c + 1) * 128, sl],
                  ins=(sdramb,), **gw(s1tb, c, NCH))
        rws = P.use_rows([5], tt)
        P.bcast(rws[0][0], rws[0][1], P.pb1, P.pb1b, TT)
        s2t, s2tb = P.s1b, P.s1bb
        for c in range(NCH):
            csl = slice(c * TT, (c + 1) * TT)
            k.op("vector", "tensor_tensor", s2t[:, csl], s1t[:, csl],
                 P.pb1[:, 0:TT], ALU.mult, ins=(s1tb, P.pb1b), outs_war=(s2tb,))
            k.op("vector", "tensor_scalar", s2t[:, csl], s2t[:, csl],
                 v_n1[0][:, c:c + 1], None, ALU.mult, ins=(v_n1[1],),
                 finals=(s2tb,) if c == NCH - 1 else ())
        pg, pgb = P.bank()
        k.mm(pgb, pg[0:8, 0:TT],
             [(w_g(kc, mo=0, width=8), s2t[:, kc * TT:(kc + 1) * TT], [P.wslabb, s2tb])
              for kc in range(NCH)])
        probs = P.sm[:, 0:TT]
        k.op("scalar", "activation", probs[0:8, :], pg[0:8, 0:TT], AF.Sigmoid,
             ins=(pgb,), outs=(P.smb,))
        for st in range(4):
            stt = tt * 4 + st
            pts, ptsb = P.bank()
            idf, idfb = P.csb["identf"]
            k.op("tensor", "transpose", pts[:, 0:8],
                 probs[0:8, st * 128:(st + 1) * 128], idf[0:8, 0:8], ins=(P.smb, idfb),
                 outs=(ptsb,))
            pm = P.sm[:, TT:TT + 8]
            k.op("vector", "tensor_copy", pm, pts[:, 0:8], ins=(ptsb,), outs=(P.smb,))
            w1c = P.sm[:, TT + 8:TT + 9]; w2c = P.sm[:, TT + 9:TT + 10]
            i1c = P.sm[:, TT + 10:TT + 11]; i2c = P.sm[:, TT + 11:TT + 12]
            em = P.sm[:, TT + 16:TT + 24]
            em2 = P.sm[:, TT + 24:TT + 32]
            emsb = P.smb_t[:, 0:8]
            k.op("vector", "tensor_reduce", w1c, pm, mybir.AxisListType.X, ALU.max,
                 ins=(P.smb,), outs=(P.smb,))
            k.op("vector", "tensor_scalar", em, pm, w1c, None, ALU.is_equal,
                 ins=(P.smb,), outs=(P.smb,))
            if DBG and tt == 0 and st == 0:
                dbe2 = P.nc.dram_tensor("dbg_e2", [128, 24], F32, kind="ExternalOutput")
                dt2 = P.k.sb("dbg_e2t", [128, 24], F32)
                k.op("vector", "tensor_copy", dt2[0][:, 0:8], pm, ins=(P.smb,), outs=(dt2[1],))
                k.op("vector", "tensor_copy", dt2[0][:, 8:16], em, ins=(P.smb,), outs=(dt2[1],))
                k.op("vector", "tensor_copy", dt2[0][:, 16:17], w1c, ins=(P.smb,), outs=(dt2[1],))
                k.dma("c", dbe2.ap(), dt2[0][:], ins=(dt2[1],), outs=(Buf("dbe2"),))
            k.op("vector", "scalar_tensor_tensor", em2, em, -65536.0, io8B[:],
                 ALU.mult, ALU.add, ins=(P.smb, io8Bb), outs=(P.smb,))
            k.op("vector", "tensor_reduce", i1c, em2, mybir.AxisListType.X, ALU.min,
                 ins=(P.smb,), outs=(P.smb,))
            k.op("vector", "tensor_scalar", em, io8[:], i1c, None, ALU.is_equal,
                 ins=(P.smb, io8b), outs=(P.smb,))
            k.op("vector", "scalar_tensor_tensor", em2, em, -BIG, pm,
                 ALU.mult, ALU.add, ins=(P.smb,), outs=(P.smb,))
            k.op("vector", "tensor_reduce", w2c, em2, mybir.AxisListType.X, ALU.max,
                 ins=(P.smb,), outs=(P.smb,))
            k.op("vector", "tensor_scalar", em2, em2, w2c, None, ALU.is_equal,
                 ins=(P.smb,), outs=(P.smb,))
            k.op("vector", "scalar_tensor_tensor", em2, em2, -65536.0, io8B[:],
                 ALU.mult, ALU.add, ins=(P.smb, io8Bb), outs=(P.smb,))
            k.op("vector", "tensor_reduce", i2c, em2, mybir.AxisListType.X, ALU.min,
                 ins=(P.smb,), outs=(P.smb,))
            k.op("vector", "tensor_scalar", em2, io8[:], i2c, None, ALU.is_equal,
                 ins=(P.smb, io8b), outs=(P.smb,))
            wsum = P.sm[:, TT + 12:TT + 13]
            k.op("vector", "tensor_scalar", wsum, w1c, 1e-6, None, ALU.add,
                 ins=(P.smb,), outs=(P.smb,))
            k.op("vector", "tensor_tensor", wsum, wsum, w2c, ALU.add,
                 ins=(P.smb,), outs=(P.smb,))
            k.op("vector", "reciprocal", wsum, wsum, ins=(P.smb,), outs=(P.smb,))
            k.op("vector", "tensor_tensor", w1c, w1c, wsum, ALU.mult,
                 ins=(P.smb,), outs=(P.smb,))
            k.op("vector", "tensor_tensor", w2c, w2c, wsum, ALU.mult,
                 ins=(P.smb,), outs=(P.smb,))
            k.op("vector", "tensor_tensor", em, em, em2, ALU.add,
                 ins=(P.smb,), outs=(P.smb,))
            k.op("vector", "tensor_copy", emsb, em, ins=(P.smb,), outs=(P.smb_b,))
            ppre, ppreb = P.bank()
            k.mm(ppreb, ppre[:, 0:8],
                 [(us[:], emsb, [usb, P.smb_b]),
                  (ones1[0:1, :], runbf[0:1, 0:8], [ones1b, runbfb]),
                  (ones1[0:1, :], runbf[0:1, 8:16], [ones1b, runbfb])])
            pct, pctb = P.bank()
            k.mm(pctb, pct[0:1, 0:8], [(cs1[:, 0:1], emsb, [cs1b, P.smb_b])])
            k.op("vector", "tensor_scalar", em, io8[:], i1c, None, ALU.is_equal,
                 ins=(P.smb, io8b), outs=(P.smb,))
            tdot = P.sm[:, TT + 32:TT + 40]
            k.op("vector", "tensor_tensor", tdot, ppre[:, 0:8], em, ALU.mult,
                 ins=(ppreb, P.smb), outs=(P.smb,))
            lr1 = meta[:, stt:stt + 1]
            k.op("vector", "tensor_reduce", lr1, tdot, mybir.AxisListType.X, ALU.add,
                 ins=(P.smb,), outs=(metab,))
            k.op("vector", "tensor_scalar", em2, io8[:], i2c, None, ALU.is_equal,
                 ins=(P.smb, io8b), outs=(P.smb,))
            k.op("vector", "tensor_tensor", tdot, ppre[:, 0:8], em2, ALU.mult,
                 ins=(ppreb, P.smb), outs=(P.smb,))
            lr2 = meta[:, 32 + stt:32 + stt + 1]
            k.op("vector", "tensor_reduce", lr2, tdot, mybir.AxisListType.X, ALU.add,
                 ins=(P.smb,), outs=(metab,))
            k.op("vector", "tensor_copy", meta[:, 64 + stt:64 + stt + 1], i1c,
                 ins=(P.smb,), outs=(metab,))
            k.op("vector", "tensor_copy", meta[:, 96 + stt:96 + stt + 1], i2c,
                 ins=(P.smb,), outs=(metab,))
            k.op("vector", "tensor_copy", meta[:, 128 + stt:128 + stt + 1], w1c,
                 ins=(P.smb,), outs=(metab,))
            k.op("vector", "tensor_copy", meta[:, 160 + stt:160 + stt + 1], w2c,
                 ins=(P.smb,), outs=(metab,))
            if DBG and tt == 0 and st == 0:
                P._dbe = P.nc.dram_tensor("dbg_e1", [128, 64], F32, kind="ExternalOutput")
                P._dt = P.k.sb("dbg_e1t", [128, 64], F32)
                dt_ = P._dt
                k.op("vector", "tensor_copy", dt_[0][:, 0:8], pm, ins=(P.smb,), outs=(dt_[1],))
                k.op("vector", "tensor_copy", dt_[0][:, 8:16], em, ins=(P.smb,), outs=(dt_[1],))
                k.op("vector", "tensor_copy", dt_[0][:, 16:24], em2, ins=(P.smb,), outs=(dt_[1],))
                k.op("vector", "tensor_copy", dt_[0][:, 24:32], ppre[:, 0:8], ins=(ppreb,), outs=(dt_[1],))
                k.op("vector", "tensor_copy", dt_[0][0:1, 32:40], pct[0:1, 0:8], ins=(pctb,), outs=(dt_[1],))
            if DBG and tt == 0 and st == 1:
                dt_ = P._dt
                k.op("vector", "tensor_copy", dt_[0][0:1, 40:48], run_[0:1, 0:8],
                     ins=(runb,), outs=(dt_[1],))
                k.op("vector", "tensor_copy", dt_[0][0:1, 48:56], pct[0:1, 0:8],
                     ins=(pctb,), outs=(dt_[1],))
            k.op("vector", "tensor_tensor", run_[0:1, 0:8], run_[0:1, 0:8],
                 pct[0:1, 0:8], ALU.add, ins=(pctb, runb), outs=(runb,))
            k.op("vector", "tensor_copy", runbf[0:1, 0:8], run_[0:1, 0:8],
                 ins=(runb,), outs=(runbfb,))
            k.op("vector", "tensor_tensor", runbf[0:1, 8:16], run_[0:1, 0:8],
                 runbf[0:1, 0:8], ALU.subtract, ins=(runb, runbfb), outs=(runbfb,))
            for kk2, (ic, lrc, col) in enumerate(
                    [(i1c, lr1, 0), (i2c, lr2, 1)]):
                slc = P.sm[:, TT + 40 + kk2:TT + 41 + kk2]
                inb = P.sm[:, TT + 42 + kk2:TT + 43 + kk2]
                k.op("vector", "scalar_tensor_tensor", slc, ic, float(ECAP), lrc,
                     ALU.mult, ALU.add, ins=(P.smb, metab), outs=(P.smb,))
                k.op("vector", "tensor_scalar", inb, lrc, float(ECAP), None,
                     ALU.is_lt, ins=(metab,), outs=(P.smb,))
                k.op("vector", "tensor_scalar", slc, slc, OOBIDX, None,
                     ALU.subtract, ins=(P.smb,), outs=(P.smb,))
                k.op("vector", "tensor_tensor", slc, slc, inb, ALU.mult,
                     ins=(P.smb,), outs=(P.smb,))
                k.op("vector", "tensor_scalar", slc, slc, OOBIDX, None,
                     ALU.add, ins=(P.smb,), outs=(P.smb,))
                k.op("vector", "tensor_copy", sidx[:, stt * 2 + kk2:stt * 2 + kk2 + 1],
                     slc, ins=(P.smb,), outs=(sidxb,))
            for c in range(NCH):
                k.op("tensor", "transpose", P.pt[:, c * 128:(c + 1) * 128],
                     s2t[:, c * TT + st * 128: c * TT + (st + 1) * 128], idt[:],
                     ins=(s2tb, idtb), **gw(P.ptb, c, NCH))
            stm = P.smb_t[:, 0:1024]
            k.op("vector", "tensor_copy", stm, P.pt[:, 0:1024], ins=(P.ptb,),
                 outs=(P.smb_b,))
            for kk2 in range(2):
                k.idma(ebuf[:],
                       IndirectOffsetOnAxis(ap=sidx[:, stt * 2 + kk2:stt * 2 + kk2 + 1], axis=0),
                       stm, None, 8 * ECAP - 1,
                       ins=(P.smb_b, sidxb), **gw(ebufb, stt * 2 + kk2, 64))
    if DBG:
        k.op("vector", "tensor_copy", P._dt[0][0:1, 56:64], run_[0:1, 0:8],
             ins=(runb,), outs=(P._dt[1],))
        k.dma("c", P._dbe.ap(), P._dt[0][:], ins=(P._dt[1],), outs=(Buf("dbe"),))
    k.dma("s", cc_in[:], run_[0:1, 0:8], ins=(runb,), outs=(cc_inb,))
    k.op("gpsimd", "collective_compute", "AllGather", ALU.bypass,
         [[0, 1, 2, 3, 4, 5, 6, 7]], [cc_in.opt()], [cc_out.opt()],
         sem="cc", inc=1, ins=(cc_inb,), outs=(cc_outb,))
    call, callb = k.sb("l0f_call", [8, 8], F32)
    k.dma("s", call[:], cc_out[:], ins=(cc_outb,), outs=(callb,))
    cbf, cbfb = k.sb("l0f_cbf", [8, 17], BF16)
    k.op("vector", "tensor_copy", cbf[0:8, 0:8], call[:], ins=(callb,), outs=(cbfb,))
    k.op("vector", "tensor_tensor", call[:], call[:], cbf[0:8, 0:8], ALU.subtract,
         ins=(callb, cbfb), outs=(callb,))
    k.op("vector", "tensor_copy", cbf[0:8, 8:16], call[:], ins=(callb,), outs=(cbfb,))
    cmb, cmbb = k.sb("l0f_cmb", [8, 1], BF16)
    cmf, cmfb = k.sb("l0f_cmf", [8, 1], F32)
    k.dma("s", cmf[:], cmask[:], ins=(cmaskb,), outs=(cmfb,))
    k.op("vector", "tensor_copy", cmb[:], cmf[:], ins=(cmfb,), outs=(cmbb,))
    pco, pcob = P.bank()
    k.mm(pcob, pco[0:1, 0:8],
         [(cmb[:], cbf[0:8, 0:8], [cmbb, cbfb]),
          (cmb[:], cbf[0:8, 8:16], [cmbb, cbfb])])
    coffr = P.sm[0:1, 0:8]
    k.op("vector", "tensor_copy", coffr, pco[0:1, 0:8], ins=(pcob,), outs=(P.smb,))
    chi = P.rbf[0:1, 0:8]; clo = P.rbf[0:1, 8:16]
    k.op("vector", "tensor_copy", chi, coffr, ins=(P.smb,), outs=(P.rbfb,))
    k.op("vector", "tensor_tensor", clo, coffr, chi, ALU.subtract,
         ins=(P.smb, P.rbfb), outs=(P.rbfb,))
    P.bcast(chi, clo, P.pb1, P.pb1b, 8)
    cofft, cofftb = k.sb("l0f_cofft", [128, 8], F32)
    k.op("vector", "tensor_copy", cofft[:], P.pb1[:, 0:8], ins=(P.pb1b,), outs=(cofftb,))
    io8_t, io8_tb = P.csb["iota8"]
    for stt in range(32):
        em = P.sm[:, 8:16]
        td = P.sm[:, 16:24]
        for kk2 in range(2):
            ic = meta[:, 64 + 32 * kk2 + stt:64 + 32 * kk2 + stt + 1]
            lrc = meta[:, 32 * kk2 + stt:32 * kk2 + stt + 1]
            wnc = meta[:, 128 + 32 * kk2 + stt:128 + 32 * kk2 + stt + 1]
            k.op("vector", "tensor_scalar", em, io8_t[:], ic, None, ALU.is_equal,
                 ins=(io8_tb, metab), outs=(P.smb,))
            k.op("vector", "tensor_tensor", td, em, cofft[:], ALU.mult,
                 ins=(P.smb, cofftb), outs=(P.smb,))
            gr = P.sm[:, 24:25]
            k.op("vector", "tensor_reduce", gr, td, mybir.AxisListType.X, ALU.add,
                 ins=(P.smb,), outs=(P.smb,))
            k.op("vector", "tensor_tensor", gr, gr, lrc, ALU.add,
                 ins=(P.smb, metab), outs=(P.smb,))
            k.op("vector", "tensor_scalar", gr, gr, float(GCAP), None, ALU.is_lt,
                 ins=(P.smb,), outs=(P.smb,))
            k.op("vector", "tensor_tensor", wnc, wnc, gr, ALU.mult,
                 ins=(P.smb, metab), outs=(metab,))
            k.op("vector", "tensor_scalar", gr, lrc, float(ECAP), None, ALU.is_lt,
                 ins=(metab,), outs=(P.smb,))
            k.op("vector", "tensor_tensor", wnc, wnc, gr, ALU.mult,
                 ins=(P.smb, metab), outs=(metab,))
    h_sb, h_sbb = P.s1b, P.s1bb
    for e in range(8):
        for half in range(2):
            w2h = P.load_w("w_l0f_w2", 0, expert=e, koff=half * 1024, kdim=1024)
            w1h = P.load_w("w_l0f_w1", 8192, expert=e, moff=half * 1024, mdim=1024)
            for sc_ in range(4):
                xe, xeb = P.abig, P.abigb
                for c in range(NCH):
                    k.dma("s", xe[:, c * TT:(c + 1) * TT],
                          ebuf[e * ECAP + sc_ * TT:e * ECAP + (sc_ + 1) * TT,
                               c * 128:(c + 1) * 128],
                          transpose=True, ins=(ebufb,), **gw(xeb, c, NCH))
                for mc in range(NCH):
                    pt_, pb_ = P.bank()
                    k.mm(pb_, pt_[:, 0:TT],
                         [(w1h(kc, mc), xe[:, kc * TT:(kc + 1) * TT],
                           [P.wslabb, xeb]) for kc in range(NCH)])
                    k.op("scalar", "activation", h_sb[:, mc * TT:(mc + 1) * TT],
                         pt_[:, 0:TT], AF.Gelu, ins=(pb_,), **gw(h_sbb, mc, NCH))
                for st2 in range(4):
                    ytmp = P.sm[:, 0:1024]
                    for n2 in range(2):
                        pt_, pb_ = P.bank()
                        k.mm(pb_, pt_[:, 0:TT],
                             [(h_sb[:, hc * TT + st2 * 128: hc * TT + (st2 + 1) * 128],
                               w2h(hc, mo=n2 * 512, width=512),
                               [h_sbb, P.wslabb]) for hc in range(NCH)])
                        k.op("scalar", "copy", ytmp[:, n2 * 512:(n2 + 1) * 512],
                             pt_[:, 0:TT], ins=(pb_,),
                             outs_war=(P.smb,), finals=(P.smb,) if n2 == 1 else ())
                    rows = ybuf[e * ECAP + sc_ * TT + st2 * 128:
                                e * ECAP + sc_ * TT + (st2 + 1) * 128, :]
                    if half == 0:
                        k.dma("s", rows, ytmp, ins=(P.smb,),
                              **gw(ybufb, ((e * 2 + half) * 4 + sc_) * 4 + st2, 256))
                    else:
                        k.dma("g", rows, ytmp, accum_op=ALU.add, ins=(P.smb,),
                              **gw(ybufb, ((e * 2 + half) * 4 + sc_) * 4 + st2, 256))
    P.dbg_dump("ebuf", "ebuf")
    P.dbg_dump("ybuf", "ybuf")
    if DBG:
        mdump = P.nc.dram_tensor("dbg_meta", [128, 192], F32, kind="ExternalOutput")
        P.k.dma("c", mdump.ap(), meta[:], ins=(metab,), outs=(Buf("dm"),))
        sdump = P.nc.dram_tensor("dbg_sidx", [128, 64], I32, kind="ExternalOutput")
        P.k.dma("c", sdump.ap(), sidx[:], ins=(sidxb,), outs=(Buf("ds"),))
        cdump = P.nc.dram_tensor("dbg_call", [8, 8], F32, kind="ExternalOutput")
        P.k.dma("c", cdump.ap(), call[:], ins=(callb,), outs=(Buf("dc"),))
    for tt in range(NTT):
        sl = slice(tt * TT, (tt + 1) * TT)
        s1t, s1tb = P.f1, P.f1b
        for c in range(NCH):
            k.dma("g", s1t[:, c * TT:(c + 1) * TT], sdram[c * 128:(c + 1) * 128, sl],
                  ins=(sdramb,), **gw(s1tb, c, NCH))
        rws = P.use_rows([5], tt)
        P.bcast(rws[0][0], rws[0][1], P.pb1, P.pb1b, TT)
        for c in range(NCH):
            csl = slice(c * TT, (c + 1) * TT)
            k.op("vector", "tensor_tensor", s1t[:, csl], s1t[:, csl],
                 P.pb1[:, 0:TT], ALU.mult, ins=(P.pb1b,), outs_war=(s1tb,))
            k.op("vector", "tensor_scalar", s1t[:, csl], s1t[:, csl],
                 v_n1[0][:, c:c + 1], None, ALU.mult, ins=(v_n1[1],),
                 finals=(s1tb,) if c == NCH - 1 else ())
        for st in range(4):
            stt = tt * 4 + st
            g1 = P.sm[:, 0:1024]
            g2 = P.sm[:, 1024:2048]
            k.idma(g1, None, ybuf[:],
                   IndirectOffsetOnAxis(ap=sidx[:, stt * 2:stt * 2 + 1], axis=0),
                   8 * ECAP - 1, ins=(ybufb, sidxb), outs=(P.smb,))
            k.idma(g2, None, ybuf[:],
                   IndirectOffsetOnAxis(ap=sidx[:, stt * 2 + 1:stt * 2 + 2], axis=0),
                   8 * ECAP - 1, ins=(ybufb, sidxb), outs=(P.smb,))
            k.op("vector", "tensor_scalar", g1, g1,
                 meta[:, 128 + stt:128 + stt + 1], None, ALU.mult,
                 ins=(metab,), outs=(P.smb,))
            k.op("vector", "scalar_tensor_tensor", g1, g2,
                 meta[:, 160 + stt:160 + stt + 1], g1, ALU.mult, ALU.add,
                 ins=(metab,), outs=(P.smb,))
            mb = P.smb_t[:, 0:1024]
            k.op("vector", "tensor_copy", mb, g1, ins=(P.smb,), outs=(P.smb_b,))
            idt2, idt2b = P.csb["ident"]
            for c in range(NCH):
                k.op("tensor", "transpose", P.pt[:, c * 128:(c + 1) * 128],
                     mb[:, c * 128:(c + 1) * 128], idt2[:], ins=(P.smb_b, idt2b),
                     **gw(P.ptb, c, NCH))
            for c in range(NCH):
                k.op("vector", "tensor_tensor",
                     s1t[:, c * TT + st * 128: c * TT + (st + 1) * 128],
                     s1t[:, c * TT + st * 128: c * TT + (st + 1) * 128],
                     P.pt[:, c * 128:(c + 1) * 128], ALU.add,
                     ins=(P.ptb,), outs=(s1tb,))
        sq = P.smb_t
        cs1_, cs1_b = P.csb["onesc1"]
        p2, p2b_ = P.pb2, P.pb2b
        for c in range(NCH):
            csl = slice(c * TT, (c + 1) * TT)
            k.dma("s", sdram[c * 128:(c + 1) * 128, sl], s1t[:, csl],
                  ins=(s1tb,), **gw(sdramb, tt * NCH + c, NTT * NCH))
            k.op("scalar", "activation", sq[:, 0:TT], s1t[:, csl], AF.Square,
                 ins=(s1tb,), outs=(P.smb_b,))
            k.op("tensor", "matmul", p2[0:1, 0:TT], cs1_[:, 0:1], sq[:, 0:TT],
                 start=(c == 0), stop=(c == NCH - 1), ins=(cs1_b, P.smb_b),
                 outs_war=(p2b_,) if c == 0 else (),
                 finals=(p2b_,) if c == NCH - 1 else ())
        r2row = P.sm[0:1, 0:TT]
        k.op("scalar", "activation", r2row, p2[0:1, 0:TT], AF.Ln,
             bias=np.float32(1e-6), scale=1.0 / D, ins=(p2b_,), outs=(P.smb,))
        k.op("scalar", "activation", r2row, r2row, AF.Exp, scale=-0.5,
             ins=(P.smb,), outs=(P.smb,))
        k.dma("s", rd[6:7, sl], r2row, ins=(P.smb,), outs=(rdb,))
    P.dbg_dump("s3", "sdram")
    src, srcb = P.d("x_in" if P.blocks.index("l0f") == 0 else "xdram")
    HT = 256
    for ht in range(RPC // HT):
        sl = slice(ht * HT, (ht + 1) * HT)
        s3t, s3tb = P.f1, P.f1b
        for c in range(NCH):
            k.dma("g", s3t[:, c * HT:(c + 1) * HT], sdram[c * 128:(c + 1) * 128, sl],
                  ins=(sdramb,), **gw(s3tb, c, 2 * NCH))
            k.dma("g", s3t[:, 2048 + c * HT:2048 + (c + 1) * HT],
                  src[c * 128:(c + 1) * 128, sl], ins=(srcb,),
                  **gw(s3tb, NCH + c, 2 * NCH))
        rd2, rd2b = P.d("rowsd")
        fsl = P.rsc[0:1, 0:HT]
        k.dma("s", fsl, rd2[6:7, sl], ins=(rd2b,), outs=(P.rscb,))
        rhi = P.rbf[0:1, 0:HT]
        rlo = P.rbf[0:1, HT:2 * HT]
        k.op("vector", "tensor_copy", rhi, fsl, ins=(P.rscb,), outs=(P.rbfb,))
        k.op("vector", "tensor_tensor", rlo, fsl, rhi, ALU.subtract,
             ins=(P.rscb, P.rbfb), outs=(P.rbfb,))
        P.bcast(rhi, rlo, P.pb1, P.pb1b, HT)
        res = P.sm[:, 0:HT]
        for c in range(NCH):
            k.op("vector", "tensor_tensor", res, s3t[:, c * HT:(c + 1) * HT],
                 P.pb1[:, 0:HT], ALU.mult, ins=(s3tb, P.pb1b), outs=(P.smb,))
            k.op("vector", "tensor_scalar", res, res, v_n2[0][:, c:c + 1], None,
                 ALU.mult, ins=(v_n2[1],), outs=(P.smb,))
            k.op("vector", "tensor_tensor", res, res,
                 s3t[:, 2048 + c * HT:2048 + (c + 1) * HT], ALU.add,
                 ins=(s3tb,), outs=(P.smb,))
            k.dma("s", dst[c * 128:(c + 1) * 128, sl], res, ins=(P.smb,),
                  **gw(dstb, ht * NCH + c, (RPC // HT) * NCH))




def build_l1a(P):
    k = P.k
    i = P.blocks.index("l1a")
    src, srcb = P.d("x_in" if i == 0 else "xdram")
    dst, dstb = P.d("x_out" if i == len(P.blocks) - 1 else "xdram")

    for w in ("w_l1a_qk", "w_l1a_v", "w_l1a_out"):
        P.precast(w)
    v_qkb = P.load_vec("v_l1a_qkb")
    v_vb = P.load_vec("v_l1a_vb2")
    v_gb = P.load_vec("v_l1a_gb")
    v_ob = P.load_vec("v_l1a_outb")

    P.stats_pass("x_in" if i == 0 else "xdram", "ln", [0, 1])
    vt, vtb = P.vt, P.vtb
    idt, idtb = P.csb["ident"]
    oh0, oh0b = P.csb["onesc_h0"]
    oh1, oh1b = P.csb["onesc_h1"]
    masks = {0: P.csb["m_diag"], 4: P.csb["m_win"], -1: P.csb["m_full"]}
    w_out = P.load_w("w_l1a_out", 2048)

    def hilo2(rA, rB, psum_t, psum_b, ins):
        hA = P.rbf[0:1, 0:TT]; lA = P.rbf[0:1, TT:2 * TT]
        hB = P.rbf[0:1, 2 * TT:3 * TT]; lB = P.rbf[0:1, 3 * TT:4 * TT]
        k.op("vector", "tensor_copy", hA, rA, ins=ins, outs=(P.rbfb,))
        k.op("vector", "tensor_tensor", lA, rA, hA, ALU.subtract,
             ins=tuple(ins) + (P.rbfb,), outs=(P.rbfb,))
        k.op("vector", "tensor_copy", hB, rB, ins=ins, outs=(P.rbfb,))
        k.op("vector", "tensor_tensor", lB, rB, hB, ALU.subtract,
             ins=tuple(ins) + (P.rbfb,), outs=(P.rbfb,))
        P.bcast(hA, lA, psum_t, psum_b, TT, pair2=(hB, lB))

    for s in range(2):
        x_in, x_inb = P.abig, P.abigb
        for lt in range(TPS):
            tt = s * TPS + lt
            sl = slice(tt * TT, (tt + 1) * TT)
            rws = P.use_rows([0, 1], tt)
            xbt, xbtb = P.s1b, P.s1bb
            for c in range(NCH):
                k.dma("g", xbt[:, c * TT:(c + 1) * TT],
                      src[c * 128:(c + 1) * 128, sl], ins=(srcb,), **gw(xbtb, c, NCH))
            P.bcast(rws[0][0], rws[0][1], P.pb1, P.pb1b, TT)
            P.bcast(rws[1][0], rws[1][1], P.pb2, P.pb2b, TT)
            for c in range(NCH):
                csl = slice(c * TT, (c + 1) * TT)
                osl = slice(c * SEQ + lt * TT, c * SEQ + (lt + 1) * TT)
                k.op("vector", "tensor_tensor", x_in[:, osl], xbt[:, csl],
                     P.pb1[:, 0:TT], ALU.mult, ins=(xbtb, P.pb1b), outs_war=(x_inb,))
                k.op("vector", "tensor_tensor", x_in[:, osl], x_in[:, osl],
                     P.pb2[:, 0:TT], ALU.add, ins=(P.pb2b,),
                     finals=(x_inb,) if (lt == TPS - 1 and c == NCH - 1) else ())
        asb, asbb = P.bbig, P.bbigb
        qkt, qktb = P.f1, P.f1b
        for h in range(16):
            w_qk = P.load_w("w_l1a_qk", 0, moff=h * 128, mdim=128)
            w_v = P.load_w("w_l1a_v", 1024, moff=h * 64, mdim=64)
            for lt in range(TPS):
                pt_, pb_ = P.bank()
                k.mm(pb_, pt_[:, 0:TT],
                     [(w_qk(kc), x_in[:, kc * SEQ + lt * TT: kc * SEQ + (lt + 1) * TT],
                       [P.wslabb, x_inb]) for kc in range(NCH)])
                k.op("vector", "tensor_scalar", qkt[:, lt * TT:(lt + 1) * TT],
                     pt_[:, 0:TT], v_qkb[0][:, h:h + 1], None, ALU.add,
                     ins=(pb_, v_qkb[1]), **gw(qktb, lt, TPS))
                pt_, pb_ = P.bank()
                k.mm(pb_, pt_[0:64, 0:TT],
                     [(w_v(kc, width=64), x_in[:, kc * SEQ + lt * TT: kc * SEQ + (lt + 1) * TT],
                       [P.wslabb, x_inb]) for kc in range(NCH)])
                k.op("vector", "tensor_scalar", P.s2b[0:64, lt * TT:(lt + 1) * TT],
                     pt_[0:64, 0:TT], v_vb[0][0:64, h:h + 1], None, ALU.add,
                     ins=(pb_, v_vb[1]), **gw(P.s2bb, lt, TPS))
            k.op("vector", "memset", vt[:, 0:16 * 65], 1.0, outs=(vtb,))
            for g2 in range(2):
                for j in range(8):
                    kt = g2 * 8 + j
                    k.op("tensor", "transpose", P.pt[:, j * 128:j * 128 + 64],
                         P.s2b[0:64, kt * 128:(kt + 1) * 128], idt[0:64, 0:64],
                         ins=(P.s2bb, idtb), **gw(P.ptb, j, 8))
                k.op("vector", "tensor_copy",
                     vt[:, (g2 * 8) * 65:(g2 * 8) * 65 + 8 * 65].rearrange(
                         "p (t c) -> p t c", c=65)[:, :, 0:64],
                     P.pt[:, 0:1024].rearrange("p (t c) -> p t c", c=128)[:, :, 0:64],
                     ins=(P.ptb,), outs_war=(vtb,), finals=(vtb,) if g2 == 1 else ())
            for ch in range(TPS):
                csl = slice(ch * TT, (ch + 1) * TT)
                qkb_t = P.smb_t[:, 0:TT]
                sqb_t = P.smb_t[:, TT:2 * TT]
                k.op("scalar", "activation", qkb_t, qkt[:, csl], AF.Copy,
                     ins=(qktb,), outs=(P.smb_b,))
                k.op("scalar", "activation", sqb_t, qkt[:, csl], AF.Square,
                     ins=(qktb,), outs=(P.smb_b,))
                pa, pab = P.bank(); pb, pbb = P.bank()
                pc, pcb = P.bank(); pd, pdb = P.bank()
                k.mm(pab, pa[0:1, 0:TT], [(oh0[:], qkb_t, [oh0b, P.smb_b])])
                k.mm(pbb, pb[0:1, 0:TT], [(oh1[:], qkb_t, [oh1b, P.smb_b])])
                k.mm(pcb, pc[0:1, 0:TT], [(oh0[:], sqb_t, [oh0b, P.smb_b])])
                k.mm(pdb, pd[0:1, 0:TT], [(oh1[:], sqb_t, [oh1b, P.smb_b])])
                R, Rb = P.rsc, P.rscb
                muq = R[0:1, 0:TT]; muk = R[0:1, TT:2 * TT]
                Aq = R[0:1, 2 * TT:3 * TT]; Ak = R[0:1, 3 * TT:4 * TT]
                k.op("vector", "tensor_scalar", muq, pa[0:1, 0:TT], 1.0 / 64, None,
                     ALU.mult, ins=(pab,), outs=(Rb,))
                k.op("vector", "tensor_tensor", Aq, muq, muq, ALU.mult, ins=(Rb,), outs=(Rb,))
                k.op("vector", "scalar_tensor_tensor", Aq, pc[0:1, 0:TT], 1.0 / 64,
                     Aq, ALU.mult, ALU.subtract, ins=(pcb, Rb), outs=(Rb,))
                k.op("scalar", "activation", Aq, Aq, AF.Ln, bias=np.float32(1e-5),
                     ins=(Rb,), outs=(Rb,))
                k.op("scalar", "activation", Aq, Aq, AF.Exp, scale=-0.5,
                     ins=(Rb,), outs=(Rb,))
                k.op("vector", "tensor_scalar", muk, pb[0:1, 0:TT], 1.0 / 64, None,
                     ALU.mult, ins=(pbb,), outs=(Rb,))
                k.op("vector", "tensor_tensor", Ak, muk, muk, ALU.mult, ins=(Rb,), outs=(Rb,))
                k.op("vector", "scalar_tensor_tensor", Ak, pd[0:1, 0:TT], 1.0 / 64,
                     Ak, ALU.mult, ALU.subtract, ins=(pdb, Rb), outs=(Rb,))
                k.op("scalar", "activation", Ak, Ak, AF.Ln, bias=np.float32(1e-5),
                     ins=(Rb,), outs=(Rb,))
                k.op("scalar", "activation", Ak, Ak, AF.Exp, scale=-0.5,
                     ins=(Rb,), outs=(Rb,))
                hilo2(Aq, Ak, P.pb1, P.pb1b, (Rb,))
                k.op("vector", "tensor_tensor", muq, muq, Aq, ALU.mult, ins=(Rb,), outs=(Rb,))
                k.op("vector", "tensor_scalar", muq, muq, -1.0, None, ALU.mult,
                     ins=(Rb,), outs=(Rb,))
                k.op("vector", "tensor_tensor", muk, muk, Ak, ALU.mult, ins=(Rb,), outs=(Rb,))
                k.op("vector", "tensor_scalar", muk, muk, -1.0, None, ALU.mult,
                     ins=(Rb,), outs=(Rb,))
                hilo2(muq, muk, P.pb2, P.pb2b, (Rb,))
                t1 = P.sm[:, 0:TT]
                k.op("vector", "tensor_tensor", t1, qkt[:, csl], P.pb1[:, 0:TT],
                     ALU.mult, ins=(qktb, P.pb1b), outs=(P.smb,))
                k.op("vector", "tensor_tensor", t1, t1, P.pb2[:, 0:TT],
                     ALU.add, ins=(P.pb2b,), outs=(P.smb,))
                qkh = P.smb_t[:, 0:TT]
                k.op("vector", "tensor_scalar", qkh, t1, v_gb[0][:, 0:1],
                     v_gb[0][:, 1:2], ALU.mult, ALU.add, ins=(P.smb, v_gb[1]),
                     outs=(P.smb_b,))
                k.op("vector", "tensor_copy", P.s2b[0:64, csl], qkh[0:64, :],
                     ins=(P.smb_b,), outs=(P.s2bb,))
                prl, prlb = P.bank()
                k.op("tensor", "matmul", prl[0:64, 0:TT], idt[64:128, 64:128],
                     qkh[64:128, :], start=True, stop=True,
                     ins=(idtb, P.smb_b), outs=(prlb,))
                k.op("scalar", "copy", P.s2b[0:64, 2048 + ch * TT:2048 + (ch + 1) * TT],
                     prl[0:64, 0:TT], ins=(prlb,), outs=(P.s2bb,))
            for qc in range(TPS):
                qsl = slice(qc * TT, (qc + 1) * TT)
                avb = P.pb2b
                ktlo = max(0, 4 * qc - 4)
                kts = list(range(ktlo, 4 * qc + 4))
                sbanks = {}

                def score(ki, kt):
                    pt_, pb_ = P.bank()
                    k.mm(pb_, pt_[:, 0:TT],
                         [(P.s2b[0:64, 2048 + kt * 128:2048 + (kt + 1) * 128],
                           P.s2b[0:64, qsl], [P.s2bb])])
                    for jt in range(4):
                        diff = (4 * qc + jt) - kt
                        if diff < 0 or diff == 0 or diff == 4:
                            mk_t, mk_b = masks[0 if diff == 0 else (4 if diff == 4 else -1)]
                            k.op("vector", "tensor_tensor",
                                 pt_[:, jt * 128:(jt + 1) * 128],
                                 pt_[:, jt * 128:(jt + 1) * 128], mk_t[:],
                                 ALU.add, ins=(pb_, mk_b), outs=(pb_,))
                    sbanks[ki] = (pt_, pb_)
                score(0, kts[0])
                for ki, kt in enumerate(kts):
                    if ki + 1 < len(kts):
                        score(ki + 1, kts[ki + 1])
                    pt_, pb_ = sbanks.pop(ki)
                    atile = P.smb_t[:, (ki % 2) * TT:(ki % 2 + 1) * TT]
                    k.op("scalar", "activation", atile, pt_[:, 0:TT], AF.Exp,
                         scale=0.125, ins=(pb_,), outs=(P.smb_b,))
                    k.op("tensor", "matmul", P.pb2[0:65, 0:TT],
                         vt[:, kt * 65:(kt + 1) * 65], atile,
                         start=(ki == 0), stop=(ki == len(kts) - 1),
                         ins=(vtb, P.smb_b),
                         outs_war=(avb,) if ki == 0 else (),
                         finals=(avb,) if ki == len(kts) - 1 else ())
                avs = P.sm[:, 0:TT]
                k.op("scalar", "copy", avs[0:65, :], P.pb2[0:65, 0:TT],
                     ins=(avb,), outs=(P.smb,))
                rrow = P.sm[0:1, TT:2 * TT]
                k.op("vector", "reciprocal", rrow, avs[64:65, :], ins=(P.smb,), outs=(P.smb,))
                hi = P.rbf[0:1, 0:TT]; lo = P.rbf[0:1, TT:2 * TT]
                k.op("vector", "tensor_copy", hi, rrow, ins=(P.smb,), outs=(P.rbfb,))
                k.op("vector", "tensor_tensor", lo, rrow, hi, ALU.subtract,
                     ins=(P.smb, P.rbfb), outs=(P.rbfb,))
                P.bcast(hi, lo, P.pb1, P.pb1b, TT)
                dst_sl = asb[(h % 2) * 64:(h % 2) * 64 + 64,
                             (h // 2) * SEQ + qc * TT:(h // 2) * SEQ + (qc + 1) * TT]
                fin = dict(outs_war=(asbb,),
                           finals=(asbb,) if (h == 15 and qc == TPS - 1) else ())
                if h % 2 == 0:
                    k.op("vector", "tensor_tensor", dst_sl, avs[0:64, :],
                         P.pb1[0:64, 0:TT], ALU.mult, ins=(P.smb, P.pb1b), **fin)
                else:
                    tmp = P.smb_t[:, 1024:1024 + TT]
                    k.op("vector", "tensor_tensor", tmp[0:64, :], avs[0:64, :],
                         P.pb1[0:64, 0:TT], ALU.mult, ins=(P.smb, P.pb1b),
                         outs=(P.smb_b,))
                    prl, prlb = P.bank()
                    k.op("tensor", "matmul", prl[64:128, 0:TT],
                         idt[0:64, 0:64], tmp[0:64, :], start=True, stop=True,
                         ins=(idtb, P.smb_b), outs=(prlb,))
                    k.op("vector", "tensor_copy", dst_sl, prl[64:128, 0:TT],
                         ins=(prlb,), **fin)
        xf, xfb = P.f1, P.f1b
        for lt in range(TPS):
            tt = s * TPS + lt
            sl = slice(tt * TT, (tt + 1) * TT)
            for c in range(NCH):
                k.dma("g", xf[:, c * TT:(c + 1) * TT], src[c * 128:(c + 1) * 128, sl],
                      ins=(srcb,), **gw(xfb, c, NCH))
            for mc in range(NCH):
                pt_, pb_ = P.bank()
                k.mm(pb_, pt_[:, 0:TT],
                     [(w_out(kc, mc), asb[:, kc * SEQ + lt * TT: kc * SEQ + (lt + 1) * TT],
                       [P.wslabb, asbb]) for kc in range(NCH)])
                res = P.sm[:, 0:TT]
                k.op("vector", "tensor_scalar", res, pt_[:, 0:TT],
                     v_ob[0][:, mc:mc + 1], None, ALU.add, ins=(pb_, v_ob[1]),
                     outs=(P.smb,))
                k.op("vector", "tensor_tensor", res, res, xf[:, mc * TT:(mc + 1) * TT],
                     ALU.add, ins=(xfb,), outs=(P.smb,))
                k.dma("s", dst[mc * 128:(mc + 1) * 128, sl], res, ins=(P.smb,),
                      **gw(dstb, (s * TPS + lt) * NCH + mc, NTT * NCH))




def build_l1f(P):
    k = P.k
    i = P.blocks.index("l1f")
    src, srcb = P.d("x_in" if i == 0 else "xdram")
    dst, dstb = P.d("x_out" if i == len(P.blocks) - 1 else "xdram")
    fout, foutb = P.d("fout")
    hg, hgb = P.d("hg")
    cmbd, cmbdb = P.d("cmbd")
    rd, rdb = P.d("rowsd")

    for w in ("w_l1f_router", "w_l1f_sg", "w_l1f_su", "w_l1f_sd"):
        P.precast(w)
    P.precast("w_l1f_g", 4)
    P.precast("w_l1f_u", 4)
    P.precast("w_l1f_d", 4)
    corr_ap, corrb_ = P.d("v_l1f_corr")

    P.stats_pass("x_in" if i == 0 else "xdram", "ln", [0, 1])
    io16, io16b = P.csb["iota16"]
    io16B, io16Bb = P.csb["iota16B"]
    g16, g16b = P.csb["grp16"]
    idf, idfb = P.csb["identf"]

    corrt, corrtb = k.sb("l1f_corrt", [128, 16], F32)
    k.dma("s", P.rsc[0:1, 0:16], corr_ap[:], ins=(corrb_,), outs=(P.rscb,))
    hi = P.rbf[0:1, 0:16]; lo = P.rbf[0:1, 16:32]
    k.op("vector", "tensor_copy", hi, P.rsc[0:1, 0:16], ins=(P.rscb,), outs=(P.rbfb,))
    k.op("vector", "tensor_tensor", lo, P.rsc[0:1, 0:16], hi, ALU.subtract,
         ins=(P.rscb, P.rbfb), outs=(P.rbfb,))
    P.bcast(hi, lo, P.pb1, P.pb1b, 16)
    k.op("vector", "tensor_copy", corrt[:], P.pb1[:, 0:16], ins=(P.pb1b,), outs=(corrtb,))

    def xin_tile(tt, dstt, dsttb):
        sl = slice(tt * TT, (tt + 1) * TT)
        rws = P.use_rows([0, 1], tt)
        xbt, xbtb = P.smb_t, P.smb_b
        P.bcast(rws[0][0], rws[0][1], P.pb1, P.pb1b, TT)
        P.bcast(rws[1][0], rws[1][1], P.pb2, P.pb2b, TT)
        for c in range(NCH):
            k.dma("g", xbt[:, 0:TT] if False else xbt[:, (c % 2) * TT:(c % 2) * TT + TT],
                  src[c * 128:(c + 1) * 128, sl], ins=(srcb,), outs=(xbtb,))
            csl = slice(c * TT, (c + 1) * TT)
            k.op("vector", "tensor_tensor", dstt[:, csl],
                 xbt[:, (c % 2) * TT:(c % 2) * TT + TT],
                 P.pb1[:, 0:TT], ALU.mult, ins=(xbtb, P.pb1b), outs_war=(dsttb,))
            k.op("vector", "tensor_tensor", dstt[:, csl], dstt[:, csl],
                 P.pb2[:, 0:TT], ALU.add, ins=(P.pb2b,),
                 finals=(dsttb,) if c == NCH - 1 else ())

    w_r = P.load_w("w_l1f_router", 16256)
    xint, xintb = P.s1b, P.s1bb

    for tt in range(NTT):
        sl = slice(tt * TT, (tt + 1) * TT)
        xin_tile(tt, xint, xintb)
        pg, pgb = P.bank()
        k.mm(pgb, pg[0:16, 0:TT],
             [(w_r(kc, mo=0, width=16), xint[:, kc * TT:(kc + 1) * TT],
               [P.wslabb, xintb]) for kc in range(NCH)])
        scf = P.sm[0:16, 1024:1536]
        k.op("scalar", "activation", scf, pg[0:16, 0:TT], AF.Sigmoid,
             ins=(pgb,), outs=(P.smb,))
        for st in range(4):
            pts, ptsb = P.bank()
            k.op("tensor", "transpose", pts[:, 0:16],
                 scf[:, st * 128:(st + 1) * 128], idf[0:16, 0:16],
                 ins=(P.smb, idfb), outs=(ptsb,))
            sc_tm = P.sm[:, 0:16]
            sb_ = P.sm[:, 16:32]
            msk = P.sm[:, 32:48]
            gst = P.sm[:, 48:52]
            t1c = P.sm[:, 52:53]; t2c = P.sm[:, 53:54]
            em = P.sm[:, 64:80]
            em2 = P.sm[:, 80:96]
            cmb = P.sm[:, 96:112]
            wk = P.sm[:, 112:116]
            k.op("vector", "tensor_copy", sc_tm, pts[:, 0:16], ins=(ptsb,), outs=(P.smb,))
            k.op("vector", "tensor_tensor", sb_, sc_tm, corrt[:], ALU.add,
                 ins=(corrtb,), outs=(P.smb,))
            for gi in range(4):
                gsl = slice(16 + gi * 4, 16 + (gi + 1) * 4)
                gs_sl = P.sm[:, gsl] if False else sb_[:, gi * 4:(gi + 1) * 4]
                k.op("vector", "tensor_reduce", t1c, gs_sl, mybir.AxisListType.X,
                     ALU.max, ins=(P.smb,), outs=(P.smb,))
                k.op("vector", "tensor_scalar", em[:, 0:4], gs_sl, t1c, None,
                     ALU.is_equal, ins=(P.smb,), outs=(P.smb,))
                k.op("vector", "scalar_tensor_tensor", em2[:, 0:4], em[:, 0:4],
                     -65536.0, io16B[:, gi * 4:(gi + 1) * 4], ALU.mult, ALU.add,
                     ins=(P.smb, io16Bb), outs=(P.smb,))
                k.op("vector", "tensor_reduce", t2c, em2[:, 0:4], mybir.AxisListType.X,
                     ALU.min, ins=(P.smb,), outs=(P.smb,))
                k.op("vector", "tensor_scalar", em[:, 0:4], io16[:, gi * 4:(gi + 1) * 4],
                     t2c, None, ALU.is_equal, ins=(P.smb, io16b), outs=(P.smb,))
                k.op("vector", "scalar_tensor_tensor", em2[:, 0:4], em[:, 0:4],
                     -BIG, gs_sl, ALU.mult, ALU.add, ins=(P.smb,), outs=(P.smb,))
                k.op("vector", "tensor_reduce", t2c, em2[:, 0:4], mybir.AxisListType.X,
                     ALU.max, ins=(P.smb,), outs=(P.smb,))
                k.op("vector", "tensor_tensor", gst[:, gi:gi + 1], t1c, t2c, ALU.add,
                     ins=(P.smb,), outs=(P.smb,))
            k.op("vector", "tensor_reduce", t1c, gst, mybir.AxisListType.X, ALU.max,
                 ins=(P.smb,), outs=(P.smb,))
            k.op("vector", "tensor_scalar", em[:, 0:4], gst, t1c, None, ALU.is_equal,
                 ins=(P.smb,), outs=(P.smb,))
            k.op("vector", "scalar_tensor_tensor", em2[:, 0:4], em[:, 0:4], -65536.0,
                 io16B[:, 0:4], ALU.mult, ALU.add, ins=(P.smb, io16Bb), outs=(P.smb,))
            k.op("vector", "tensor_reduce", t1c, em2[:, 0:4], mybir.AxisListType.X,
                 ALU.min, ins=(P.smb,), outs=(P.smb,))
            k.op("vector", "tensor_scalar", em[:, 0:4], io16[:, 0:4], t1c, None,
                 ALU.is_equal, ins=(P.smb, io16b), outs=(P.smb,))
            k.op("vector", "scalar_tensor_tensor", em2[:, 0:4], em[:, 0:4], -BIG,
                 gst, ALU.mult, ALU.add, ins=(P.smb,), outs=(P.smb,))
            k.op("vector", "tensor_reduce", t2c, em2[:, 0:4], mybir.AxisListType.X,
                 ALU.max, ins=(P.smb,), outs=(P.smb,))
            k.op("vector", "tensor_scalar", em2[:, 0:4], em2[:, 0:4], t2c, None,
                 ALU.is_equal, ins=(P.smb,), outs=(P.smb,))
            k.op("vector", "scalar_tensor_tensor", em2[:, 0:4], em2[:, 0:4], -65536.0,
                 io16B[:, 0:4], ALU.mult, ALU.add, ins=(P.smb, io16Bb), outs=(P.smb,))
            k.op("vector", "tensor_reduce", t2c, em2[:, 0:4], mybir.AxisListType.X,
                 ALU.min, ins=(P.smb,), outs=(P.smb,))
            k.op("vector", "tensor_scalar", msk, g16[:], t1c, None, ALU.is_equal,
                 ins=(P.smb, g16b), outs=(P.smb,))
            k.op("vector", "tensor_scalar", em, g16[:], t2c, None, ALU.is_equal,
                 ins=(P.smb, g16b), outs=(P.smb,))
            k.op("vector", "tensor_tensor", msk, msk, em, ALU.add, ins=(P.smb,), outs=(P.smb,))
            if DBG and tt == 1 and st == 3:
                dbr_t = P.nc.dram_tensor("dbg_rt", [128, 48], F32, kind="ExternalOutput")
                drt = P.k.sb("dbg_rtt", [128, 48], F32)
                k.op("vector", "tensor_copy", drt[0][:, 0:4], gst, ins=(P.smb,), outs=(drt[1],))
                k.op("vector", "tensor_copy", drt[0][:, 4:5], t1c, ins=(P.smb,), outs=(drt[1],))
                k.op("vector", "tensor_copy", drt[0][:, 5:6], t2c, ins=(P.smb,), outs=(drt[1],))
                k.op("vector", "tensor_copy", drt[0][:, 6:22], msk, ins=(P.smb,), outs=(drt[1],))
                k.op("vector", "tensor_copy", drt[0][:, 22:38], sb_, ins=(P.smb,), outs=(drt[1],))
                k.dma("c", dbr_t.ap(), drt[0][:], ins=(drt[1],), outs=(Buf("drt"),))
            k.op("vector", "tensor_scalar", em2, sb_, 1000.0, None, ALU.subtract,
                 ins=(P.smb,), outs=(P.smb,))
            k.op("vector", "scalar_tensor_tensor", msk, msk, 1000.0, em2,
                 ALU.mult, ALU.add, ins=(P.smb,), outs=(P.smb,))
            k.op("vector", "memset", cmb, 0.0, outs=(P.smb,))
            k.op("vector", "memset", wk, 0.0, outs=(P.smb,))
            for kk2 in range(4):
                k.op("vector", "tensor_reduce", t1c, msk, mybir.AxisListType.X,
                     ALU.max, ins=(P.smb,), outs=(P.smb,))
                k.op("vector", "tensor_scalar", em, msk, t1c, None, ALU.is_equal,
                     ins=(P.smb,), outs=(P.smb,))
                k.op("vector", "scalar_tensor_tensor", em2, em, -65536.0, io16B[:],
                     ALU.mult, ALU.add, ins=(P.smb, io16Bb), outs=(P.smb,))
                k.op("vector", "tensor_reduce", t1c, em2, mybir.AxisListType.X,
                     ALU.min, ins=(P.smb,), outs=(P.smb,))
                k.op("vector", "tensor_scalar", em, io16[:], t1c, None, ALU.is_equal,
                     ins=(P.smb, io16b), outs=(P.smb,))
                k.op("vector", "scalar_tensor_tensor", msk, em, -BIG, msk,
                     ALU.mult, ALU.add, ins=(P.smb,), outs=(P.smb,))
                k.op("vector", "tensor_tensor", em2, em, sc_tm, ALU.mult,
                     ins=(P.smb,), outs=(P.smb,))
                k.op("vector", "tensor_reduce", wk[:, kk2:kk2 + 1], em2,
                     mybir.AxisListType.X, ALU.add, ins=(P.smb,), outs=(P.smb,))
                k.op("vector", "scalar_tensor_tensor", cmb, em, 1.0, cmb,
                     ALU.mult, ALU.add, ins=(P.smb,), outs=(P.smb,))
            k.op("vector", "tensor_reduce", t1c, wk, mybir.AxisListType.X, ALU.add,
                 ins=(P.smb,), outs=(P.smb,))
            k.op("vector", "tensor_scalar", t1c, t1c, 1e-20, None, ALU.add,
                 ins=(P.smb,), outs=(P.smb,))
            k.op("vector", "reciprocal", t1c, t1c, ins=(P.smb,), outs=(P.smb,))
            k.op("vector", "tensor_tensor", cmb, cmb, sc_tm, ALU.mult,
                 ins=(P.smb,), outs=(P.smb,))
            k.op("vector", "tensor_scalar", cmb, cmb, t1c, None, ALU.mult,
                 ins=(P.smb,), outs=(P.smb,))
            pcm, pcmb = P.bank()
            k.op("tensor", "transpose", pcm[0:16, 0:128], cmb, idf[:],
                 ins=(P.smb, idfb), outs=(pcmb,))
            cm_s = P.sm[0:16, 1536:1664]
            k.op("vector", "tensor_copy", cm_s, pcm[0:16, 0:128], ins=(pcmb,), outs=(P.smb,))
            k.dma("s", cmbd[0:16, tt * TT + st * 128: tt * TT + (st + 1) * 128],
                  cm_s, ins=(P.smb,), outs=(cmbdb,))

    P.dbg_dump("cmbd", "cmbd")
    for phase, wname in (("sg", "w_l1f_sg"), ("su", "w_l1f_su")):
        w_s = P.load_w(wname, 0)
        for tt in range(NTT):
            sl = slice(tt * TT, (tt + 1) * TT)
            xin_tile(tt, xint, xintb)
            for mc in range(16):
                pt_, pb_ = P.bank()
                k.mm(pb_, pt_[:, 0:TT],
                     [(w_s(kc, mc), xint[:, kc * TT:(kc + 1) * TT],
                       [P.wslabb, xintb]) for kc in range(NCH)])
                ht = P.smb_t[:, 1024:1024 + TT]
                if phase == "sg":
                    k.op("scalar", "activation", ht, pt_[:, 0:TT], AF.Silu,
                         ins=(pb_,), outs=(P.smb_b,))
                else:
                    gt2 = P.smb_t[:, 1536:1536 + TT]
                    k.dma("g", gt2, hg[mc * 128:(mc + 1) * 128, sl],
                          ins=(hgb,), outs=(P.smb_b,))
                    k.op("vector", "tensor_tensor", ht, gt2, pt_[:, 0:TT],
                         ALU.mult, ins=(P.smb_b, pb_), outs=(P.smb_b,))
                k.dma("s", hg[mc * 128:(mc + 1) * 128, sl], ht, ins=(P.smb_b,),
                      outs=(hgb,))
    w_sd = P.load_w("w_l1f_sd", 0)
    for tt in range(NTT):
        sl = slice(tt * TT, (tt + 1) * TT)
        hb1, hb1b = P.s1b, P.s1bb
        hb2, hb2b = P.s2b, P.s2bb
        for c in range(16):
            t_, b_ = (hb1, hb1b) if c < 8 else (hb2, hb2b)
            k.dma("g", t_[:, (c % 8) * TT:(c % 8 + 1) * TT],
                  hg[c * 128:(c + 1) * 128, sl], ins=(hgb,), **gw(b_, c % 8, 8))
        for mc in range(NCH):
            pt_, pb_ = P.bank()
            k.mm(pb_, pt_[:, 0:TT],
                 [((w_sd(kc, mc)), (hb1 if kc < 8 else hb2)[:, (kc % 8) * TT:(kc % 8 + 1) * TT],
                   [P.wslabb, hb1b if kc < 8 else hb2b]) for kc in range(16)])
            fo = P.sm[:, 0:TT]
            k.op("scalar", "copy", fo, pt_[:, 0:TT], ins=(pb_,), outs=(P.smb,))
            k.dma("s", fout[mc * 128:(mc + 1) * 128, sl], fo, ins=(P.smb,),
                  **gw(foutb, tt * NCH + mc, NTT * NCH))

    racc, raccb = P.f1, P.f1b
    h_all, h_allb = P.bbig, P.bbigb
    wsA, wsB = Buf("wsA"), Buf("wsB")
    for _wb in (wsA, wsB):
        _wb.w = dict(P.wslabb.w)
        _wb.r = dict(P.wslabb.r)

    def ld_e(e):
        wb_ = wsA if e % 2 == 0 else wsB
        qq = "s" if e % 2 == 0 else "c"
        g_ = P.load_w("w_l1f_g", (e % 2) * 4096, expert=e, q=qq, wbuf=wb_)
        u_ = P.load_w("w_l1f_u", (e % 2) * 4096 + 2048, expert=e, q=qq, wbuf=wb_)
        return g_, u_, wb_

    nxt = ld_e(0)
    for tt in range(NTT):
        sl = slice(tt * TT, (tt + 1) * TT)
        xin_tile(tt, xint, xintb)
        for e in range(16):
            w_ge, w_ue, wb = nxt
            if e < 15:
                nxt = ld_e(e + 1)
            elif tt < NTT - 1:
                nxt = ld_e(0)
            fsl = P.rsc[0:1, 0:TT]
            k.dma("g", fsl, cmbd[e:e + 1, sl], ins=(cmbdb,), outs=(P.rscb,))
            chi = P.rbf[0:1, 0:TT]; clo = P.rbf[0:1, TT:2 * TT]
            k.op("vector", "tensor_copy", chi, fsl, ins=(P.rscb,), outs=(P.rbfb,))
            k.op("vector", "tensor_tensor", clo, fsl, chi, ALU.subtract,
                 ins=(P.rscb, P.rbfb), outs=(P.rbfb,))
            P.bcast(chi, clo, P.pb1, P.pb1b, TT)
            for m2 in range(2):
                pt_, pb_ = P.bank()
                k.mm(pb_, pt_[:, 0:TT],
                     [(w_ge(kc, m2), xint[:, kc * TT:(kc + 1) * TT],
                       [wb, xintb]) for kc in range(NCH)])
                gt2 = P.smb_t[:, 0:TT]
                k.op("scalar", "activation", gt2, pt_[:, 0:TT], AF.Silu,
                     ins=(pb_,), outs=(P.smb_b,))
                pt2_, pb2_ = P.bank()
                k.mm(pb2_, pt2_[:, 0:TT],
                     [(w_ue(kc, m2), xint[:, kc * TT:(kc + 1) * TT],
                       [wb, xintb]) for kc in range(NCH)])
                hcol = (e * 2 + m2) * TT
                k.op("vector", "tensor_tensor", h_all[:, hcol:hcol + TT], gt2,
                     pt2_[:, 0:TT], ALU.mult, ins=(P.smb_b, pb2_),
                     outs_war=(h_allb,))
                k.op("vector", "tensor_tensor", h_all[:, hcol:hcol + TT],
                     h_all[:, hcol:hcol + TT], P.pb1[:, 0:TT], ALU.mult,
                     ins=(P.pb1b,),
                     finals=(h_allb,) if (e == 15 and m2 == 1) else ())
        for eg in range(2):
            wd, wdb = P.abig, P.abigb
            ap_, dbuf_ = P.d("w_l1f_d_bf")
            for ei in range(8):
                k.dma("g", wd[:, ei * 2048:(ei + 1) * 2048].rearrange(
                          "p (kc m) -> p kc m", kc=2),
                      ap_[eg * 8 + ei].rearrange("(kc p) m -> p kc m", p=128),
                      ins=(dbuf_,), **gw(wdb, ei, 8))
            for mc in range(NCH):
                pt_, pb_ = P.bank()
                parts = []
                for ei in range(8):
                    for kc in range(2):
                        parts.append((wd[:, (ei * 2 + kc) * 1024 + mc * 128:
                                         (ei * 2 + kc) * 1024 + mc * 128 + 128],
                                      h_all[:, ((eg * 8 + ei) * 2 + kc) * TT:
                                            ((eg * 8 + ei) * 2 + kc + 1) * TT],
                                      [wdb, h_allb]))
                k.mm(pb_, pt_[:, 0:TT], parts)
                if eg == 0:
                    k.op("scalar", "copy", racc[:, mc * TT:(mc + 1) * TT],
                         pt_[:, 0:TT], ins=(pb_,), outs_war=(raccb,))
                else:
                    k.op("vector", "tensor_tensor", racc[:, mc * TT:(mc + 1) * TT],
                         racc[:, mc * TT:(mc + 1) * TT], pt_[:, 0:TT], ALU.add,
                         ins=(pb_,), finals=(raccb,) if mc == NCH - 1 else ())
        for mc in range(NCH):
            fo = P.sm[:, 0:TT]
            x3t = P.sm[:, TT:2 * TT]
            k.dma("g", fo, fout[mc * 128:(mc + 1) * 128, sl], ins=(foutb,), outs=(P.smb,))
            k.dma("g", x3t, src[mc * 128:(mc + 1) * 128, sl], ins=(srcb,), outs=(P.smb,))
            k.op("vector", "tensor_tensor", fo, fo, racc[:, mc * TT:(mc + 1) * TT],
                 ALU.add, ins=(raccb, P.smb), outs=(P.smb,))
            k.op("vector", "tensor_tensor", fo, fo, x3t, ALU.add, ins=(P.smb,), outs=(P.smb,))
            k.dma("s", dst[mc * 128:(mc + 1) * 128, sl], fo, ins=(P.smb,),
                  **gw(dstb, tt * NCH + mc, NTT * NCH))


BLOCK_BUILDERS = {}


BLOCK_BUILDERS["l0a"] = build_l0a
BLOCK_BUILDERS["l0f"] = build_l0f
BLOCK_BUILDERS["l1a"] = build_l1a
BLOCK_BUILDERS["l1f"] = build_l1f


def run_blocks(blocks, g, xs, cms, trace=False):
    P = Prog(blocks, g)
    nc = P.build()
    in_maps = []
    for c in range(NCORES):
        m = {n: g[n] for n in P.used_inputs if n in g}
        if "x_in" in P.used_inputs:
            m["x_in"] = xs[c]
        if "core_mask" in P.used_inputs:
            m["core_mask"] = cms[c]
        in_maps.append(m)
    return run_bass_kernel_spmd(nc, in_maps, core_ids=list(range(NCORES)), trace=trace)


def kernel(**inputs):
    g, xs, cms = stage_inputs(inputs)
    res = run_blocks(["l0a", "l0f", "l1a", "l1f"], g, xs, cms)
    outs = []
    for c in range(NCORES):
        o = res.results[c]["x_out"]
        outs.append(np.ascontiguousarray(o.T).reshape(2, SEQ, D))
    return np.concatenate(outs, axis=0)
